# version 08
# speedup vs baseline: 1.3277x; 1.0819x over previous
import numpy as np
import ml_dtypes
import concourse.bass as bass
import concourse.mybir as mybir
from concourse.bass_utils import run_bass_kernel_spmd

F32 = mybir.dt.float32
BF16 = mybir.dt.bfloat16
AF = mybir.ActivationFunctionType
ALU = mybir.AluOpType
AX = mybir.AxisListType.X
BFNP = ml_dtypes.bfloat16

NCORES = 8
B = 2
S = 2048
D = 1024
H, HD = 16, 64
NT = S // 128


_CONST_CACHE = {}


def host_consts():
    if _CONST_CACHE:
        return _CONST_CACHE
    c = {}
    j = np.arange(D)
    Fc = np.exp(-2j * np.pi * np.outer(j, j) / D)
    c["Fr"] = Fc.real.astype(np.float32).astype(BFNP)
    c["Fi"] = Fc.imag.astype(np.float32).astype(BFNP)
    c["FiN"] = (-Fc.imag).astype(np.float32).astype(BFNP)
    c["Gr"] = (Fc.real / D).astype(np.float32).astype(BFNP)
    c["G2"] = (Fc.imag / D).astype(np.float32).astype(BFNP)
    c["ident_bf"] = np.eye(128, dtype=np.float32).astype(BFNP)
    c["ident_f32"] = np.eye(128, dtype=np.float32)
    theta = 10000.0 ** (-2.0 / D * np.arange(D, dtype=np.float64))
    ang = np.arange(S, dtype=np.float64)[:, None] * theta[None]
    c["cfm"] = np.ascontiguousarray(np.cos(ang).T).astype(np.float32).astype(BFNP)
    c["sfm"] = np.ascontiguousarray(np.sin(ang).T).astype(np.float32).astype(BFNP)
    c["ctm"] = np.cos(ang).astype(np.float32).astype(BFNP)
    c["stm"] = np.sin(ang).astype(np.float32).astype(BFNP)
    i_ = np.arange(128)
    c["maskU"] = (i_[:, None] <= i_[None, :]).astype(np.float32).astype(BFNP)
    c["maskLs"] = (i_[:, None] > i_[None, :]).astype(np.float32).astype(BFNP)
    c["lmask"] = (np.arange(64)[:, None] <= (np.arange(S) // 32)[None, :]
                  ).astype(np.float32).astype(BFNP)
    sel8 = np.zeros((8, 1024), np.float32)
    for jj in range(8):
        sel8[jj, jj * 128:(jj + 1) * 128] = 1.0
    c["sel8"] = sel8
    blk = np.zeros((8, 128, 16), np.float32)
    for cc in range(8):
        blk[cc, 0:64, 2 * cc] = 1.0
        blk[cc, 64:128, 2 * cc + 1] = 1.0
    c["blk16"] = blk.reshape(8 * 128, 16)
    hx = np.zeros((8, 16, 128), np.float32)
    for cc in range(8):
        hx[cc, 2 * cc, 0:64] = 1.0
        hx[cc, 2 * cc + 1, 64:128] = 1.0
    c["hexp"] = np.ascontiguousarray(hx.transpose(1, 0, 2)).reshape(16, 8 * 128
                                                                    ).astype(BFNP)
    lx = np.zeros((4, 8, 128), np.float32)
    for cc in range(4):
        lx[cc, 2 * cc, 0:64] = 1.0
        lx[cc, 2 * cc + 1, 64:128] = 1.0
    c["lexp"] = np.ascontiguousarray(lx.transpose(1, 0, 2)).reshape(8, 4 * 128
                                                                    ).astype(BFNP)
    c["inv1024"] = np.full((128, 1), 1.0 / 1024.0, np.float32)
    c["ones_bf"] = np.ones((128, 1), np.float32).astype(BFNP)
    _CONST_CACHE.update(c)
    return c


class Sched:
    def __init__(self, nc, sem_names):
        self.nc = nc
        self.sems = {n: nc.alloc_semaphore(n) for n in sem_names}
        self.cnt = None
        self.E = None
        self.eng = None

    def free(self):
        for h in self.sems.values():
            self.nc.release_semaphore(h)

    def start(self, engine_name, eng):
        self.E = engine_name
        self.eng = eng
        self.cnt = {n: 0 for n in self.sems}

    def op(self, e, fn, waits=(), inc=None, inc_n=1):
        if e == self.E:
            for s, v in waits:
                if v > 0:
                    self.eng.wait_ge(self.sems[s], v)
            instr = fn(self.eng)
            if inc is not None and instr is not None:
                instr.then_inc(self.sems[inc], inc_n)
        if inc is not None:
            self.cnt[inc] += inc_n
        return self.cnt[inc] if inc is not None else None

    def wait(self, e, waits):
        if e == self.E:
            for s, v in waits:
                if v > 0:
                    self.eng.wait_ge(self.sems[s], v)

    def at(self, sem):
        return self.cnt[sem]


def setup_const_pool(nc, values):
    vals = sorted(set(float(v) for v in values))
    pool = nc.alloc_sbuf_tensor("constpool", [128, max(1, len(vals))], F32)
    for i, v in enumerate(vals):
        nc.const_aps.register(F32, v, pool[:, i:i + 1])
    return pool, vals


def emit_const_pool(sd, pool, vals, inc="dg"):
    for i, v in enumerate(vals):
        sd.op("gpsimd", lambda e, i=i, v=v: e.memset(pool[:, i:i + 1], v),
              inc=inc)
    return sd.at(inc)


def run_block(nc, sched, body):
    with nc.Block() as block:
        @block.sync
        def _(eng):
            sched.start("sync", eng)
            body(sched)

        @block.tensor
        def _(eng):
            sched.start("tensor", eng)
            body(sched)

        @block.vector
        def _(eng):
            sched.start("vector", eng)
            body(sched)

        @block.scalar
        def _(eng):
            sched.start("scalar", eng)
            body(sched)

        @block.gpsimd
        def _(eng):
            sched.start("gpsimd", eng)
            body(sched)



class Arena:

    def __init__(self, nc, base, limit):
        base = (base + 31) // 32 * 32
        self.nc, self.base, self.limit = nc, base, limit
        self.off = base
        self.prefix = ""
        self.peak = base

    def reset(self, prefix):
        self.off = self.base
        self.prefix = prefix

    def t(self, name, shape, dtype):
        esz = {F32: 4, BF16: 2}[dtype]
        fbytes = int(np.prod(shape[1:])) * esz
        fbytes = (fbytes + 31) // 32 * 32
        t = self.nc.alloc_sbuf_tensor_at(self.prefix + name, list(shape), dtype,
                                         offset=self.off)
        self.off += fbytes
        self.peak = max(self.peak, self.off)
        assert self.off <= self.limit, (self.prefix, name, self.off, self.limit)
        return t


def _mk_dram(nc, ctx):
    def dram(name, shape, dtype, kind):
        if ctx is not None:
            return ctx["D"][name]
        return nc.dram_tensor(name, shape, dtype, kind=kind).ap()
    return dram


def _mk_psum(nc, ctx):
    if ctx is not None:
        return ctx["PS"], ctx["psT"], ctx["psTf"]
    PS = [nc.alloc_psum_tensor(f"bank{i}", [128, 512], F32) for i in range(6)]
    psT = nc.alloc_psum_tensor("psT", [128, 128], BF16)
    psTf = nc.alloc_psum_tensor("psTf", [128, 128], F32)
    return PS, psT, psTf


def barrier(sd):
    vals = [(s, sd.at(s)) for s in sd.sems]
    for e in ("sync", "tensor", "vector", "scalar", "gpsimd"):
        sd.wait(e, vals)


def build_prep(ctx=None):
    nc = ctx["nc"] if ctx else bass.Bass("TRN2", target_bir_lowering=False)
    dram = _mk_dram(nc, ctx)
    x = dram("x", [B, D, S], F32, "ExternalInput")
    wfr = dram("wfr", [D, D], F32, "ExternalInput")
    wfi = dram("wfi", [D, D], F32, "ExternalInput")
    cF = {}
    for n in ("Fr", "Fi", "FiN", "Gr", "G2"):
        cF[n] = dram(n, [D, D], BF16, "ExternalInput")
    idb = dram("ident_bf", [128, 128], BF16, "ExternalInput")
    idf = dram("ident_f32", [128, 128], F32, "ExternalInput")
    sel8 = dram("sel8", [8, 1024], F32, "ExternalInput")
    inv1024 = dram("inv1024", [128, 1], F32, "ExternalInput")
    M_out = dram("M", [D, D], BF16, "ExternalOutput")
    xn_out = dram("xn", [B, D, S], BF16, "ExternalOutput")

    sb = ctx["arena"].t if ctx else nc.alloc_sbuf_tensor
    cFr = sb("cFr", [128, 8192], BF16)
    cFi = sb("cFi", [128, 8192], BF16)
    cFiN = sb("cFiN", [128, 8192], BF16)
    cGr = sb("cGr", [128, 8192], BF16)
    cG2 = sb("cG2", [128, 8192], BF16)
    Wr = sb("Wr", [128, 8192], BF16)
    Wi = sb("Wi", [128, 8192], BF16)
    Qr = sb("Qr", [128, 8192], BF16)
    Qi = sb("Qi", [128, 8192], BF16)
    idb_sb = sb("idb_sb", [128, 128], BF16)
    idf_sb = sb("idf_sb", [128, 128], F32)
    sel8_sb = sb("sel8_sb", [8, 1024], F32)
    i1024_sb = sb("i1024_sb", [128, 1], F32)
    x_sb = sb("x_sb", [128, 4096], F32)
    sq_sb = sb("sq_sb", [128, 4096], F32)
    xn_sb = sb("xn_sb", [128, 4096], BF16)
    stats_sb = sb("stats_sb", [128, 8], F32)
    var_sb = sb("var_sb", [128, 4], F32)
    msq2_sb = sb("msq2_sb", [128, 4], F32)
    ln_sb = sb("ln_sb", [128, 4], F32)
    fct_sb = sb("fct_sb", [128, 8], F32)
    rowT_sb = sb("rowT_sb", [8, 128], F32)
    PS, psT, psTf = _mk_psum(nc, ctx)
    psA, psB = PS[0], PS[1]
    psStat = PS[2]
    psBC0, psBC1 = PS[3], PS[4]
    cpool, cvals = ctx["cpool"] if ctx else setup_const_pool(nc, [1e-5])
    if True:
        sched = None if ctx else Sched(nc, ["dq", "dg", "pe", "v", "s"])

        def body(sd):
            op, at = sd.op, sd.at
            cpool_done = emit_const_pool(sd, cpool, cvals)
            for name, t in (("Fr", cFr), ("Fi", cFi), ("FiN", cFiN),
                            ("Gr", cGr), ("G2", cG2)):
                for k in range(8):
                    op("sync", lambda e, n=name, t=t, k=k: e.dma_start(
                        out=t[:, k * 1024:(k + 1) * 1024],
                        in_=cF[n][k * 128:(k + 1) * 128, :]), inc="dq", inc_n=16)
            op("sync", lambda e: e.dma_start(out=idb_sb[:], in_=idb), inc="dq", inc_n=16)
            op("sync", lambda e: e.dma_start(out=idf_sb[:], in_=idf), inc="dq", inc_n=16)
            op("sync", lambda e: e.dma_start(out=sel8_sb[:], in_=sel8), inc="dq", inc_n=16)
            op("sync", lambda e: e.dma_start(out=i1024_sb[:], in_=inv1024), inc="dq", inc_n=16)
            for src, t in ((wfr, Wr), (wfi, Wi)):
                for k in range(8):
                    op("gpsimd", lambda e, s=src, t=t, k=k: e.dma_start(
                        out=t[:, k * 1024:(k + 1) * 1024],
                        in_=s[k * 128:(k + 1) * 128, :]), inc="dg", inc_n=16)
            consts_done = at("dq")
            w_done = at("dg")

            def ln_tile(b, t0):
                for cch in range(8):
                    op("sync", lambda e, b=b, t0=t0, c=cch: e.dma_start(
                        out=x_sb[:, c * 512:(c + 1) * 512],
                        in_=x[b, c * 128:(c + 1) * 128, t0:t0 + 512]),
                       waits=([("v", at("v"))] if cch == 0 else ()),
                       inc="dq", inc_n=16)
                ld = at("dq")
                op("scalar", lambda e: e.activation(
                    out=sq_sb[:], in_=x_sb[:], func=AF.Square),
                   waits=[("dq", ld), ("v", at("v"))], inc="s")
                sq_ready = at("s")
                for jj in range(4):
                    for src, col in ((x_sb, 2 * jj), (sq_sb, 2 * jj + 1)):
                        for cch in range(8):
                            last = cch == 7
                            op("tensor", lambda e, src=src, col=col, jj=jj,
                               c=cch, f=(cch == 0), l=last: e.matmul(
                                   psStat[:, col:col + 1],
                                   src[:, c * 512 + jj * 128:c * 512 + jj * 128 + 128],
                                   i1024_sb[:], start=f, stop=l),
                               waits=([("s", sq_ready), ("dq", ld),
                                       ("v", at("v"))]
                                      if (col == 0 and cch == 0) else ()),
                               inc=("pe" if last else None))
                op("vector", lambda e: e.tensor_copy(stats_sb[:], psStat[:, 0:8]),
                   waits=[("pe", at("pe"))], inc="v")
                op("scalar", lambda e: e.activation(
                    out=msq2_sb[:], in_=stats_sb[:, 0:8:2], func=AF.Square),
                   waits=[("v", at("v"))], inc="s")
                op("vector", lambda e: e.tensor_sub(
                    var_sb[:], stats_sb[:, 1:8:2], msq2_sb[:]),
                   waits=[("s", at("s"))], inc="v")
                op("scalar", lambda e: e.activation(
                    out=ln_sb[:], in_=var_sb[:], func=AF.Ln, bias=1e-5),
                   waits=[("v", at("v")), ("dg", cpool_done)], inc="s")
                op("scalar", lambda e: e.activation(
                    out=fct_sb[:, 4:8], in_=ln_sb[:], func=AF.Exp,
                    scale=-0.5), inc="s")
                op("vector", lambda e: e.tensor_copy(
                    fct_sb[:, 0:4], stats_sb[:, 0:8:2]), inc="v")
                op("tensor", lambda e: e.transpose(
                    psTf[0:8, :], fct_sb[:, 0:8], idf_sb[:]),
                   waits=[("s", at("s")), ("v", at("v"))], inc="pe")
                op("vector", lambda e: e.tensor_copy(rowT_sb[:], psTf[0:8, :]),
                   waits=[("pe", at("pe"))], inc="v")
                rowv = at("v")
                for jj in range(4):
                    for w in range(2):
                        op("tensor", lambda e, jj=jj, w=w: e.matmul(
                            (psBC0 if w == 0 else psBC1)[:, jj * 128:(jj + 1) * 128],
                            sel8_sb[:, (jj + 4 * w) * 128:(jj + 4 * w + 1) * 128],
                            rowT_sb[:], start=True, stop=True),
                           waits=([("v", rowv)] if (jj == 0 and w == 0) else ()),
                           inc="pe")
                bc = at("pe")
                for cch in range(8):
                    op("vector", lambda e, c=cch: e.scalar_tensor_tensor(
                        sq_sb[:, c * 512:(c + 1) * 512], psBC0[:], -1.0,
                        x_sb[:, c * 512:(c + 1) * 512], ALU.mult, ALU.add),
                       waits=([("pe", bc)] if cch == 0 else ()), inc="v")
                    op("vector", lambda e, c=cch: e.tensor_mul(
                        xn_sb[:, c * 512:(c + 1) * 512],
                        sq_sb[:, c * 512:(c + 1) * 512],
                        psBC1[:]), inc="v")
                xn_ready = at("v")
                for cch in range(8):
                    op("sync", lambda e, b=b, t0=t0, c=cch: e.dma_start(
                        out=xn_out[b, c * 128:(c + 1) * 128, t0:t0 + 512],
                        in_=xn_sb[:, c * 512:(c + 1) * 512]),
                       waits=([("v", xn_ready)] if cch == 0 else ()),
                       inc="dq", inc_n=16)

            ln_iter = iter([(b, t0) for b in range(B)
                            for t0 in range(0, S, 512)])

            def maybe_ln():
                try:
                    b, t0 = next(ln_iter)
                except StopIteration:
                    return
                ln_tile(b, t0)

            evac_at = {0: 0, 1: 0}
            for dst, terms in ((Qr, ((cFr, Wr), (cFiN, Wi))),
                               (Qi, ((cFr, Wi), (cFi, Wr)))):
                for m in range(8):
                    for half in range(2):
                        bk = (m + half) % 2
                        ps = psA if bk == 0 else psB
                        nmm = 0
                        for k in range(8):
                            for A, Bm in terms:
                                first = nmm == 0
                                last = nmm == 15
                                op("tensor", lambda e, ps=ps, A=A, Bm=Bm, k=k, m=m,
                                   h=half, f=first, l=last: e.matmul(
                                       ps[:],
                                       A[:, k * 1024 + m * 128:k * 1024 + (m + 1) * 128],
                                       Bm[:, k * 1024 + h * 512:k * 1024 + h * 512 + 512],
                                       start=f, stop=l),
                                   waits=([("dq", consts_done), ("dg", w_done),
                                           ("s", evac_at[bk])]
                                          if nmm == 0 else ()),
                                   inc=("pe" if last else None))
                                nmm += 1
                        op("scalar", lambda e, ps=ps, dst=dst, m=m, h=half: e.copy(
                            out=dst[:, m * 1024 + h * 512:m * 1024 + h * 512 + 512],
                            in_=ps[:]),
                           waits=[("pe", at("pe"))], inc="s")
                        evac_at[bk] = at("s")
                        if half == 1:
                            maybe_ln()
            t_ev = 0
            for src, dst in ((Qr, Wr), (Qi, Wi)):
                for r in range(8):
                    for cc in range(8):
                        op("tensor", lambda e, src=src, r=r, cc=cc: e.transpose(
                            psT[:], src[:, cc * 1024 + r * 128:cc * 1024 + (r + 1) * 128],
                            idb_sb[:]),
                           waits=[("s", t_ev)], inc="pe")
                        op("scalar", lambda e, dst=dst, r=r, cc=cc: e.copy(
                            out=dst[:, r * 1024 + cc * 128:r * 1024 + (cc + 1) * 128],
                            in_=psT[:]),
                           waits=[("pe", at("pe"))], inc="s")
                        t_ev = at("s")
            for m in range(8):
                for half in range(2):
                    bk = (m + half) % 2
                    ps = psA if bk == 0 else psB
                    nmm = 0
                    for k in range(8):
                        for A, Bm in ((cGr, Wr), (cG2, Wi)):
                            first = nmm == 0
                            last = nmm == 15
                            op("tensor", lambda e, ps=ps, A=A, Bm=Bm, k=k, m=m,
                               h=half, f=first, l=last: e.matmul(
                                   ps[:],
                                   A[:, k * 1024 + m * 128:k * 1024 + (m + 1) * 128],
                                   Bm[:, k * 1024 + h * 512:k * 1024 + h * 512 + 512],
                                   start=f, stop=l),
                               waits=([("s", max(evac_at[bk], t_ev))]
                                      if nmm == 0 else ()),
                               inc=("pe" if last else None))
                            nmm += 1
                    op("scalar", lambda e, ps=ps, m=m, h=half: e.copy(
                        out=Qr[:, m * 1024 + h * 512:m * 1024 + h * 512 + 512],
                        in_=ps[:]),
                       waits=[("pe", at("pe"))], inc="s")
                    evac_at[bk] = at("s")
                    if half == 1:
                        maybe_ln()
            for r in range(8):
                for cc in range(8):
                    op("tensor", lambda e, r=r, cc=cc: e.transpose(
                        psT[:], Qr[:, cc * 1024 + r * 128:cc * 1024 + (r + 1) * 128],
                        idb_sb[:]),
                       waits=[("s", t_ev)], inc="pe")
                    op("scalar", lambda e, r=r, cc=cc: e.copy(
                        out=Qi[:, r * 1024 + cc * 128:r * 1024 + (cc + 1) * 128],
                        in_=psT[:]),
                       waits=[("pe", at("pe"))], inc="s")
                    t_ev = at("s")
            for r in range(8):
                op("sync", lambda e, r=r: e.dma_start(
                    out=M_out[r * 128:(r + 1) * 128, :],
                    in_=Qi[:, r * 1024:(r + 1) * 1024]),
                   waits=[("s", at("s"))], inc="dq", inc_n=16)
            while True:
                try:
                    b, t0 = next(ln_iter)
                except StopIteration:
                    break
                ln_tile(b, t0)
            sd.wait("gpsimd", [("dq", at("dq"))])

        if ctx is not None:
            return body
        run_block(nc, sched, body)
        sched.free()
    return nc


def build_attn0(dbg=False, ctx=None):
    nc = ctx["nc"] if ctx else bass.Bass("TRN2", target_bir_lowering=False)
    dram = _mk_dram(nc, ctx)
    xn = dram("xn", [B, D, S], BF16, "ExternalInput")
    wq_d = dram("wq", [D, D], F32, "ExternalInput")
    wk_d = dram("wk", [D, D], F32, "ExternalInput")
    wv_d = dram("wv", [D, D], F32, "ExternalInput")
    wo_d = dram("wo", [D, D], F32, "ExternalInput")
    wg1_d = dram("wg1", [D, 64], F32, "ExternalInput")
    wg2_d = dram("wg2", [64, D], F32, "ExternalInput")
    nw_d = dram("norm_w_pc", [128, 8], F32, "ExternalInput")
    cfm_d = dram("cfm", [D, S], BF16, "ExternalInput")
    sfm_d = dram("sfm", [D, S], BF16, "ExternalInput")
    ctm_d = dram("ctm", [S, D], BF16, "ExternalInput")
    stm_d = dram("stm", [S, D], BF16, "ExternalInput")
    maskU_d = dram("maskU", [128, 128], BF16, "ExternalInput")
    blk16_d = dram("blk16", [8 * 128, 16], F32, "ExternalInput")
    hexp_d = dram("hexp", [16, 8 * 128], BF16, "ExternalInput")
    a0_d = dram("a0", [B, D, S], F32, "ExternalOutput")
    if dbg:
        dbg_qt = dram("dbg_qt", [B, 4, 128, 4096], BF16, "ExternalOutput")
        dbg_e = dram("dbg_e", [B, 4, 128, 4096], BF16, "ExternalOutput")
        dbg_o = dram("dbg_o", [B, 4, 128, 4096], F32, "ExternalOutput")
        dbg_rq = dram("dbg_rq", [B, 4, 128, 4096], F32, "ExternalOutput")
        dbg_sc = dram("dbg_sc", [B, 4, 128, 512], F32, "ExternalOutput")
        dbg_carry = dram("dbg_carry", [B, 4, 128, 8], F32, "ExternalOutput")
        dbg_cs = dram("dbg_cs", [B, 4, 128, 4096], F32, "ExternalOutput")

    sb = ctx["arena"].t if ctx else nc.alloc_sbuf_tensor
    wq = sb("wq_s", [128, 8192], BF16)
    wk = sb("wk_s", [128, 8192], BF16)
    wv = sb("wv_s", [128, 8192], BF16)
    wo = sb("wo_s", [128, 8192], BF16)
    wg1 = sb("wg1_s", [128, 512], BF16)
    wg2 = sb("wg2_s", [64, 1024], BF16)
    nw = sb("nw_s", [128, 8], F32)
    blk16 = sb("blk16_s", [128, 128], F32)
    hexp = sb("hexp_s", [16, 1024], BF16)
    maskU = sb("maskU_s", [128, 128], BF16)
    ones_t = sb("ones_t", [128, 512], F32)
    xn_sc = sb("xn_sc", [128, 8192], BF16)
    cfm_t = sb("cfm_t", [128, 2048], BF16)
    sfm_t = sb("sfm_t", [128, 2048], BF16)
    ctm_t = sb("ctm_t", [128, 2048], BF16)
    stm_t = sb("stm_t", [128, 2048], BF16)
    e_bf = sb("e_bf", [128, 4096], BF16)
    csrq = sb("csrq", [128, 4096], F32)
    carry = [sb("carryA", [128, 8], F32), sb("carryB", [128, 8], F32)]
    rsum_scr = sb("rsum_scr", [128, 1], F32)
    qt_bf = sb("qt_bf", [128, 4096], BF16)
    e_scr = sb("e_scr", [128, 512], F32)
    ln_scr = sb("ln_scr", [128, 512], F32)
    t_scr = sb("t_scr", [128, 512], F32)
    u_scr = sb("u_scr", [128, 512], F32)
    v_tm = sb("v_tm", [128, 4096], BF16)
    etm_scr = sb("etm_scr", [128, 1024], BF16)
    kctm = sb("kctm", [128, 1024], BF16)
    kstm = sb("kstm", [128, 1024], BF16)
    qc_t = sb("qc_t", [128, 1024], BF16)
    qs_t = sb("qs_t", [128, 1024], BF16)
    kc_t = sb("kc_t", [128, 1024], BF16)
    ks_t = sb("ks_t", [128, 1024], BF16)
    atm = sb("atm", [128, 256], BF16)
    o_sb = sb("o_sb", [128, 4096], F32)
    Sc = [sb("Sc0", [128, 512], F32), sb("Sc1", [128, 512], F32)]
    Ss = [sb("Ss0", [128, 512], F32), sb("Ss1", [128, 512], F32)]
    Scb = sb("Scb", [128, 512], BF16)
    Ssb = sb("Ssb", [128, 512], BF16)
    g1_bf = sb("g1_bf", [64, 512], BF16)
    g2_sb = csrq
    sq_scr = sb("sq_scr", [128, 512], F32)
    rstd16 = sb("rstd16", [16, 512], F32)
    rstdb16 = sb("rstdb16", [16, 512], BF16)
    ln16 = sb("ln16", [16, 512], F32)
    nrm_bf = qt_bf
    a0_scr = sb("a0_scr", [128, 1024], F32)
    PS, psT, psTf = _mk_psum(nc, ctx)
    cpool, cvals = ctx["cpool"] if ctx else setup_const_pool(nc, [-0.6931472, 1e-6])

    sched = None if ctx else Sched(nc, ["dq", "dg", "pe", "v", "s"])

    def body(sd):
        op, at = sd.op, sd.at
        cp = emit_const_pool(sd, cpool, cvals)
        op("gpsimd", lambda e: e.memset(ones_t[:], 1.0), inc="dg")
        for t in (Sc[0], Sc[1], Ss[0], Ss[1], Scb, Ssb):
            op("gpsimd", lambda e, t=t: e.memset(t[:], 0.0), inc="dg")
        init_done = at("dg")
        for src, dst in ((wq_d, wq), (wk_d, wk), (wv_d, wv), (wo_d, wo)):
            for k in range(8):
                op("gpsimd", lambda e, s=src, d=dst, k=k: e.dma_start(
                    out=d[:, k * 1024:(k + 1) * 1024],
                    in_=s[k * 128:(k + 1) * 128, :]), inc="dg", inc_n=16)
        for k in range(8):
            op("gpsimd", lambda e, k=k: e.dma_start(
                out=wg1[:, k * 64:(k + 1) * 64],
                in_=wg1_d[k * 128:(k + 1) * 128, :]), inc="dg", inc_n=16)
        op("gpsimd", lambda e: e.dma_start(out=wg2[:], in_=wg2_d), inc="dg", inc_n=16)
        op("gpsimd", lambda e: e.dma_start(out=hexp[:], in_=hexp_d), inc="dg", inc_n=16)
        op("sync", lambda e: e.dma_start(out=nw[:], in_=nw_d), inc="dq", inc_n=16)
        for k in range(8):
            op("sync", lambda e, k=k: e.dma_start(
                out=blk16[:, k * 16:(k + 1) * 16],
                in_=blk16_d[k * 128:(k + 1) * 128, :]), inc="dq", inc_n=16)
        op("sync", lambda e: e.dma_start(out=maskU[:], in_=maskU_d), inc="dq", inc_n=16)
        w_done_g = at("dg")
        w_done_q = at("dq")

        NSC = S // 512
        dg_gate = at("dg")
        for b in range(B):
            if b > 0:
                st_reset_wait = [("pe", at("pe")), ("s", at("s"))]
                for t in (Sc[0], Sc[1], Ss[0], Ss[1], Scb, Ssb):
                    op("gpsimd", lambda e, t=t: e.memset(t[:], 0.0),
                       waits=st_reset_wait, inc="dg")
                    st_reset_wait = ()
                dg_gate = at("dg")
            op("vector", lambda e, b=b: e.memset(carry[0][:], 1.0),
               waits=[("dg", at("dg"))], inc="v")
            op("vector", lambda e, b=b: e.memset(carry[1][:], 1.0), inc="v")
            for sc in range(NSC):
                t0 = sc * 512
                xnb = (sc + b * NSC) % 2 * 4096
                if sc == 0 and b == 0:
                    for c in range(8):
                        op("sync", lambda e, b=b, t0=t0, c=c, xnb=xnb:
                           e.dma_start(
                               out=xn_sc[:, xnb + c * 512:xnb + (c + 1) * 512],
                               in_=xn[b, c * 128:(c + 1) * 128, t0:t0 + 512]),
                           inc="dq", inc_n=16)
                    xn_ld = at("dq")
                else:
                    xn_ld = xn_ld_next
                for m in range(8):
                    bank = PS[m % 2]
                    for k in range(8):
                        op("tensor", lambda e, m=m, k=k, bank=bank, xnb=xnb: e.matmul(
                            bank[:],
                            wk[:, k * 1024 + m * 128:k * 1024 + (m + 1) * 128],
                            xn_sc[:, xnb + k * 512:xnb + (k + 1) * 512],
                            start=(k == 0), stop=(k == 7)),
                           waits=([("dq", xn_ld), ("dg", max(w_done_g, dg_gate,
                                                              at("dg") if dbg else 0)),
                                   ("s", at("s")), ("v", at("v"))]
                                  if k == 0 else ()),
                           inc=("pe" if k == 7 else None))
                    op("scalar", lambda e, m=m, bank=bank: e.activation(
                        out=e_bf[:, m * 512:(m + 1) * 512], in_=bank[:],
                        func=AF.Exp), waits=[("pe", at("pe"))], inc="s")
                    op("scalar", lambda e, m=m, bank=bank: e.activation(
                        out=e_scr[:], in_=bank[:], func=AF.Exp,
                        accum_out=rsum_scr[:]), inc="s")
                    op("vector", lambda e, m=m, sc=sc: e.tensor_tensor_scan(
                        csrq[:, m * 512:(m + 1) * 512], ones_t[:], e_scr[:],
                        carry[sc % 2][:, m:m + 1], ALU.mult, ALU.add),
                       waits=[("s", at("s")), ("dg", init_done)], inc="v")
                    op("vector", lambda e, m=m, sc=sc: e.tensor_add(
                        carry[1 - sc % 2][:, m:m + 1], carry[sc % 2][:, m:m + 1],
                        rsum_scr[:]), waits=[("s", at("s"))], inc="v")
                    if dbg:
                        op("gpsimd", lambda e, b=b, sc=sc, m=m: e.dma_start(
                            out=dbg_cs[b, sc, :, m * 512:(m + 1) * 512],
                            in_=csrq[:, m * 512:(m + 1) * 512]),
                           waits=[("v", at("v"))], inc="dg", inc_n=16)
                    op("scalar", lambda e, m=m: e.activation(
                        out=ln_scr[:], in_=csrq[:, m * 512:(m + 1) * 512],
                        func=AF.Ln),
                       waits=[("v", at("v"))] + ([("dg", at("dg"))] if dbg else []),
                       inc="s")
                    op("scalar", lambda e, m=m: e.activation(
                        out=csrq[:, m * 512:(m + 1) * 512], in_=ln_scr[:],
                        func=AF.Exp, scale=-1.0, bias=-0.6931472),
                       waits=[("dg", cp)], inc="s")
                for m in range(8):
                    bank = PS[m % 2]
                    for k in range(8):
                        op("tensor", lambda e, m=m, k=k, bank=bank, xnb=xnb: e.matmul(
                            bank[:],
                            wq[:, k * 1024 + m * 128:k * 1024 + (m + 1) * 128],
                            xn_sc[:, xnb + k * 512:xnb + (k + 1) * 512],
                            start=(k == 0), stop=(k == 7)),
                           waits=([("s", at("s")), ("v", at("v"))]
                                  if k == 0 else ()),
                           inc=("pe" if k == 7 else None))
                    op("scalar", lambda e, bank=bank: e.activation(
                        out=t_scr[:], in_=bank[:], func=AF.Tanh, scale=0.5),
                       waits=[("pe", at("pe"))], inc="s")
                    op("vector", lambda e, bank=bank: e.scalar_tensor_tensor(
                        u_scr[:], t_scr[:], 1.0, bank[:], ALU.add, ALU.mult),
                       waits=[("s", at("s"))], inc="v")
                    op("vector", lambda e, m=m: e.tensor_mul(
                        qt_bf[:, m * 512:(m + 1) * 512], u_scr[:],
                        csrq[:, m * 512:(m + 1) * 512]), inc="v")
                for tc in range(4):
                    for half in range(2):
                        bank = PS[(tc * 2 + half) % 2]
                        for k in range(8):
                            op("tensor", lambda e, tc=tc, h=half, k=k, t0=t0,
                               bank=bank, xnb=xnb: e.matmul(
                                   bank[:],
                                   xn_sc[:, xnb + k * 512 + tc * 128:xnb + k * 512 + (tc + 1) * 128],
                                   wv[:, k * 1024 + h * 512:k * 1024 + h * 512 + 512],
                                   start=(k == 0), stop=(k == 7)),
                               waits=([("v", at("v")), ("s", at("s"))]
                                      if k == 0 else ()),
                               inc=("pe" if k == 7 else None))
                        op("scalar", lambda e, tc=tc, h=half, bank=bank:
                           e.activation(
                               out=v_tm[:, tc * 1024 + h * 512:tc * 1024 + h * 512 + 512],
                               in_=bank[:], func=AF.Identity),
                           waits=[("pe", at("pe"))], inc="s")
                if dbg:
                    op("gpsimd", lambda e, b=b, sc=sc: e.dma_start(
                        out=dbg_carry[b, sc], in_=carry[1 - sc % 2][:]),
                       waits=[("v", at("v"))], inc="dg", inc_n=16)
                    dw = [("v", at("v")), ("s", at("s"))]
                    for t_, d_ in ((qt_bf, dbg_qt), (e_bf, dbg_e), (csrq, dbg_rq)):
                        op("gpsimd", lambda e, t_=t_, d_=d_, b=b, sc=sc:
                           e.dma_start(out=d_[b, sc], in_=t_[:]),
                           waits=dw, inc="dg", inc_n=16)
                        dw = ()
                def trig_load(tau2):
                    tg2 = t0 + tau2 * 128
                    db2 = tau2 % 2
                    ld_w = [("v", at("v"))]
                    for s_, d_ in ((cfm_d, cfm_t), (sfm_d, sfm_t)):
                        for c in range(8):
                            op("sync", lambda e, s=s_, d=d_, db=db2, c=c,
                               tg=tg2: e.dma_start(
                                   out=d[:, db * 1024 + c * 128:db * 1024 + (c + 1) * 128],
                                   in_=s[c * 128:(c + 1) * 128, tg:tg + 128]),
                               waits=ld_w, inc="dq", inc_n=16)
                            ld_w = ()
                    for s_, d_ in ((ctm_d, ctm_t), (stm_d, stm_t)):
                        op("sync", lambda e, s=s_, d=d_, db=db2, tg=tg2:
                           e.dma_start(out=d[:, db * 1024:(db + 1) * 1024],
                                       in_=s[tg:tg + 128, :]),
                           inc="dq", inc_n=16)
                    return at("dq")

                trig_at = {0: trig_load(0)}
                nsc_all = b * NSC + sc + 1
                if nsc_all < B * NSC:
                    b2, sc2 = divmod(nsc_all, NSC)
                    t02 = sc2 * 512
                    xnb2 = nsc_all % 2 * 4096
                    pw = [("pe", at("pe"))]
                    for c in range(8):
                        op("sync", lambda e, b2=b2, t02=t02, c=c, xnb2=xnb2:
                           e.dma_start(
                               out=xn_sc[:, xnb2 + c * 512:xnb2 + (c + 1) * 512],
                               in_=xn[b2, c * 128:(c + 1) * 128, t02:t02 + 512]),
                           waits=pw, inc="dq", inc_n=16)
                        pw = ()
                    xn_ld_next = at("dq")
                for tau in range(4):
                    tg = t0 + tau * 128
                    db = tau % 2
                    trig_ld = trig_at[tau]
                    for c in range(8):
                        for dst, a_, b_ in ((qc_t, qt_bf, cfm_t), (qs_t, qt_bf, sfm_t),
                                            (kc_t, e_bf, cfm_t), (ks_t, e_bf, sfm_t)):
                            op("vector", lambda e, dst=dst, a_=a_, b_=b_, c=c,
                               db=db, tau=tau: e.tensor_mul(
                                   dst[:, c * 128:(c + 1) * 128],
                                   a_[:, c * 512 + tau * 128:c * 512 + (tau + 1) * 128],
                                   b_[:, db * 1024 + c * 128:db * 1024 + (c + 1) * 128]),
                               waits=([("dq", trig_ld)] if c == 0 and dst is qc_t
                                      else ()),
                               inc="v")
                    rot_done = at("v")
                    if tau < 3:
                        trig_at[tau + 1] = trig_load(tau + 1)
                    for half in range(2):
                        bank = PS[half % 2]
                        for k in range(8):
                            op("tensor", lambda e, h=half, k=k, tau=tau, bank=bank,
                               xnb=xnb:
                               e.matmul(
                                   bank[:],
                                   xn_sc[:, xnb + k * 512 + tau * 128:xnb + k * 512 + (tau + 1) * 128],
                                   wk[:, k * 1024 + h * 512:k * 1024 + h * 512 + 512],
                                   start=(k == 0), stop=(k == 7)),
                               waits=([("v", rot_done), ("s", at("s"))]
                                      if k == 0 else ()),
                               inc=("pe" if k == 7 else None))
                        op("scalar", lambda e, h=half, bank=bank: e.activation(
                            out=etm_scr[:, h * 512:h * 512 + 512], in_=bank[:],
                            func=AF.Exp), waits=[("pe", at("pe"))], inc="s")
                    etm_done = at("s")
                    for dst, trg in ((kctm, ctm_t), (kstm, stm_t)):
                        op("vector", lambda e, dst=dst, trg=trg, db=db:
                           e.tensor_mul(dst[:], etm_scr[:],
                                        trg[:, db * 1024:(db + 1) * 1024]),
                           waits=[("s", etm_done)], inc="v")
                    ktm_done = at("v")
                    mask_at, oevac_at, stadd_at = {}, {}, {}

                    def head_front(h):
                        par = (h % 2) * 64
                        hc_ = h // 2
                        bAT = PS[2 + h % 2]
                        hsl = slice(par, par + 64)
                        csl = slice(hc_ * 128, hc_ * 128 + 128)
                        at_w = [("v", mask_at[h - 2] if h >= 2 else rot_done)]
                        op("tensor", lambda e, h=h, bAT=bAT, hsl=hsl, csl=csl:
                           e.matmul(bAT[:, 0:128], kc_t[hsl, csl], qc_t[hsl, csl],
                                    start=True, stop=False),
                           waits=at_w, inc=None)
                        op("tensor", lambda e, h=h, bAT=bAT, hsl=hsl, csl=csl:
                           e.matmul(bAT[:, 0:128], ks_t[hsl, csl], qs_t[hsl, csl],
                                    start=False, stop=True), inc="pe")
                        op("vector", lambda e, h=h, bAT=bAT: e.tensor_mul(
                            atm[:, (h % 2) * 128:(h % 2) * 128 + 128],
                            bAT[:, 0:128], maskU[:]),
                           waits=[("pe", at("pe"))], inc="v")
                        mask_at[h] = at("v")

                    def head_tail(h):
                        par = (h % 2) * 64
                        hc_ = h // 2
                        bO = PS[4 + h % 2]
                        bKV = PS[0 + h % 2]
                        hsl = slice(par, par + 64)
                        csl = slice(hc_ * 128, hc_ * 128 + 128)
                        o_w = [("v", mask_at[h])] + (
                            [("s", oevac_at[h - 2])] if h >= 2 else [])
                        op("tensor", lambda e, h=h, bO=bO, tau=tau, par=par:
                           e.matmul(bO[par:par + 64, 0:128],
                                    v_tm[:, tau * 1024 + h * 64:tau * 1024 + (h + 1) * 64],
                                    atm[:, (h % 2) * 128:(h % 2) * 128 + 128],
                                    start=True, stop=False),
                           waits=o_w, inc=None)
                        op("tensor", lambda e, h=h, bO=bO, hsl=hsl, csl=csl,
                           par=par, hc_=hc_: e.matmul(
                               bO[par:par + 64, 0:128],
                               Scb[hsl, hc_ * 64:hc_ * 64 + 64], qc_t[hsl, csl],
                               start=False, stop=False), inc=None)
                        op("tensor", lambda e, h=h, bO=bO, hsl=hsl, csl=csl,
                           par=par, hc_=hc_: e.matmul(
                               bO[par:par + 64, 0:128],
                               Ssb[hsl, hc_ * 64:hc_ * 64 + 64], qs_t[hsl, csl],
                               start=False, stop=True), inc="pe")
                        op("scalar", lambda e, h=h, bO=bO, tau=tau, par=par:
                           e.activation(
                               out=o_sb[par:par + 64,
                                        (h // 2) * 512 + tau * 128:
                                        (h // 2) * 512 + (tau + 1) * 128],
                               in_=bO[par:par + 64, 0:128], func=AF.Identity),
                           waits=[("pe", at("pe"))], inc="s")
                        oevac_at[h] = at("s")
                        kv_w = [("v", stadd_at[h - 2] if h >= 2 else ktm_done)]
                        op("tensor", lambda e, h=h, bKV=bKV, tau=tau, par=par:
                           e.matmul(bKV[par:par + 64, 0:64],
                                    kctm[:, h * 64:(h + 1) * 64],
                                    v_tm[:, tau * 1024 + h * 64:tau * 1024 + (h + 1) * 64],
                                    start=True, stop=True),
                           waits=kv_w, inc=None)
                        op("tensor", lambda e, h=h, bKV=bKV, tau=tau, par=par:
                           e.matmul(bKV[par:par + 64, 64:128],
                                    kstm[:, h * 64:(h + 1) * 64],
                                    v_tm[:, tau * 1024 + h * 64:tau * 1024 + (h + 1) * 64],
                                    start=True, stop=True), inc="pe")
                        cur = (tau + sc * 4) % 2
                        nxt = 1 - cur
                        op("vector", lambda e, h=h, bKV=bKV, par=par, hc_=hc_,
                           cur=cur, nxt=nxt: e.tensor_add(
                               Sc[nxt][par:par + 64, hc_ * 64:hc_ * 64 + 64],
                               Sc[cur][par:par + 64, hc_ * 64:hc_ * 64 + 64],
                               bKV[par:par + 64, 0:64]),
                           waits=[("pe", at("pe"))], inc="v")
                        op("vector", lambda e, h=h, bKV=bKV, par=par, hc_=hc_,
                           cur=cur, nxt=nxt: e.tensor_add(
                               Ss[nxt][par:par + 64, hc_ * 64:hc_ * 64 + 64],
                               Ss[cur][par:par + 64, hc_ * 64:hc_ * 64 + 64],
                               bKV[par:par + 64, 64:128]), inc="v")
                        stadd_at[h] = at("v")
                        op("scalar", lambda e, h=h, par=par, hc_=hc_, nxt=nxt:
                           e.activation(
                               out=Scb[par:par + 64, hc_ * 64:hc_ * 64 + 64],
                               in_=Sc[nxt][par:par + 64, hc_ * 64:hc_ * 64 + 64],
                               func=AF.Identity),
                           waits=[("v", at("v"))], inc="s")
                        op("scalar", lambda e, h=h, par=par, hc_=hc_, nxt=nxt:
                           e.activation(
                               out=Ssb[par:par + 64, hc_ * 64:hc_ * 64 + 64],
                               in_=Ss[nxt][par:par + 64, hc_ * 64:hc_ * 64 + 64],
                               func=AF.Identity), inc="s")

                    for h in range(16):
                        head_front(h)
                        if h >= 1:
                            head_tail(h - 1)
                    head_tail(15)
                if dbg:
                    op("gpsimd", lambda e, b=b, sc=sc: e.dma_start(
                        out=dbg_o[b, sc], in_=o_sb[:]),
                       waits=[("s", at("s"))], inc="dg", inc_n=16)
                    op("gpsimd", lambda e, b=b, sc=sc: e.dma_start(
                        out=dbg_sc[b, sc], in_=Sc[(sc * 4 + 4) % 2][:]),
                       waits=[("v", at("v"))], inc="dg", inc_n=16)
                bank = PS[0]
                for k in range(8):
                    op("tensor", lambda e, k=k, bank=bank, xnb=xnb: e.matmul(
                        bank[0:64, :], wg1[:, k * 64:(k + 1) * 64],
                        xn_sc[:, xnb + k * 512:xnb + (k + 1) * 512],
                        start=(k == 0), stop=(k == 7)),
                       waits=([("v", at("v")), ("s", at("s"))] if k == 0 else ()),
                       inc=("pe" if k == 7 else None))
                op("scalar", lambda e, bank=bank: e.activation(
                    out=g1_bf[:], in_=bank[0:64, :], func=AF.Identity),
                   waits=[("pe", at("pe"))], inc="s")
                g1_done = at("s")
                for m in range(8):
                    bank = PS[1 + m % 2]
                    op("tensor", lambda e, m=m, bank=bank: e.matmul(
                        bank[:], wg2[:, m * 128:(m + 1) * 128], g1_bf[:],
                        start=True, stop=True),
                       waits=([("s", g1_done)] if m == 0 else
                              [("s", at("s"))]), inc="pe")
                    op("scalar", lambda e, m=m, bank=bank: e.activation(
                        out=t_scr[:], in_=bank[:], func=AF.Tanh, scale=0.5),
                       waits=[("pe", at("pe"))], inc="s")
                    op("vector", lambda e, m=m: e.scalar_tensor_tensor(
                        g2_sb[:, m * 512:(m + 1) * 512], t_scr[:], 1.0,
                        o_sb[:, m * 512:(m + 1) * 512], ALU.add, ALU.mult),
                       waits=[("s", at("s"))], inc="v")
                    op("scalar", lambda e, m=m: e.activation(
                        out=sq_scr[:], in_=g2_sb[:, m * 512:(m + 1) * 512],
                        func=AF.Square), waits=[("v", at("v"))], inc="s")
                    op("tensor", lambda e, m=m: e.matmul(
                        PS[3][0:16, :], blk16[:, m * 16:(m + 1) * 16], sq_scr[:],
                        start=(m == 0), stop=(m == 7)),
                       waits=[("s", at("s"))],
                       inc=("pe" if m == 7 else None))
                op("scalar", lambda e: e.activation(
                    out=ln16[:], in_=PS[3][0:16, :], func=AF.Ln,
                    scale=1.0 / 256.0, bias=1e-6),
                   waits=[("pe", at("pe")), ("dg", cp)], inc="s")
                op("scalar", lambda e: e.activation(
                    out=rstd16[:], in_=ln16[:], func=AF.Exp, scale=-0.5,
                    bias=-0.6931472), inc="s")
                op("scalar", lambda e: e.activation(
                    out=rstdb16[:], in_=rstd16[:], func=AF.Identity), inc="s")
                rstd_done = at("s")
                for c in range(8):
                    bank = PS[4 + c % 2]
                    op("tensor", lambda e, c=c, bank=bank: e.matmul(
                        bank[:], hexp[:, c * 128:(c + 1) * 128], rstdb16[:],
                        start=True, stop=True),
                       waits=([("s", rstd_done)] if c == 0 else
                              [("v", at("v"))]), inc="pe")
                    op("vector", lambda e, c=c, bank=bank: e.tensor_mul(
                        u_scr[:], g2_sb[:, c * 512:(c + 1) * 512], bank[:]),
                       waits=[("pe", at("pe"))], inc="v")
                    op("vector", lambda e, c=c: e.tensor_scalar(
                        nrm_bf[:, c * 512:(c + 1) * 512], u_scr[:],
                        nw[:, c:c + 1], None, ALU.mult), inc="v")
                nrm_done = at("v")
                a0dma_at = {}
                for m in range(8):
                    bank = PS[0 + m % 2]
                    for k in range(8):
                        op("tensor", lambda e, m=m, k=k, bank=bank: e.matmul(
                            bank[:],
                            wo[:, k * 1024 + m * 128:k * 1024 + (m + 1) * 128],
                            nrm_bf[:, k * 512:(k + 1) * 512],
                            start=(k == 0), stop=(k == 7)),
                           waits=([("v", nrm_done)] if (m == 0 and k == 0) else
                                  ([("v", at("v"))] if k == 0 else ())),
                           inc=("pe" if k == 7 else None))
                    op("vector", lambda e, m=m, bank=bank: e.tensor_copy(
                        a0_scr[:, (m % 2) * 512:(m % 2 + 1) * 512], bank[:]),
                       waits=[("pe", at("pe"))] + (
                           [("dq", a0dma_at[m - 2])] if m >= 2 else []),
                       inc="v")
                    a0dma_at[m] = op("sync", lambda e, b=b, m=m, t0=t0: e.dma_start(
                        out=a0_d[b, m * 128:(m + 1) * 128, t0:t0 + 512],
                        in_=a0_scr[:, (m % 2) * 512:(m % 2 + 1) * 512]),
                       waits=[("v", at("v"))], inc="dq", inc_n=16)
        sd.wait("gpsimd", [("dq", at("dq"))])

    if ctx is not None:
        return body
    run_block(nc, sched, body)
    sched.free()
    return nc


def build_mid(ctx=None):
    nc = ctx["nc"] if ctx else bass.Bass("TRN2", target_bir_lowering=False)
    dram = _mk_dram(nc, ctx)
    x = dram("x", [B, D, S], F32, "ExternalInput")
    a0 = dram("a0", [B, D, S], F32, "ExternalInput")
    xn0 = dram("xn0", [B, D, S], BF16, "ExternalInput")
    M_d = dram("M", [D, D], BF16, "ExternalInput")
    bias_d = dram("bias_pc", [128, 8], F32, "ExternalInput")
    idf = dram("ident_f32", [128, 128], F32, "ExternalInput")
    sel8 = dram("sel8", [8, 1024], F32, "ExternalInput")
    inv1024 = dram("inv1024", [128, 1], F32, "ExternalInput")
    x1_out = dram("x1", [B, D, S], F32, "ExternalOutput")
    xn1_out = dram("xn1", [B, D, S], BF16, "ExternalOutput")

    sb = ctx["arena"].t if ctx else nc.alloc_sbuf_tensor
    M_sb = sb("M_sb", [128, 8192], BF16)
    bias = sb("bias_sb", [128, 8], F32)
    idf_sb = sb("idf_sb", [128, 128], F32)
    sel8_sb = sb("sel8_sb", [8, 1024], F32)
    i1024_sb = sb("i1024_sb", [128, 1], F32)
    x_sb = sb("x_sb", [128, 8192], F32)
    a0_sb = sb("a0_sb", [128, 8192], F32)
    xn0_sb = sb("xn0_sb", [128, 8192], BF16)
    x1_sb = sb("x1_sb", [128, 8192], F32)
    sq_sb = sb("sq_sb", [128, 4096], F32)
    u_sb = sb("u_sb", [128, 4096], F32)
    xn_sb = sb("xn_sb", [128, 4096], BF16)
    stats_sb = sb("stats_sb", [128, 8], F32)
    var_sb = sb("var_sb", [128, 4], F32)
    msq2_sb = sb("msq2_sb", [128, 4], F32)
    ln_sb = sb("ln_sb", [128, 4], F32)
    fct_sb = sb("fct_sb", [128, 8], F32)
    rowT_sb = sb("rowT_sb", [8, 128], F32)
    PS, psT, psTf = _mk_psum(nc, ctx)
    psA, psB = PS[0], PS[1]
    psStat = PS[2]
    psBC0, psBC1 = PS[3], PS[4]
    cpool, cvals = ctx["cpool"] if ctx else setup_const_pool(nc, [1e-5])

    sched = None if ctx else Sched(nc, ["dq", "dg", "pe", "v", "s"])
    tiles = [(b, t0) for b in range(B) for t0 in range(0, S, 512)]

    def body(sd):
        op, at = sd.op, sd.at
        cp = emit_const_pool(sd, cpool, cvals)
        for k in range(8):
            op("gpsimd", lambda e, k=k: e.dma_start(
                out=M_sb[:, k * 1024:(k + 1) * 1024],
                in_=M_d[k * 128:(k + 1) * 128, :]), inc="dg", inc_n=16)
        op("sync", lambda e: e.dma_start(out=bias[:], in_=bias_d), inc="dq", inc_n=16)
        op("sync", lambda e: e.dma_start(out=idf_sb[:], in_=idf), inc="dq", inc_n=16)
        op("sync", lambda e: e.dma_start(out=sel8_sb[:], in_=sel8), inc="dq", inc_n=16)
        op("sync", lambda e: e.dma_start(out=i1024_sb[:], in_=inv1024), inc="dq", inc_n=16)
        w_g, w_q = at("dg"), at("dq")
        ld_at, x1_at = {}, {}

        def front(i):
            b, t0 = tiles[i]
            o = i % 2 * 4096
            lw = [("v", at("v")), ("s", at("s"))]
            for src_, dst in ((x, x_sb), (a0, a0_sb), (xn0, xn0_sb)):
                for c in range(8):
                    op("sync", lambda e, s=src_, d=dst, b=b, t0=t0, c=c, o=o:
                       e.dma_start(out=d[:, o + c * 512:o + (c + 1) * 512],
                                   in_=s[b, c * 128:(c + 1) * 128, t0:t0 + 512]),
                       waits=lw, inc="dq", inc_n=16)
                    lw.clear()
            ld = at("dq")
            for m in range(8):
                bank = psA if m % 2 == 0 else psB
                for k in range(8):
                    op("tensor", lambda e, m=m, k=k, bank=bank, o=o: e.matmul(
                        bank[:],
                        M_sb[:, k * 1024 + m * 128:k * 1024 + (m + 1) * 128],
                        xn0_sb[:, o + k * 512:o + (k + 1) * 512],
                        start=(k == 0), stop=(k == 7)),
                       waits=([("dq", ld), ("dg", w_g), ("v", at("v"))]
                              if k == 0 else ()),
                       inc=("pe" if k == 7 else None))
                op("vector", lambda e, m=m, bank=bank, o=o: e.scalar_tensor_tensor(
                    u_sb[:, m * 512:(m + 1) * 512], bank[:], bias[:, m:m + 1],
                    a0_sb[:, o + m * 512:o + (m + 1) * 512], ALU.add, ALU.add),
                   waits=[("pe", at("pe"))], inc="v")
                op("vector", lambda e, m=m, o=o: e.tensor_add(
                    x1_sb[:, o + m * 512:o + (m + 1) * 512],
                    u_sb[:, m * 512:(m + 1) * 512],
                    x_sb[:, o + m * 512:o + (m + 1) * 512]), inc="v")
            x1_done = at("v")
            for c in range(8):
                op("sync", lambda e, b=b, t0=t0, c=c, o=o: e.dma_start(
                    out=x1_out[b, c * 128:(c + 1) * 128, t0:t0 + 512],
                    in_=x1_sb[:, o + c * 512:o + (c + 1) * 512]),
                   waits=([("v", x1_done)] if c == 0 else ()),
                   inc="dq", inc_n=16)
            x1_at[i] = x1_done

        def tail(i):
            b, t0 = tiles[i]
            o = i % 2 * 4096
            op("scalar", lambda e, o=o: e.activation(
                out=sq_sb[:], in_=x1_sb[:, o:o + 4096], func=AF.Square),
               waits=[("v", at("v"))], inc="s")
            sq_ready = at("s")
            for jj in range(4):
                for src_, col, off in ((x1_sb, 0, o), (sq_sb, 1, 0)):
                    for cch in range(8):
                        last = cch == 7
                        op("tensor", lambda e, src=src_, col=2 * jj + col,
                           jj=jj, c=cch, off=off, f=(cch == 0), l=last: e.matmul(
                               psStat[:, col:col + 1],
                               src[:, off + c * 512 + jj * 128:
                                   off + c * 512 + jj * 128 + 128],
                               i1024_sb[:], start=f, stop=l),
                           waits=([("s", sq_ready), ("v", at("v"))]
                                  if (col == 2 * jj and cch == 0) else ()),
                           inc=("pe" if last else None))
            op("vector", lambda e: e.tensor_copy(stats_sb[:], psStat[:, 0:8]),
               waits=[("pe", at("pe"))], inc="v")
            op("scalar", lambda e: e.activation(
                out=msq2_sb[:], in_=stats_sb[:, 0:8:2], func=AF.Square),
               waits=[("v", at("v"))], inc="s")
            op("vector", lambda e: e.tensor_sub(
                var_sb[:], stats_sb[:, 1:8:2], msq2_sb[:]),
               waits=[("s", at("s"))], inc="v")
            op("scalar", lambda e: e.activation(
                out=ln_sb[:], in_=var_sb[:], func=AF.Ln, bias=1e-5),
               waits=[("v", at("v")), ("dg", cp)], inc="s")
            op("scalar", lambda e: e.activation(
                out=fct_sb[:, 4:8], in_=ln_sb[:], func=AF.Exp,
                scale=-0.5), inc="s")
            op("vector", lambda e: e.tensor_copy(
                fct_sb[:, 0:4], stats_sb[:, 0:8:2]), inc="v")
            op("tensor", lambda e: e.transpose(
                psTf[0:8, :], fct_sb[:, 0:8], idf_sb[:]),
               waits=[("s", at("s")), ("v", at("v"))], inc="pe")
            op("vector", lambda e: e.tensor_copy(rowT_sb[:], psTf[0:8, :]),
               waits=[("pe", at("pe"))], inc="v")
            rowv = at("v")
            for jj in range(4):
                for w in range(2):
                    op("tensor", lambda e, jj=jj, w=w: e.matmul(
                        (psBC0 if w == 0 else psBC1)[:, jj * 128:(jj + 1) * 128],
                        sel8_sb[:, (jj + 4 * w) * 128:(jj + 4 * w + 1) * 128],
                        rowT_sb[:], start=True, stop=True),
                       waits=([("v", rowv)] if (jj == 0 and w == 0) else ()),
                       inc="pe")
            bc = at("pe")
            for cch in range(8):
                op("vector", lambda e, c=cch, o=o: e.scalar_tensor_tensor(
                    sq_sb[:, c * 512:(c + 1) * 512], psBC0[:], -1.0,
                    x1_sb[:, o + c * 512:o + (c + 1) * 512], ALU.mult, ALU.add),
                   waits=([("pe", bc)] if cch == 0 else ()), inc="v")
                op("vector", lambda e, c=cch: e.tensor_mul(
                    xn_sb[:, c * 512:(c + 1) * 512],
                    sq_sb[:, c * 512:(c + 1) * 512],
                    psBC1[:]), inc="v")
            xn_ready = at("v")
            for cch in range(8):
                op("sync", lambda e, b=b, t0=t0, c=cch: e.dma_start(
                    out=xn1_out[b, c * 128:(c + 1) * 128, t0:t0 + 512],
                    in_=xn_sb[:, c * 512:(c + 1) * 512]),
                   waits=([("v", xn_ready)] if cch == 0 else ()),
                   inc="dq", inc_n=16)

        for i in range(len(tiles)):
            front(i)
            if i >= 1:
                tail(i - 1)
        tail(len(tiles) - 1)
        sd.wait("gpsimd", [("dq", at("dq"))])

    if ctx is not None:
        return body
    run_block(nc, sched, body)
    sched.free()
    return nc

def build_attn1(phases=('t1','t2','t3','t4','t5'), dbg=False, ctx=None):
    nc = ctx["nc"] if ctx else bass.Bass("TRN2", target_bir_lowering=False)
    dram = _mk_dram(nc, ctx)
    xn = dram("xn1", [B, D, S], BF16, "ExternalInput")
    x1_d = dram("x1", [B, D, S], F32, "ExternalInput")
    wq_d = dram("wq", [D, D], F32, "ExternalInput")
    wk_d = dram("wk", [D, D], F32, "ExternalInput")
    wv_d = dram("wv", [D, D], F32, "ExternalInput")
    wgq_d = dram("wgq", [D, 512], F32, "ExternalInput")
    wgk_d = dram("wgk", [D, 512], F32, "ExternalInput")
    wgv_d = dram("wgv", [D, 512], F32, "ExternalInput")
    rout_d = dram("r_out_w", [512, D], F32, "ExternalInput")
    wout_d = dram("w_out", [D, D], F32, "ExternalInput")
    glgr_d = dram("glgr", [128, 32], F32, "ExternalInput")
    maskU_d = dram("maskU", [128, 128], BF16, "ExternalInput")
    maskLs_d = dram("maskLs", [128, 128], BF16, "ExternalInput")
    lmask_d = dram("lmask", [64, S], BF16, "ExternalInput")
    hexp_d = dram("hexp", [16, 8 * 128], BF16, "ExternalInput")
    lexp_d = dram("lexp", [8, 4 * 128], BF16, "ExternalInput")
    idf_d = dram("ident_f32", [128, 128], F32, "ExternalInput")
    onesb_d = dram("ones_bf", [128, 1], BF16, "ExternalInput")
    a1_d = dram("a1", [B, D, S], F32, "ExternalOutput")
    if dbg:
        dbg_fct = dram("dbg_fct", [2, 128, 56], F32, "ExternalOutput")
        dbg_fs = dram("dbg_fs", [2, 128, 56], F32, "ExternalOutput")
        dbg_loc = dram("dbg_loc", [2, 128, 1024], F32, "ExternalOutput")
        dbg_rl = dram("dbg_rl", [2, 128, 512], BF16, "ExternalOutput")
        dbg_comb = dram("dbg_comb", [2, 128, 1024], BF16, "ExternalOutput")
        dbg_pt = dram("dbg_pt", [2, 128, 768], BF16, "ExternalOutput")
        dbg_k = dram("dbg_k", [2, 128, 4096], BF16, "ExternalOutput")
        dbg_v = dram("dbg_v", [2, 128, 4096], BF16, "ExternalOutput")

    sb = ctx["arena"].t if ctx else nc.alloc_sbuf_tensor
    wq = sb("wq_s", [128, 8192], BF16)
    wk = sb("wk_s", [128, 8192], BF16)
    wv = sb("wv_s", [128, 8192], BF16)
    wout = sb("wout_s", [128, 8192], BF16)
    wgq = sb("wgq_s", [128, 4096], BF16)
    wgk = sb("wgk_s", [128, 4096], BF16)
    wgv = sb("wgv_s", [128, 4096], BF16)
    rout = sb("rout_s", [128, 4096], BF16)
    glgr = sb("glgr_s", [128, 32], BF16)
    maskU = sb("maskU_s", [128, 128], BF16)
    maskLs = sb("maskLs_s", [128, 128], BF16)
    lmask = sb("lmask_s", [64, 2048], BF16)
    hexp = sb("hexp_s", [16, 1024], BF16)
    lexp = sb("lexp_s", [8, 512], BF16)
    idf_sb = sb("idf_s", [128, 128], F32)
    onesb = sb("onesb_s", [128, 1], BF16)
    xn_st = sb("xn_st", [128, 4096], BF16)
    x1_st = sb("x1_st", [128, 4096], F32)
    Q_bf = sb("Q_bf", [128, 4096], BF16)
    K_bf = sb("K_bf", [128, 4096], BF16)
    v_tm = sb("v_tm1", [128, 4096], BF16)
    K_tail = sb("K_tail", [128, 2048], BF16)
    V_tail = sb("V_tail", [128, 2048], BF16)
    GQ_bf = sb("GQ_bf", [128, 2048], BF16)
    xpool = sb("xpool", [128, 512], F32)
    xpool_bf = sb("xpool_bf", [128, 512], BF16)
    GK_sb = sb("GK_sb", [128, 256], BF16)
    GV_sb = sb("GV_sb", [64, 512], BF16)
    P_T = sb("P_T", [128, 768], BF16)
    eL_T = sb("eL_T", [64, 256], BF16)
    local_sb = sb("local_sb", [128, 1024], F32)
    remlat_sb = sb("remlat_sb", [128, 512], BF16)
    remn_sb = sb("remn_sb", [128, 512], BF16)
    rem_sb = sb("rem_sb", [128, 256], F32)
    fct = sb("fct", [128, 56], F32)
    f_scr = sb("f_scr", [128, 56], F32)
    fT = sb("fT", [16, 256], BF16)
    rlT = sb("rlT", [8, 128], BF16)
    comb_bf = sb("comb_bf", [128, 1024], BF16)
    c1_sb = sb("c1_sb", [128, 128], F32)
    rem2_sb = sb("rem2_sb", [128, 128], F32)
    tg_sb = sb("tg_sb", [128, 32], F32)
    a1_scr = sb("a1_scr", [128, 2048], F32)
    PS, psT, psTf = _mk_psum(nc, ctx)

    sched = None if ctx else Sched(nc, ["dq", "dg", "pe", "v", "s"])

    def body(sd):
        op, at = sd.op, sd.at
        for src, dst, n in ((wq_d, wq, 8), (wk_d, wk, 8), (wv_d, wv, 8),
                            (wout_d, wout, 8), (wgq_d, wgq, 8), (wgk_d, wgk, 8),
                            (wgv_d, wgv, 8), (rout_d, rout, 4)):
            w = 1024 if n == 8 and dst in (wq, wk, wv, wout) else 512
            w = dst.shape[1] // n
            for k in range(n):
                op("gpsimd", lambda e, s=src, d=dst, k=k, w=w: e.dma_start(
                    out=d[:, k * w:(k + 1) * w],
                    in_=s[k * 128:(k + 1) * 128, :]), inc="dg", inc_n=16)
        op("gpsimd", lambda e: e.dma_start(out=glgr[:], in_=glgr_d),
           inc="dg", inc_n=16)
        for src, dst in ((maskU_d, maskU), (maskLs_d, maskLs), (lmask_d, lmask),
                         (hexp_d, hexp), (lexp_d, lexp), (idf_d, idf_sb),
                         (onesb_d, onesb)):
            op("sync", lambda e, s=src, d=dst: e.dma_start(out=d[:], in_=s),
               inc="dq", inc_n=16)
        w_g, w_q = at("dg"), at("dq")

        NST = S // 512
        fT_ev_prev = 0
        a1dma_at = {}
        for b in range(B):
            op("gpsimd", lambda e: e.memset(xpool[:], 0.0),
               waits=[("pe", at("pe")), ("s", at("s")), ("v", at("v"))],
               inc="dg")
            xp_init = at("dg")
            for st in range(NST):
                t0 = st * 512
                if st > 0:
                    tw = [("pe", at("pe"))]
                    for c in range(8):
                        op("gpsimd", lambda e, c=c: e.tensor_copy(
                            K_tail[:, c * 256:(c + 1) * 256],
                            K_bf[:, c * 512 + 256:(c + 1) * 512]),
                           waits=tw, inc="dg")
                        tw = ()
                    op("gpsimd", lambda e: e.tensor_copy(
                        V_tail[:], v_tm[:, 2048:4096]), inc="dg")
                tails = at("dg")
                lw = [("pe", at("pe")), ("v", at("v")), ("s", at("s"))]
                for c in range(8):
                    op("sync", lambda e, b=b, t0=t0, c=c: e.dma_start(
                        out=xn_st[:, c * 512:(c + 1) * 512],
                        in_=xn[b, c * 128:(c + 1) * 128, t0:t0 + 512]),
                       waits=lw, inc="dq", inc_n=16)
                    lw = ()
                for c in range(8):
                    op("sync", lambda e, b=b, t0=t0, c=c: e.dma_start(
                        out=x1_st[:, c * 512:(c + 1) * 512],
                        in_=x1_d[b, c * 128:(c + 1) * 128, t0:t0 + 512]),
                       inc="dq", inc_n=16)
                ld = at("dq")
                for wsrc, dst in ((wq, Q_bf), (wk, K_bf)):
                    for m in range(8):
                        bank = PS[m % 2]
                        for k in range(8):
                            op("tensor", lambda e, wsrc=wsrc, m=m, k=k, bank=bank:
                               e.matmul(
                                   bank[:],
                                   wsrc[:, k * 1024 + m * 128:k * 1024 + (m + 1) * 128],
                                   xn_st[:, k * 512:(k + 1) * 512],
                                   start=(k == 0), stop=(k == 7)),
                               waits=([("dq", ld), ("dg", max(w_g, tails)),
                                       ("s", at("s")), ("v", at("v"))]
                                      if k == 0 else ()),
                               inc=("pe" if k == 7 else None))
                        op("scalar", lambda e, dst=dst, m=m, bank=bank:
                           e.activation(out=dst[:, m * 512:(m + 1) * 512],
                                        in_=bank[:], func=AF.Identity),
                           waits=[("pe", at("pe"))], inc="s")
                for tc in range(4):
                    for half in range(2):
                        bank = PS[(tc * 2 + half) % 2]
                        for k in range(8):
                            op("tensor", lambda e, tc=tc, h=half, k=k, bank=bank:
                               e.matmul(
                                   bank[:],
                                   xn_st[:, k * 512 + tc * 128:k * 512 + (tc + 1) * 128],
                                   wv[:, k * 1024 + h * 512:k * 1024 + h * 512 + 512],
                                   start=(k == 0), stop=(k == 7)),
                               waits=([("s", at("s"))] if k == 0 else ()),
                               inc=("pe" if k == 7 else None))
                        op("scalar", lambda e, tc=tc, h=half, bank=bank:
                           e.activation(
                               out=v_tm[:, tc * 1024 + h * 512:tc * 1024 + h * 512 + 512],
                               in_=bank[:], func=AF.Identity),
                           waits=[("pe", at("pe"))], inc="s")
                for gm in range(4):
                    bank = PS[gm % 2]
                    for k in range(8):
                        op("tensor", lambda e, gm=gm, k=k, bank=bank: e.matmul(
                            bank[:],
                            wgq[:, k * 512 + gm * 128:k * 512 + (gm + 1) * 128],
                            xn_st[:, k * 512:(k + 1) * 512],
                            start=(k == 0), stop=(k == 7)),
                           waits=([("s", at("s"))] if k == 0 else ()),
                           inc=("pe" if k == 7 else None))
                    op("scalar", lambda e, gm=gm, bank=bank: e.activation(
                        out=GQ_bf[:, gm * 512:(gm + 1) * 512], in_=bank[:],
                        func=AF.Identity),
                       waits=[("pe", at("pe"))], inc="s")
                for c in range(8):
                    op("vector", lambda e, c=c, st=st: e.reduce_sum(
                        xpool[:, c * 64 + st * 16:c * 64 + (st + 1) * 16],
                        xn_st[:, c * 512:(c + 1) * 512].rearrange(
                            "p (g k) -> p g k", k=32), axis=AX),
                       waits=([("dg", xp_init), ("dq", ld)] if c == 0 else ()),
                       inc="v")
                xp_done = at("v")
                op("scalar", lambda e: e.activation(
                    out=xpool_bf[:], in_=xpool[:], func=AF.Identity),
                   waits=[("v", xp_done)], inc="s")
                xpb_done = at("s")
                for gc in range(4):
                    bank = PS[2 + gc % 2]
                    for k in range(8):
                        op("tensor", lambda e, gc=gc, k=k, bank=bank: e.matmul(
                            bank[:, 0:64],
                            wgk[:, k * 512 + gc * 128:k * 512 + (gc + 1) * 128],
                            xpool_bf[:, k * 64:(k + 1) * 64],
                            start=(k == 0), stop=(k == 7)),
                           waits=([("s", at("s"))]
                                  if k == 0 else ()),
                           inc=("pe" if k == 7 else None))
                    op("scalar", lambda e, gc=gc, bank=bank: e.activation(
                        out=GK_sb[:, gc * 64:(gc + 1) * 64], in_=bank[:, 0:64],
                        func=AF.Identity, scale=1.0 / 32.0),
                       waits=[("pe", at("pe"))], inc="s")
                bank = PS[4]
                for k in range(8):
                    op("tensor", lambda e, k=k, bank=bank: e.matmul(
                        bank[0:64, :], xpool_bf[:, k * 64:(k + 1) * 64],
                        wgv[:, k * 512:(k + 1) * 512],
                        start=(k == 0), stop=(k == 7)),
                       waits=([("s", at("s"))] if k == 0 else ()),
                       inc=("pe" if k == 7 else None))
                op("scalar", lambda e, bank=bank: e.activation(
                    out=GV_sb[:], in_=bank[0:64, :], func=AF.Identity,
                    scale=1.0 / 32.0),
                   waits=[("pe", at("pe"))], inc="s")
                proj_done_s = at("s")

                for tb in range(4):
                    if not phases:
                        continue
                    loc_ev_at, lat_ev_at = {}, {}
                    tbg = st * 4 + tb
                    tkey = b * 16 + tbg
                    tg = t0 + tb * 128
                    dbf = tbg % 2
                    kbs = [kb for kb in (tbg - 2, tbg - 1, tbg) if kb >= 0]
                    exp_at, pmask_at = {}, {}
                    tile_start_s = at("s")

                    def t1_front(h):
                        par = (h % 2) * 64
                        hc_ = h // 2
                        bS = PS[h % 2]
                        sl = (h % 2) * 384
                        first_w = [("s", exp_at[h - 2] if h >= 2
                                    else tile_start_s)] + (
                            [("dg", at("dg"))] if (dbg and h == 0) else [])
                        for ci, kb in enumerate(kbs):
                            if kb >= st * 4:
                                kloc = kb - st * 4
                                kap = K_bf[par:par + 64,
                                           hc_ * 512 + kloc * 128:
                                           hc_ * 512 + (kloc + 1) * 128]
                            else:
                                kloc = kb - (st * 4 - 2)
                                kap = K_tail[par:par + 64,
                                             hc_ * 256 + kloc * 128:
                                             hc_ * 256 + (kloc + 1) * 128]
                            op("tensor", lambda e, kap=kap, h=h, ci=ci, bS=bS,
                               par=par, hc_=hc_, tb=tb: e.matmul(
                                   bS[:, ci * 128:(ci + 1) * 128], kap,
                                   Q_bf[par:par + 64,
                                        hc_ * 512 + tb * 128:hc_ * 512 + (tb + 1) * 128],
                                   start=True, stop=True),
                               waits=(first_w if ci == 0 else ()), inc=None)
                            first_w = ()
                        op("tensor", lambda e, h=h, par=par, hc_=hc_, tb=tb:
                           e.matmul(
                               PS[4][:, 24 + 2 * h:26 + 2 * h],
                               Q_bf[par:par + 64,
                                    hc_ * 512 + tb * 128:hc_ * 512 + (tb + 1) * 128],
                               glgr[par:par + 64, 2 * h:2 * h + 2],
                               start=True, stop=True), inc="pe")
                        pe_sc = at("pe")
                        op("scalar", lambda e, h=h, bS=bS, sl=sl, n=len(kbs):
                           e.activation(
                               out=P_T[:, sl:sl + n * 128],
                               in_=bS[:, 0:n * 128], func=AF.Exp, scale=0.125),
                           waits=[("pe", pe_sc)], inc="s")
                        exp_at[h] = at("s")
                        if len(kbs) == 3:
                            op("vector", lambda e, sl=sl: e.tensor_mul(
                                P_T[:, sl:sl + 128],
                                P_T[:, sl:sl + 128], maskLs[:]),
                               waits=[("s", at("s"))], inc="v")
                        ci_last = len(kbs) - 1
                        op("vector", lambda e, sl=sl, ci=ci_last: e.tensor_mul(
                            P_T[:, sl + ci * 128:sl + (ci + 1) * 128],
                            P_T[:, sl + ci * 128:sl + (ci + 1) * 128],
                            maskU[:]),
                           waits=[("s", at("s"))], inc="v")
                        pmask_at[h] = at("v")

                    def t1_tail(h):
                        par = (h % 2) * 64
                        hc_ = h // 2
                        bL = PS[2 + (h // 2) % 2]
                        sl = (h % 2) * 384
                        for ci in range(len(kbs)):
                            op("tensor", lambda e, h=h, ci=ci, sl=sl,
                               f=(ci == 0), l=(ci == len(kbs) - 1): e.matmul(
                                   PS[4][:, h:h + 1],
                                   P_T[:, sl + ci * 128:sl + (ci + 1) * 128],
                                   onesb[:], start=f, stop=l),
                               waits=([("v", pmask_at[h])] if ci == 0 else ()),
                               inc=None)
                        for ci, kb in enumerate(kbs):
                            if kb >= st * 4:
                                kloc = kb - st * 4
                                vap = v_tm[:, kloc * 1024 + h * 64:
                                           kloc * 1024 + (h + 1) * 64]
                            else:
                                kloc = kb - (st * 4 - 2)
                                vap = V_tail[:, kloc * 1024 + h * 64:
                                             kloc * 1024 + (h + 1) * 64]
                            op("tensor", lambda e, vap=vap, h=h, ci=ci, bL=bL,
                               par=par, sl=sl, f=(ci == 0),
                               l=(ci == len(kbs) - 1): e.matmul(
                                   bL[par:par + 64, 0:128], vap,
                                   P_T[:, sl + ci * 128:sl + (ci + 1) * 128],
                                   start=f, stop=l),
                               waits=([("s", loc_ev_at[h - 4])]
                                      if (ci == 0 and h >= 4) else ()),
                               inc=("pe" if ci == len(kbs) - 1 else None))
                        op("scalar", lambda e, h=h, bL=bL, par=par, hc_=hc_:
                           e.activation(
                               out=local_sb[par:par + 64, hc_ * 128:(hc_ + 1) * 128],
                               in_=bL[par:par + 64, 0:128], func=AF.Identity),
                           waits=[("pe", at("pe"))], inc="s")
                        loc_ev_at[h] = at("s")

                    if 't1' in phases:
                        for h in range(16):
                            t1_front(h)
                            if h >= 1:
                                t1_tail(h - 1)
                        t1_tail(15)
                    lexp_at, lmask_at = {}, {}

                    def t2_front(lh):
                        par = (lh % 2) * 64
                        lc = lh // 2
                        bS = PS[0 + lh % 2]
                        sl = (lh % 2) * 128
                        op("tensor", lambda e, lh=lh, par=par, lc=lc, bS=bS,
                           tb=tb: e.matmul(
                               bS[0:64, 0:128],
                               GK_sb[par:par + 64, lc * 64:(lc + 1) * 64],
                               GQ_bf[par:par + 64,
                                     lc * 512 + tb * 128:lc * 512 + (tb + 1) * 128],
                               start=True, stop=True),
                           waits=[("v", lat_ev_at[lh - 2] if lh >= 2
                                   else at("v"))],
                           inc="pe")
                        op("scalar", lambda e, lh=lh, bS=bS, sl=sl: e.activation(
                            out=eL_T[:, sl:sl + 128],
                            in_=bS[0:64, 0:128], func=AF.Exp, scale=0.125),
                           waits=[("pe", at("pe"))], inc="s")
                        lexp_at[lh] = at("s")
                        op("vector", lambda e, lh=lh, sl=sl, tg=tg: e.tensor_mul(
                            eL_T[:, sl:sl + 128],
                            eL_T[:, sl:sl + 128],
                            lmask[:, tg:tg + 128]),
                           waits=[("s", at("s"))], inc="v")
                        lmask_at[lh] = at("v")

                    def t2_tail(lh):
                        par = (lh % 2) * 64
                        lc = lh // 2
                        bR = PS[0 + lh % 2]
                        sl = (lh % 2) * 128
                        op("tensor", lambda e, lh=lh, sl=sl: e.matmul(
                            PS[4][:, 16 + lh:17 + lh],
                            eL_T[:, sl:sl + 128], onesb[0:64, :],
                            start=True, stop=True),
                           waits=[("v", lmask_at[lh])], inc=None)
                        op("tensor", lambda e, lh=lh, par=par, sl=sl, bR=bR:
                           e.matmul(
                               bR[par:par + 64, 128:256],
                               GV_sb[:, lh * 64:(lh + 1) * 64],
                               eL_T[:, sl:sl + 128],
                               start=True, stop=True), inc="pe")
                        op("vector", lambda e, lh=lh, par=par, lc=lc, bR=bR:
                           e.tensor_copy(
                               remlat_sb[par:par + 64, lc * 128:(lc + 1) * 128],
                               bR[par:par + 64, 128:256]),
                           waits=[("pe", at("pe"))], inc="v")
                        lat_ev_at[lh] = at("v")

                    if 't2' in phases:
                        for lh in range(8):
                            t2_front(lh)
                            if lh >= 1:
                                t2_tail(lh - 1)
                        t2_tail(7)
                    if 't3' not in phases:
                        continue
                    op("vector", lambda e: e.tensor_copy(fct[:], PS[4][:, 0:56]),
                       waits=[("pe", at("pe")), ("v", at("v")),
                              ("s", at("s"))], inc="v")
                    op("vector", lambda e: e.reciprocal(
                        f_scr[:, 0:24], fct[:, 0:24]), inc="v")
                    rd_done = at("v")
                    op("scalar", lambda e: e.activation(
                        out=tg_sb[:], in_=fct[:, 24:56], func=AF.Tanh,
                        scale=0.5), waits=[("v", rd_done)], inc="s")
                    op("vector", lambda e: e.scalar_tensor_tensor(
                        f_scr[:, 24:40], tg_sb[:, 0:32:2], 1.0, f_scr[:, 0:16],
                        ALU.add, ALU.mult), waits=[("s", at("s"))], inc="v")
                    op("vector", lambda e: e.tensor_scalar(
                        f_scr[:, 40:56], tg_sb[:, 1:32:2], 1.0, None, ALU.add),
                       inc="v")
                    f_done = at("v")
                    op("tensor", lambda e: e.transpose(
                        PS[5][0:16, 0:128], f_scr[:, 24:40], idf_sb[:]),
                       waits=[("v", f_done)] + (
                           [("s", fT_ev_prev)] if fT_ev_prev else []), inc=None)
                    op("tensor", lambda e: e.transpose(
                        PS[5][0:16, 128:256], f_scr[:, 40:56], idf_sb[:]),
                       inc=None)
                    op("tensor", lambda e: e.transpose(
                        PS[5][0:8, 256:384], f_scr[:, 16:24], idf_sb[:]),
                       inc="pe")
                    op("scalar", lambda e: e.activation(
                        out=fT[:], in_=PS[5][0:16, 0:256], func=AF.Identity),
                       waits=[("pe", at("pe"))], inc="s")
                    op("scalar", lambda e: e.activation(
                        out=rlT[:], in_=PS[5][0:8, 256:384], func=AF.Identity),
                       inc="s")
                    fT_ev_prev = at("s")
                    if 't4' not in phases:
                        continue
                    for c4 in range(4):
                        op("tensor", lambda e, c4=c4: e.matmul(
                            PS[2][:, 0:128], lexp[:, c4 * 128:(c4 + 1) * 128],
                            rlT[:], start=True, stop=True),
                           waits=[("s", fT_ev_prev)] if c4 == 0 else
                                 [("v", at("v"))], inc="pe")
                        op("vector", lambda e, c4=c4: e.tensor_mul(
                            remn_sb[:, c4 * 128:(c4 + 1) * 128],
                            remlat_sb[:, c4 * 128:(c4 + 1) * 128],
                            PS[2][:, 0:128]),
                           waits=[("pe", at("pe"))], inc="v")
                    remn_done = at("v")
                    for m in range(8):
                        bR = PS[m % 2]
                        for k in range(4):
                            op("tensor", lambda e, m=m, k=k, bR=bR: e.matmul(
                                bR[:, 0:128],
                                rout[:, k * 1024 + m * 128:k * 1024 + (m + 1) * 128],
                                remn_sb[:, k * 128:(k + 1) * 128],
                                start=(k == 0), stop=(k == 3)),
                               waits=([("v", remn_done if m < 1 else at("v"))]
                                      if k == 0 else ()),
                               inc=("pe" if k == 3 else None))
                        op("tensor", lambda e, m=m: e.matmul(
                            PS[3][:, 0:128], hexp[:, m * 128:(m + 1) * 128],
                            fT[:, 0:128], start=True, stop=True), inc=None)
                        op("tensor", lambda e, m=m: e.matmul(
                            PS[3][:, 128:256], hexp[:, m * 128:(m + 1) * 128],
                            fT[:, 128:256], start=True, stop=True), inc="pe")
                        pe_m = at("pe")
                        op("scalar", lambda e, m=m, bR=bR: e.activation(
                            out=rem_sb[:, (m % 2) * 128:(m % 2 + 1) * 128],
                            in_=bR[:, 0:128], func=AF.Identity),
                           waits=[("pe", pe_m)], inc="s")
                        op("vector", lambda e, m=m: e.tensor_mul(
                            c1_sb[:], local_sb[:, m * 128:(m + 1) * 128],
                            PS[3][:, 0:128]),
                           waits=[("pe", pe_m), ("s", at("s"))], inc="v")
                        op("vector", lambda e, m=m: e.tensor_mul(
                            rem2_sb[:], rem_sb[:, (m % 2) * 128:(m % 2 + 1) * 128],
                            PS[3][:, 128:256]), inc="v")
                        op("vector", lambda e, m=m: e.tensor_add(
                            comb_bf[:, m * 128:(m + 1) * 128], c1_sb[:],
                            rem2_sb[:]), inc="v")
                    comb_done = at("v")
                    if dbg and tkey < 2:
                        for t_, d_ in ((fct, dbg_fct), (f_scr, dbg_fs),
                                       (local_sb, dbg_loc), (remlat_sb, dbg_rl),
                                       (comb_bf, dbg_comb), (P_T, dbg_pt),
                                       (K_bf, dbg_k), (v_tm, dbg_v)):
                            op("gpsimd", lambda e, t_=t_, d_=d_, tk=tkey:
                               e.dma_start(out=d_[tk], in_=t_[:]),
                               waits=[("v", at("v")), ("s", at("s"))],
                               inc="dg", inc_n=16)
                    if 't5' not in phases:
                        continue
                    for m in range(8):
                        bW = PS[4 + m % 2]
                        for k in range(8):
                            op("tensor", lambda e, m=m, k=k, bW=bW: e.matmul(
                                bW[:, 0:128],
                                wout[:, k * 1024 + m * 128:k * 1024 + (m + 1) * 128],
                                comb_bf[:, k * 128:(k + 1) * 128],
                                start=(k == 0), stop=(k == 7)),
                               waits=([("v", comb_done if m < 2 else at("v"))]
                                      if k == 0 else ()),
                               inc=("pe" if k == 7 else None))
                        op("vector", lambda e, m=m, bW=bW, dbf=dbf, tb=tb:
                           e.scalar_tensor_tensor(
                            a1_scr[:, dbf * 1024 + m * 128:dbf * 1024 + (m + 1) * 128],
                            bW[:, 0:128], 0.5,
                            x1_st[:, m * 512 + tb * 128:m * 512 + (tb + 1) * 128],
                            ALU.mult, ALU.add),
                           waits=[("pe", at("pe"))] + (
                               [("dq", a1dma_at.get(tkey - 2, 0))] if m == 0
                               else []),
                           inc="v")
                    a1_done = at("v")
                    for m in range(8):
                        a1dma_at[tkey] = op("sync", lambda e, b=b, m=m, tg=tg,
                                           dbf=dbf: e.dma_start(
                            out=a1_d[b, m * 128:(m + 1) * 128, tg:tg + 128],
                            in_=a1_scr[:, dbf * 1024 + m * 128:dbf * 1024 + (m + 1) * 128]),
                           waits=([("v", a1_done)] if m == 0 else ()),
                           inc="dq", inc_n=16)
        sd.wait("gpsimd", [("dq", at("dq"))])

    if ctx is not None:
        return body
    run_block(nc, sched, body)
    sched.free()
    return nc


def build_ffn1(ctx=None):
    nc = ctx["nc"] if ctx else bass.Bass("TRN2", target_bir_lowering=False)
    dram = _mk_dram(nc, ctx)
    xa = dram("x1", [B, D, S], F32, "ExternalInput")
    xn1 = dram("xn1", [B, D, S], BF16, "ExternalInput")
    w1_d = dram("w1c", [D, 4096], F32, "ExternalInput")
    w2_d = dram("w2c", [4096, D], F32, "ExternalInput")
    b1_d = dram("b1_pc", [128, 32], F32, "ExternalInput")
    b2_d = dram("b2_pc", [128, 8], F32, "ExternalInput")
    out_d = dram("out", [B, D, S], F32, "ExternalOutput")

    sb = ctx["arena"].t if ctx else nc.alloc_sbuf_tensor
    w1 = sb("w1_s", [128, 32768], BF16)
    w2 = sb("w2_s", [128, 32768], BF16)
    b1 = sb("b1_s", [128, 32], F32)
    b2 = sb("b2_s", [128, 8], F32)
    xn_sb = sb("xn_sb", [128, 4096], BF16)
    xa_sb = sb("xa_sb", [128, 4096], F32)
    h_bf = sb("h_bf", [128, 16384], BF16)
    o_scr = sb("o_scr", [128, 1024], F32)
    PS, psT, psTf = _mk_psum(nc, ctx)

    sched = None if ctx else Sched(nc, ["dq", "dg", "pe", "v", "s"])
    TT = 512

    def body(sd):
        op, at = sd.op, sd.at
        for k in range(8):
            op("gpsimd", lambda e, k=k: e.dma_start(
                out=w1[:, k * 4096:(k + 1) * 4096],
                in_=w1_d[k * 128:(k + 1) * 128, :]), inc="dg", inc_n=16)
        for k in range(32):
            op("gpsimd", lambda e, k=k: e.dma_start(
                out=w2[:, k * 1024:(k + 1) * 1024],
                in_=w2_d[k * 128:(k + 1) * 128, :]), inc="dg", inc_n=16)
        op("sync", lambda e: e.dma_start(out=b1[:], in_=b1_d), inc="dq", inc_n=16)
        op("sync", lambda e: e.dma_start(out=b2[:], in_=b2_d), inc="dq", inc_n=16)
        w_g, w_q = at("dg"), at("dq")
        out_dma_at = 0
        for b in range(B):
            for t0 in range(0, S, TT):
                lw = [("v", at("v")), ("s", at("s"))]
                for src_, dst in ((xn1, xn_sb), (xa, xa_sb)):
                    for c in range(8):
                        op("sync", lambda e, s=src_, d=dst, b=b, t0=t0, c=c:
                           e.dma_start(out=d[:, c * TT:(c + 1) * TT],
                                       in_=s[b, c * 128:(c + 1) * 128, t0:t0 + TT]),
                           waits=lw, inc="dq", inc_n=16)
                        lw = ()
                ld = at("dq")
                for fc in range(32):
                    bank = PS[fc % 2]
                    for k in range(8):
                        op("tensor", lambda e, fc=fc, k=k, bank=bank: e.matmul(
                            bank[:],
                            w1[:, k * 4096 + fc * 128:k * 4096 + (fc + 1) * 128],
                            xn_sb[:, k * TT:(k + 1) * TT],
                            start=(k == 0), stop=(k == 7)),
                           waits=([("dq", ld), ("dg", w_g), ("s", at("s"))]
                                  if k == 0 else ()),
                           inc=("pe" if k == 7 else None))
                    op("scalar", lambda e, fc=fc, bank=bank: e.activation(
                        out=h_bf[:, fc * TT:(fc + 1) * TT], in_=bank[:],
                        func=AF.Gelu_apprx_tanh, bias=b1[:, fc:fc + 1]),
                       waits=[("pe", at("pe")), ("dq", w_q)], inc="s")
                h_done = at("s")
                for m in range(8):
                    bank = PS[2 + m % 2]
                    for k in range(32):
                        op("tensor", lambda e, m=m, k=k, bank=bank: e.matmul(
                            bank[:],
                            w2[:, k * 1024 + m * 128:k * 1024 + (m + 1) * 128],
                            h_bf[:, k * TT:(k + 1) * TT],
                            start=(k == 0), stop=(k == 31)),
                           waits=([("s", h_done), ("v", at("v"))] if k == 0 else ()),
                           inc=("pe" if k == 31 else None))
                    op("vector", lambda e, m=m, bank=bank: e.scalar_tensor_tensor(
                        o_scr[:, (m % 2) * TT:(m % 2 + 1) * TT], bank[:],
                        b2[:, m:m + 1], xa_sb[:, m * TT:(m + 1) * TT],
                        ALU.add, ALU.add),
                       waits=[("pe", at("pe"))] + (
                           [("dq", out_dma_at)] if m < 2 else []), inc="v")
                    out_dma_at = op("sync", lambda e, b=b, t0=t0, m=m: e.dma_start(
                        out=out_d[b, m * 128:(m + 1) * 128, t0:t0 + TT],
                        in_=o_scr[:, (m % 2) * TT:(m % 2 + 1) * TT]),
                       waits=[("v", at("v"))], inc="dq", inc_n=16)
        sd.wait("gpsimd", [("dq", at("dq"))])

    if ctx is not None:
        return body
    run_block(nc, sched, body)
    sched.free()
    return nc



_NC_CACHE = {}


def _get_nc(name, builder):
    if name not in _NC_CACHE:
        _NC_CACHE[name] = builder()
    return _NC_CACHE[name]


def _fm(a):
    return np.ascontiguousarray(np.asarray(a).transpose(0, 2, 1))


def _pc(v, cols):
    return np.ascontiguousarray(np.asarray(v, np.float32).reshape(cols, 128).T)


def kernel_unfused(**inputs):
    hc = host_consts()
    inp = {k: np.asarray(v) for k, v in inputs.items()}
    x = inp["L0_attn_x"]

    glw = inp["L1_attn_gate_local_w"]
    grw = inp["L1_attn_gate_remote_w"]
    glgr = np.empty((64, 32), np.float32)
    glgr[:, 0::2] = glw.T
    glgr[:, 1::2] = grw.T
    glgr_dup = np.ascontiguousarray(np.concatenate([glgr, glgr], axis=0))

    cores = list(range(NCORES))
    x_fm = [_fm(x[2 * c:2 * c + 2]) for c in cores]

    nc = _get_nc("prep", build_prep)
    maps = [{
        "x": x_fm[c],
        "wfr": inp["L0_ffn_w_freq_real"], "wfi": inp["L0_ffn_w_freq_imag"],
        "Fr": hc["Fr"], "Fi": hc["Fi"], "FiN": hc["FiN"],
        "Gr": hc["Gr"], "G2": hc["G2"],
        "ident_bf": hc["ident_bf"], "ident_f32": hc["ident_f32"],
        "sel8": hc["sel8"], "inv1024": hc["inv1024"],
    } for c in cores]
    r1 = run_bass_kernel_spmd(nc, maps, core_ids=cores).results

    nc = _get_nc("attn0", build_attn0)
    nw_pc = _pc(inp["L0_attn_norm_w"], 8)
    maps = [{
        "xn": r1[c]["xn"],
        "wq": inp["L0_attn_wq"], "wk": inp["L0_attn_wk"],
        "wv": inp["L0_attn_wv"], "wo": inp["L0_attn_wo"],
        "wg1": inp["L0_attn_wg1"], "wg2": inp["L0_attn_wg2"],
        "norm_w_pc": nw_pc,
        "cfm": hc["cfm"], "sfm": hc["sfm"], "ctm": hc["ctm"], "stm": hc["stm"],
        "maskU": hc["maskU"], "blk16": hc["blk16"], "hexp": hc["hexp"],
    } for c in cores]
    r2 = run_bass_kernel_spmd(nc, maps, core_ids=cores).results

    nc = _get_nc("mid", build_mid)
    bias_pc = _pc(inp["L0_ffn_bias"], 8)
    maps = [{
        "x": x_fm[c], "a0": r2[c]["a0"], "xn0": r1[c]["xn"], "M": r1[c]["M"],
        "bias_pc": bias_pc,
        "ident_f32": hc["ident_f32"], "sel8": hc["sel8"],
        "inv1024": hc["inv1024"],
    } for c in cores]
    r3 = run_bass_kernel_spmd(nc, maps, core_ids=cores).results

    nc = _get_nc("attn1", build_attn1)
    maps = [{
        "xn1": r3[c]["xn1"], "x1": r3[c]["x1"],
        "wq": inp["L1_attn_wq"], "wk": inp["L1_attn_wk"],
        "wv": inp["L1_attn_wv"], "wgq": inp["L1_attn_wgq"],
        "wgk": inp["L1_attn_wgk"], "wgv": inp["L1_attn_wgv"],
        "r_out_w": inp["L1_attn_r_out_w"], "w_out": inp["L1_attn_w_out"],
        "glgr": glgr_dup,
        "maskU": hc["maskU"], "maskLs": hc["maskLs"], "lmask": hc["lmask"],
        "hexp": hc["hexp"], "lexp": hc["lexp"], "ident_f32": hc["ident_f32"],
        "ones_bf": hc["ones_bf"],
    } for c in cores]
    r4 = run_bass_kernel_spmd(nc, maps, core_ids=cores).results

    nc = _get_nc("ffn1", build_ffn1)
    maps = [{
        "x1": r4[c]["a1"], "xn1": r3[c]["xn1"],
        "w1c": inp["L1_ffn_w1"][c], "w2c": inp["L1_ffn_w2"][c],
        "b1_pc": _pc(inp["L1_ffn_b1"][c], 32),
        "b2_pc": _pc(inp["L1_ffn_b2"][c], 8),
    } for c in cores]
    r5 = run_bass_kernel_spmd(nc, maps, core_ids=cores).results

    out = np.empty((16, 2048, 1024), np.float32)
    for c in cores:
        out[2 * c:2 * c + 2] = r5[c]["out"].transpose(0, 2, 1)
    return out


def build_fused(nphases=5):
    nc = bass.Bass("TRN2", target_bir_lowering=False)

    def ein(name, shape, dt):
        return nc.dram_tensor(name, shape, dt, kind="ExternalInput").ap()

    def internal(name, shape, dt):
        return nc.dram_tensor(name, shape, dt, kind="Internal").ap()

    E = {}
    E["x"] = ein("x", [B, D, S], F32)
    for n in ("wfr", "wfi"):
        E[n] = ein(n, [D, D], F32)
    for n in ("Fr", "Fi", "FiN", "Gr", "G2"):
        E[n] = ein(n, [D, D], BF16)
    E["ident_bf"] = ein("ident_bf", [128, 128], BF16)
    E["ident_f32"] = ein("ident_f32", [128, 128], F32)
    E["sel8"] = ein("sel8", [8, 1024], F32)
    E["inv1024"] = ein("inv1024", [128, 1], F32)
    for n in ("l0_wq", "l0_wk", "l0_wv", "l0_wo"):
        E[n] = ein(n, [D, D], F32)
    E["wg1"] = ein("wg1", [D, 64], F32)
    E["wg2"] = ein("wg2", [64, D], F32)
    E["norm_w_pc"] = ein("norm_w_pc", [128, 8], F32)
    E["cfm"] = ein("cfm", [D, S], BF16)
    E["sfm"] = ein("sfm", [D, S], BF16)
    E["ctm"] = ein("ctm", [S, D], BF16)
    E["stm"] = ein("stm", [S, D], BF16)
    E["maskU"] = ein("maskU", [128, 128], BF16)
    E["maskLs"] = ein("maskLs", [128, 128], BF16)
    E["blk16"] = ein("blk16", [8 * 128, 16], F32)
    E["hexp"] = ein("hexp", [16, 8 * 128], BF16)
    E["lexp"] = ein("lexp", [8, 4 * 128], BF16)
    E["bias_pc"] = ein("bias_pc", [128, 8], F32)
    for n in ("l1_wq", "l1_wk", "l1_wv", "w_out"):
        E[n] = ein(n, [D, D], F32)
    for n in ("wgq", "wgk", "wgv"):
        E[n] = ein(n, [D, 512], F32)
    E["r_out_w"] = ein("r_out_w", [512, D], F32)
    E["glgr"] = ein("glgr", [128, 32], F32)
    E["lmask"] = ein("lmask", [64, S], BF16)
    E["ones_bf"] = ein("ones_bf", [128, 1], BF16)
    E["w1c"] = ein("w1c", [D, 4096], F32)
    E["w2c"] = ein("w2c", [4096, D], F32)
    E["b1_pc"] = ein("b1_pc", [128, 32], F32)
    E["b2_pc"] = ein("b2_pc", [128, 8], F32)

    xn0_i = internal("i_xn0", [B, D, S], BF16)
    M_i = internal("i_M", [D, D], BF16)
    a0_i = internal("i_a0", [B, D, S], F32)
    x1_i = internal("i_x1", [B, D, S], F32)
    xn1_i = internal("i_xn1", [B, D, S], BF16)
    a1_i = internal("i_a1", [B, D, S], F32)
    out_e = nc.dram_tensor("out", [B, D, S], F32, kind="ExternalOutput").ap()

    PS, psT, psTf = _mk_psum(nc, None)
    cpool = setup_const_pool(nc, [1e-5, -0.6931472, 1e-6])
    ar = Arena(nc, nc.sbuf_base, nc.sbuf_top)
    ctx = {"nc": nc, "PS": PS, "psT": psT, "psTf": psTf, "cpool": cpool,
           "arena": ar}

    emits = []
    ar.reset("p1_")
    ctx["D"] = {"x": E["x"], "wfr": E["wfr"], "wfi": E["wfi"],
                "Fr": E["Fr"], "Fi": E["Fi"], "FiN": E["FiN"],
                "Gr": E["Gr"], "G2": E["G2"], "ident_bf": E["ident_bf"],
                "ident_f32": E["ident_f32"], "sel8": E["sel8"],
                "inv1024": E["inv1024"], "M": M_i, "xn": xn0_i}
    emits.append(build_prep(ctx=ctx))
    ar.reset("p2_")
    ctx["D"] = {"xn": xn0_i, "wq": E["l0_wq"], "wk": E["l0_wk"],
                "wv": E["l0_wv"], "wo": E["l0_wo"], "wg1": E["wg1"],
                "wg2": E["wg2"], "norm_w_pc": E["norm_w_pc"],
                "cfm": E["cfm"], "sfm": E["sfm"], "ctm": E["ctm"],
                "stm": E["stm"], "maskU": E["maskU"], "blk16": E["blk16"],
                "hexp": E["hexp"], "a0": a0_i}
    emits.append(build_attn0(ctx=ctx))
    ar.reset("p3_")
    ctx["D"] = {"x": E["x"], "a0": a0_i, "xn0": xn0_i, "M": M_i,
                "bias_pc": E["bias_pc"], "ident_f32": E["ident_f32"],
                "sel8": E["sel8"], "inv1024": E["inv1024"],
                "x1": x1_i, "xn1": xn1_i}
    emits.append(build_mid(ctx=ctx))
    ar.reset("p4_")
    ctx["D"] = {"xn1": xn1_i, "x1": x1_i, "wq": E["l1_wq"], "wk": E["l1_wk"],
                "wv": E["l1_wv"], "wgq": E["wgq"], "wgk": E["wgk"],
                "wgv": E["wgv"], "r_out_w": E["r_out_w"],
                "w_out": E["w_out"], "glgr": E["glgr"], "maskU": E["maskU"],
                "maskLs": E["maskLs"], "lmask": E["lmask"], "hexp": E["hexp"],
                "lexp": E["lexp"], "ident_f32": E["ident_f32"],
                "ones_bf": E["ones_bf"], "a1": a1_i}
    emits.append(build_attn1(ctx=ctx))
    ar.reset("p5_")
    ctx["D"] = {"x1": a1_i, "xn1": xn1_i, "w1c": E["w1c"],
                "w2c": E["w2c"], "b1_pc": E["b1_pc"], "b2_pc": E["b2_pc"],
                "out": out_e}
    emits.append(build_ffn1(ctx=ctx))

    emits = emits[:nphases]

    def mega(sd):
        for em in emits:
            em(sd)
            barrier(sd)

    sched = Sched(nc, ["dq", "dg", "pe", "v", "s"])
    run_block(nc, sched, mega)
    sched.free()
    return nc


def kernel(**inputs):
    hc = host_consts()
    inp = {k: np.asarray(v) for k, v in inputs.items()}
    x = inp["L0_attn_x"]

    glw = inp["L1_attn_gate_local_w"]
    grw = inp["L1_attn_gate_remote_w"]
    glgr = np.empty((64, 32), np.float32)
    glgr[:, 0::2] = glw.T
    glgr[:, 1::2] = grw.T
    glgr_dup = np.ascontiguousarray(np.concatenate([glgr, glgr], axis=0))

    cores = list(range(NCORES))
    nc = _get_nc("fused", build_fused)
    shared = {
        "wfr": inp["L0_ffn_w_freq_real"], "wfi": inp["L0_ffn_w_freq_imag"],
        "Fr": hc["Fr"], "Fi": hc["Fi"], "FiN": hc["FiN"],
        "Gr": hc["Gr"], "G2": hc["G2"],
        "ident_bf": hc["ident_bf"], "ident_f32": hc["ident_f32"],
        "sel8": hc["sel8"], "inv1024": hc["inv1024"],
        "l0_wq": inp["L0_attn_wq"], "l0_wk": inp["L0_attn_wk"],
        "l0_wv": inp["L0_attn_wv"], "l0_wo": inp["L0_attn_wo"],
        "wg1": inp["L0_attn_wg1"], "wg2": inp["L0_attn_wg2"],
        "norm_w_pc": _pc(inp["L0_attn_norm_w"], 8),
        "cfm": hc["cfm"], "sfm": hc["sfm"], "ctm": hc["ctm"], "stm": hc["stm"],
        "maskU": hc["maskU"], "maskLs": hc["maskLs"], "blk16": hc["blk16"],
        "hexp": hc["hexp"], "lexp": hc["lexp"],
        "bias_pc": _pc(inp["L0_ffn_bias"], 8),
        "l1_wq": inp["L1_attn_wq"], "l1_wk": inp["L1_attn_wk"],
        "l1_wv": inp["L1_attn_wv"], "w_out": inp["L1_attn_w_out"],
        "wgq": inp["L1_attn_wgq"], "wgk": inp["L1_attn_wgk"],
        "wgv": inp["L1_attn_wgv"], "r_out_w": inp["L1_attn_r_out_w"],
        "glgr": glgr_dup, "lmask": hc["lmask"], "ones_bf": hc["ones_bf"],
    }
    maps = []
    for c in cores:
        m = dict(shared)
        m["x"] = _fm(x[2 * c:2 * c + 2])
        m["w1c"] = inp["L1_ffn_w1"][c]
        m["w2c"] = inp["L1_ffn_w2"][c]
        m["b1_pc"] = _pc(inp["L1_ffn_b1"][c], 32)
        m["b2_pc"] = _pc(inp["L1_ffn_b2"][c], 8)
        maps.append(m)
    res = run_bass_kernel_spmd(nc, maps, core_ids=cores).results
    out = np.empty((16, 2048, 1024), np.float32)
    for c in cores:
        out[2 * c:2 * c + 2] = res[c]["out"].transpose(0, 2, 1)
    return out


# version 09
# speedup vs baseline: 1.3720x; 1.0334x over previous
import numpy as np
import ml_dtypes
import concourse.bass as bass
import concourse.mybir as mybir
from concourse.bass_utils import run_bass_kernel_spmd

F32 = mybir.dt.float32
BF16 = mybir.dt.bfloat16
AF = mybir.ActivationFunctionType
ALU = mybir.AluOpType
AX = mybir.AxisListType.X
BFNP = ml_dtypes.bfloat16

NCORES = 8
B = 2
S = 2048
D = 1024
H, HD = 16, 64
NT = S // 128


_CONST_CACHE = {}


def host_consts():
    if _CONST_CACHE:
        return _CONST_CACHE
    c = {}
    j = np.arange(D)
    Fc = np.exp(-2j * np.pi * np.outer(j, j) / D)
    c["Fr"] = Fc.real.astype(np.float32).astype(BFNP)
    c["Fi"] = Fc.imag.astype(np.float32).astype(BFNP)
    c["FiN"] = (-Fc.imag).astype(np.float32).astype(BFNP)
    c["Gr"] = (Fc.real / D).astype(np.float32).astype(BFNP)
    c["G2"] = (Fc.imag / D).astype(np.float32).astype(BFNP)
    c["ident_bf"] = np.eye(128, dtype=np.float32).astype(BFNP)
    c["ident_f32"] = np.eye(128, dtype=np.float32)
    theta = 10000.0 ** (-2.0 / D * np.arange(D, dtype=np.float64))
    ang = np.arange(S, dtype=np.float64)[:, None] * theta[None]
    c["cfm"] = np.ascontiguousarray(np.cos(ang).T).astype(np.float32).astype(BFNP)
    c["sfm"] = np.ascontiguousarray(np.sin(ang).T).astype(np.float32).astype(BFNP)
    c["ctm"] = np.cos(ang).astype(np.float32).astype(BFNP)
    c["stm"] = np.sin(ang).astype(np.float32).astype(BFNP)
    i_ = np.arange(128)
    c["maskU"] = (i_[:, None] <= i_[None, :]).astype(np.float32).astype(BFNP)
    c["maskLs"] = (i_[:, None] > i_[None, :]).astype(np.float32).astype(BFNP)
    c["lmask"] = (np.arange(64)[:, None] <= (np.arange(S) // 32)[None, :]
                  ).astype(np.float32).astype(BFNP)
    sel8 = np.zeros((8, 1024), np.float32)
    for jj in range(8):
        sel8[jj, jj * 128:(jj + 1) * 128] = 1.0
    c["sel8"] = sel8
    blk = np.zeros((8, 128, 16), np.float32)
    for cc in range(8):
        blk[cc, 0:64, 2 * cc] = 1.0
        blk[cc, 64:128, 2 * cc + 1] = 1.0
    c["blk16"] = blk.reshape(8 * 128, 16)
    hx = np.zeros((8, 16, 128), np.float32)
    for cc in range(8):
        hx[cc, 2 * cc, 0:64] = 1.0
        hx[cc, 2 * cc + 1, 64:128] = 1.0
    c["hexp"] = np.ascontiguousarray(hx.transpose(1, 0, 2)).reshape(16, 8 * 128
                                                                    ).astype(BFNP)
    lx = np.zeros((4, 8, 128), np.float32)
    for cc in range(4):
        lx[cc, 2 * cc, 0:64] = 1.0
        lx[cc, 2 * cc + 1, 64:128] = 1.0
    c["lexp"] = np.ascontiguousarray(lx.transpose(1, 0, 2)).reshape(8, 4 * 128
                                                                    ).astype(BFNP)
    c["inv1024"] = np.full((128, 1), 1.0 / 1024.0, np.float32)
    c["ones_bf"] = np.ones((128, 1), np.float32).astype(BFNP)
    _CONST_CACHE.update(c)
    return c


class Sched:
    def __init__(self, nc, sem_names):
        self.nc = nc
        self.sems = {n: nc.alloc_semaphore(n) for n in sem_names}
        self.cnt = None
        self.E = None
        self.eng = None

    def free(self):
        for h in self.sems.values():
            self.nc.release_semaphore(h)

    def start(self, engine_name, eng):
        self.E = engine_name
        self.eng = eng
        self.cnt = {n: 0 for n in self.sems}

    def op(self, e, fn, waits=(), inc=None, inc_n=1):
        if e == self.E:
            for s, v in waits:
                if v > 0:
                    self.eng.wait_ge(self.sems[s], v)
            instr = fn(self.eng)
            if inc is not None and instr is not None:
                instr.then_inc(self.sems[inc], inc_n)
        if inc is not None:
            self.cnt[inc] += inc_n
        return self.cnt[inc] if inc is not None else None

    def wait(self, e, waits):
        if e == self.E:
            for s, v in waits:
                if v > 0:
                    self.eng.wait_ge(self.sems[s], v)

    def at(self, sem):
        return self.cnt[sem]


def setup_const_pool(nc, values):
    vals = sorted(set(float(v) for v in values))
    pool = nc.alloc_sbuf_tensor("constpool", [128, max(1, len(vals))], F32)
    for i, v in enumerate(vals):
        nc.const_aps.register(F32, v, pool[:, i:i + 1])
    return pool, vals


def emit_const_pool(sd, pool, vals, inc="dg"):
    for i, v in enumerate(vals):
        sd.op("gpsimd", lambda e, i=i, v=v: e.memset(pool[:, i:i + 1], v),
              inc=inc)
    return sd.at(inc)


def run_block(nc, sched, body):
    with nc.Block() as block:
        @block.sync
        def _(eng):
            sched.start("sync", eng)
            body(sched)

        @block.tensor
        def _(eng):
            sched.start("tensor", eng)
            body(sched)

        @block.vector
        def _(eng):
            sched.start("vector", eng)
            body(sched)

        @block.scalar
        def _(eng):
            sched.start("scalar", eng)
            body(sched)

        @block.gpsimd
        def _(eng):
            sched.start("gpsimd", eng)
            body(sched)



class Arena:

    def __init__(self, nc, base, limit):
        base = (base + 31) // 32 * 32
        self.nc, self.base, self.limit = nc, base, limit
        self.off = base
        self.prefix = ""
        self.peak = base

    def reset(self, prefix):
        self.off = self.base
        self.prefix = prefix

    def t(self, name, shape, dtype):
        esz = {F32: 4, BF16: 2}[dtype]
        fbytes = int(np.prod(shape[1:])) * esz
        fbytes = (fbytes + 31) // 32 * 32
        t = self.nc.alloc_sbuf_tensor_at(self.prefix + name, list(shape), dtype,
                                         offset=self.off)
        self.off += fbytes
        self.peak = max(self.peak, self.off)
        assert self.off <= self.limit, (self.prefix, name, self.off, self.limit)
        return t


def _mk_dram(nc, ctx):
    def dram(name, shape, dtype, kind):
        if ctx is not None:
            return ctx["D"][name]
        return nc.dram_tensor(name, shape, dtype, kind=kind).ap()
    return dram


def _mk_psum(nc, ctx):
    if ctx is not None:
        return ctx["PS"], ctx["psT"], ctx["psTf"]
    PS = [nc.alloc_psum_tensor(f"bank{i}", [128, 512], F32) for i in range(6)]
    psT = nc.alloc_psum_tensor("psT", [128, 128], BF16)
    psTf = nc.alloc_psum_tensor("psTf", [128, 128], F32)
    return PS, psT, psTf


def barrier(sd):
    vals = [(s, sd.at(s)) for s in sd.sems]
    for e in ("sync", "tensor", "vector", "scalar", "gpsimd"):
        sd.wait(e, vals)


def build_prep(ctx=None):
    nc = ctx["nc"] if ctx else bass.Bass("TRN2", target_bir_lowering=False)
    dram = _mk_dram(nc, ctx)
    x = dram("x", [B, D, S], F32, "ExternalInput")
    wfr = dram("wfr", [D, D], F32, "ExternalInput")
    wfi = dram("wfi", [D, D], F32, "ExternalInput")
    cF = {}
    for n in ("Fr", "Fi", "FiN", "Gr", "G2"):
        cF[n] = dram(n, [D, D], BF16, "ExternalInput")
    idb = dram("ident_bf", [128, 128], BF16, "ExternalInput")
    idf = dram("ident_f32", [128, 128], F32, "ExternalInput")
    sel8 = dram("sel8", [8, 1024], F32, "ExternalInput")
    inv1024 = dram("inv1024", [128, 1], F32, "ExternalInput")
    M_out = dram("M", [D, D], BF16, "ExternalOutput")
    xn_out = dram("xn", [B, D, S], BF16, "ExternalOutput")

    sb = ctx["arena"].t if ctx else nc.alloc_sbuf_tensor
    cFr = sb("cFr", [128, 8192], BF16)
    cFi = sb("cFi", [128, 8192], BF16)
    cFiN = sb("cFiN", [128, 8192], BF16)
    cGr = sb("cGr", [128, 8192], BF16)
    cG2 = sb("cG2", [128, 8192], BF16)
    Wr = sb("Wr", [128, 8192], BF16)
    Wi = sb("Wi", [128, 8192], BF16)
    Qr = sb("Qr", [128, 8192], BF16)
    Qi = sb("Qi", [128, 8192], BF16)
    idb_sb = sb("idb_sb", [128, 128], BF16)
    idf_sb = sb("idf_sb", [128, 128], F32)
    sel8_sb = sb("sel8_sb", [8, 1024], F32)
    i1024_sb = sb("i1024_sb", [128, 1], F32)
    x_sb = sb("x_sb", [128, 4096], F32)
    sq_sb = sb("sq_sb", [128, 4096], F32)
    xn_sb = sb("xn_sb", [128, 4096], BF16)
    stats_sb = sb("stats_sb", [128, 8], F32)
    var_sb = sb("var_sb", [128, 4], F32)
    msq2_sb = sb("msq2_sb", [128, 4], F32)
    ln_sb = sb("ln_sb", [128, 4], F32)
    fct_sb = sb("fct_sb", [128, 8], F32)
    rowT_sb = sb("rowT_sb", [8, 128], F32)
    PS, psT, psTf = _mk_psum(nc, ctx)
    psA, psB = PS[0], PS[1]
    psStat = PS[2]
    psBC0, psBC1 = PS[3], PS[4]
    cpool, cvals = ctx["cpool"] if ctx else setup_const_pool(nc, [1e-5])
    if True:
        sched = None if ctx else Sched(nc, ["dq", "dg", "pe", "v", "s"])

        def body(sd):
            op, at = sd.op, sd.at
            cpool_done = emit_const_pool(sd, cpool, cvals)
            for name, t in (("Fr", cFr), ("Fi", cFi), ("FiN", cFiN),
                            ("Gr", cGr), ("G2", cG2)):
                for k in range(8):
                    op("sync", lambda e, n=name, t=t, k=k: e.dma_start(
                        out=t[:, k * 1024:(k + 1) * 1024],
                        in_=cF[n][k * 128:(k + 1) * 128, :]), inc="dq", inc_n=16)
            op("sync", lambda e: e.dma_start(out=idb_sb[:], in_=idb), inc="dq", inc_n=16)
            op("sync", lambda e: e.dma_start(out=idf_sb[:], in_=idf), inc="dq", inc_n=16)
            op("sync", lambda e: e.dma_start(out=sel8_sb[:], in_=sel8), inc="dq", inc_n=16)
            op("sync", lambda e: e.dma_start(out=i1024_sb[:], in_=inv1024), inc="dq", inc_n=16)
            for src, t in ((wfr, Wr), (wfi, Wi)):
                for k in range(8):
                    op("gpsimd", lambda e, s=src, t=t, k=k: e.dma_start(
                        out=t[:, k * 1024:(k + 1) * 1024],
                        in_=s[k * 128:(k + 1) * 128, :]), inc="dg", inc_n=16)
            consts_done = at("dq")
            w_done = at("dg")

            ln_state = {}

            def ln_tile(b, t0):
                for cch in range(8):
                    op("sync", lambda e, b=b, t0=t0, c=cch: e.dma_start(
                        out=x_sb[:, c * 512:(c + 1) * 512],
                        in_=x[b, c * 128:(c + 1) * 128, t0:t0 + 512]),
                       waits=([("v", at("v"))] if cch == 0 else ()),
                       inc="dq", inc_n=16)
                ld = at("dq")
                op("scalar", lambda e: e.activation(
                    out=sq_sb[:], in_=x_sb[:], func=AF.Square),
                   waits=[("dq", ld), ("v", at("v"))], inc="s")
                sq_ready = at("s")
                for jj in range(4):
                    for src, col in ((x_sb, 2 * jj), (sq_sb, 2 * jj + 1)):
                        for cch in range(8):
                            last = cch == 7
                            op("tensor", lambda e, src=src, col=col, jj=jj,
                               c=cch, f=(cch == 0), l=last: e.matmul(
                                   psStat[:, col:col + 1],
                                   src[:, c * 512 + jj * 128:c * 512 + jj * 128 + 128],
                                   i1024_sb[:], start=f, stop=l),
                               waits=([("s", sq_ready), ("dq", ld),
                                       ("v", at("v"))]
                                      if (col == 0 and cch == 0) else ()),
                               inc=("pe" if last else None))
                op("vector", lambda e: e.tensor_copy(stats_sb[:], psStat[:, 0:8]),
                   waits=[("pe", at("pe"))], inc="v")
                op("scalar", lambda e: e.activation(
                    out=msq2_sb[:], in_=stats_sb[:, 0:8:2], func=AF.Square),
                   waits=[("v", at("v"))], inc="s")
                op("vector", lambda e: e.tensor_sub(
                    var_sb[:], stats_sb[:, 1:8:2], msq2_sb[:]),
                   waits=[("s", at("s"))], inc="v")
                op("scalar", lambda e: e.activation(
                    out=ln_sb[:], in_=var_sb[:], func=AF.Ln, bias=1e-5),
                   waits=[("v", at("v")), ("dg", cpool_done)], inc="s")
                op("scalar", lambda e: e.activation(
                    out=fct_sb[:, 4:8], in_=ln_sb[:], func=AF.Exp,
                    scale=-0.5), inc="s")
                op("vector", lambda e: e.tensor_copy(
                    fct_sb[:, 0:4], stats_sb[:, 0:8:2]), inc="v")
                op("tensor", lambda e: e.transpose(
                    psTf[0:8, :], fct_sb[:, 0:8], idf_sb[:]),
                   waits=[("s", at("s")), ("v", at("v"))], inc="pe")
                op("vector", lambda e: e.tensor_copy(rowT_sb[:], psTf[0:8, :]),
                   waits=[("pe", at("pe"))], inc="v")
                rowv = at("v")
                for jj in range(4):
                    for w in range(2):
                        op("tensor", lambda e, jj=jj, w=w: e.matmul(
                            (psBC0 if w == 0 else psBC1)[:, jj * 128:(jj + 1) * 128],
                            sel8_sb[:, (jj + 4 * w) * 128:(jj + 4 * w + 1) * 128],
                            rowT_sb[:], start=True, stop=True),
                           waits=([("v", rowv)] if (jj == 0 and w == 0) else ()),
                           inc="pe")
                bc = at("pe")
                for cch in range(8):
                    op("vector", lambda e, c=cch: e.scalar_tensor_tensor(
                        sq_sb[:, c * 512:(c + 1) * 512], psBC0[:], -1.0,
                        x_sb[:, c * 512:(c + 1) * 512], ALU.mult, ALU.add),
                       waits=([("pe", bc)] if cch == 0 else ()), inc="v")
                    op("vector", lambda e, c=cch: e.tensor_mul(
                        xn_sb[:, c * 512:(c + 1) * 512],
                        sq_sb[:, c * 512:(c + 1) * 512],
                        psBC1[:]), inc="v")
                xn_ready = at("v")
                for cch in range(8):
                    op("sync", lambda e, b=b, t0=t0, c=cch: e.dma_start(
                        out=xn_out[b, c * 128:(c + 1) * 128, t0:t0 + 512],
                        in_=xn_sb[:, c * 512:(c + 1) * 512]),
                       waits=([("v", xn_ready)] if cch == 0 else ()),
                       inc="dq", inc_n=16)

            ln_iter = iter([(b, t0) for b in range(B)
                            for t0 in range(0, S, 512)])

            def maybe_ln():
                try:
                    b, t0 = next(ln_iter)
                except StopIteration:
                    return
                ln_tile(b, t0)

            evac_at = {0: 0, 1: 0}
            for dst, terms in ((Qr, ((cFr, Wr), (cFiN, Wi))),
                               (Qi, ((cFr, Wi), (cFi, Wr)))):
                for m in range(8):
                    for half in range(2):
                        bk = (m + half) % 2
                        ps = psA if bk == 0 else psB
                        nmm = 0
                        for k in range(8):
                            for A, Bm in terms:
                                first = nmm == 0
                                last = nmm == 15
                                op("tensor", lambda e, ps=ps, A=A, Bm=Bm, k=k, m=m,
                                   h=half, f=first, l=last: e.matmul(
                                       ps[:],
                                       A[:, k * 1024 + m * 128:k * 1024 + (m + 1) * 128],
                                       Bm[:, k * 1024 + h * 512:k * 1024 + h * 512 + 512],
                                       start=f, stop=l),
                                   waits=([("dq", consts_done), ("dg", w_done),
                                           ("s", evac_at[bk])]
                                          if nmm == 0 else ()),
                                   inc=("pe" if last else None))
                                nmm += 1
                        op("scalar", lambda e, ps=ps, dst=dst, m=m, h=half: e.copy(
                            out=dst[:, m * 1024 + h * 512:m * 1024 + h * 512 + 512],
                            in_=ps[:]),
                           waits=[("pe", at("pe"))], inc="s")
                        evac_at[bk] = at("s")
                        if half == 1:
                            maybe_ln()
            t_ev = 0
            for src, dst in ((Qr, Wr), (Qi, Wi)):
                for r in range(8):
                    for cc in range(8):
                        op("tensor", lambda e, src=src, r=r, cc=cc: e.transpose(
                            psT[:], src[:, cc * 1024 + r * 128:cc * 1024 + (r + 1) * 128],
                            idb_sb[:]),
                           waits=[("s", t_ev)], inc="pe")
                        op("scalar", lambda e, dst=dst, r=r, cc=cc: e.copy(
                            out=dst[:, r * 1024 + cc * 128:r * 1024 + (cc + 1) * 128],
                            in_=psT[:]),
                           waits=[("pe", at("pe"))], inc="s")
                        t_ev = at("s")
            for m in range(8):
                for half in range(2):
                    bk = (m + half) % 2
                    ps = psA if bk == 0 else psB
                    nmm = 0
                    for k in range(8):
                        for A, Bm in ((cGr, Wr), (cG2, Wi)):
                            first = nmm == 0
                            last = nmm == 15
                            op("tensor", lambda e, ps=ps, A=A, Bm=Bm, k=k, m=m,
                               h=half, f=first, l=last: e.matmul(
                                   ps[:],
                                   A[:, k * 1024 + m * 128:k * 1024 + (m + 1) * 128],
                                   Bm[:, k * 1024 + h * 512:k * 1024 + h * 512 + 512],
                                   start=f, stop=l),
                               waits=([("s", max(evac_at[bk], t_ev))]
                                      if nmm == 0 else ()),
                               inc=("pe" if last else None))
                            nmm += 1
                    op("scalar", lambda e, ps=ps, m=m, h=half: e.copy(
                        out=Qr[:, m * 1024 + h * 512:m * 1024 + h * 512 + 512],
                        in_=ps[:]),
                       waits=[("pe", at("pe"))], inc="s")
                    evac_at[bk] = at("s")
                    if half == 1:
                        maybe_ln()
            for r in range(8):
                for cc in range(8):
                    op("tensor", lambda e, r=r, cc=cc: e.transpose(
                        psT[:], Qr[:, cc * 1024 + r * 128:cc * 1024 + (r + 1) * 128],
                        idb_sb[:]),
                       waits=[("s", t_ev)], inc="pe")
                    op("scalar", lambda e, r=r, cc=cc: e.copy(
                        out=Qi[:, r * 1024 + cc * 128:r * 1024 + (cc + 1) * 128],
                        in_=psT[:]),
                       waits=[("pe", at("pe"))], inc="s")
                    t_ev = at("s")
            for r in range(8):
                op("sync", lambda e, r=r: e.dma_start(
                    out=M_out[r * 128:(r + 1) * 128, :],
                    in_=Qi[:, r * 1024:(r + 1) * 1024]),
                   waits=[("s", at("s"))], inc="dq", inc_n=16)
            while True:
                try:
                    b, t0 = next(ln_iter)
                except StopIteration:
                    break
                ln_tile(b, t0)
            sd.wait("gpsimd", [("dq", at("dq"))])

        if ctx is not None:
            return body
        run_block(nc, sched, body)
        sched.free()
    return nc


def build_attn0(dbg=False, ctx=None):
    nc = ctx["nc"] if ctx else bass.Bass("TRN2", target_bir_lowering=False)
    dram = _mk_dram(nc, ctx)
    xn = dram("xn", [B, D, S], BF16, "ExternalInput")
    wq_d = dram("wq", [D, D], F32, "ExternalInput")
    wk_d = dram("wk", [D, D], F32, "ExternalInput")
    wv_d = dram("wv", [D, D], F32, "ExternalInput")
    wo_d = dram("wo", [D, D], F32, "ExternalInput")
    wg1_d = dram("wg1", [D, 64], F32, "ExternalInput")
    wg2_d = dram("wg2", [64, D], F32, "ExternalInput")
    nw_d = dram("norm_w_pc", [128, 8], F32, "ExternalInput")
    cfm_d = dram("cfm", [D, S], BF16, "ExternalInput")
    sfm_d = dram("sfm", [D, S], BF16, "ExternalInput")
    ctm_d = dram("ctm", [S, D], BF16, "ExternalInput")
    stm_d = dram("stm", [S, D], BF16, "ExternalInput")
    maskU_d = dram("maskU", [128, 128], BF16, "ExternalInput")
    blk16_d = dram("blk16", [8 * 128, 16], F32, "ExternalInput")
    hexp_d = dram("hexp", [16, 8 * 128], BF16, "ExternalInput")
    a0_d = dram("a0", [B, D, S], F32, "ExternalOutput")
    if dbg:
        dbg_qt = dram("dbg_qt", [B, 4, 128, 4096], BF16, "ExternalOutput")
        dbg_e = dram("dbg_e", [B, 4, 128, 4096], BF16, "ExternalOutput")
        dbg_o = dram("dbg_o", [B, 4, 128, 4096], F32, "ExternalOutput")
        dbg_rq = dram("dbg_rq", [B, 4, 128, 4096], F32, "ExternalOutput")
        dbg_sc = dram("dbg_sc", [B, 4, 128, 512], F32, "ExternalOutput")
        dbg_carry = dram("dbg_carry", [B, 4, 128, 8], F32, "ExternalOutput")
        dbg_cs = dram("dbg_cs", [B, 4, 128, 4096], F32, "ExternalOutput")

    sb = ctx["arena"].t if ctx else nc.alloc_sbuf_tensor
    wq = sb("wq_s", [128, 8192], BF16)
    wk = sb("wk_s", [128, 8192], BF16)
    wv = sb("wv_s", [128, 8192], BF16)
    wo = sb("wo_s", [128, 8192], BF16)
    wg1 = sb("wg1_s", [128, 512], BF16)
    wg2 = sb("wg2_s", [64, 1024], BF16)
    nw = sb("nw_s", [128, 8], F32)
    blk16 = sb("blk16_s", [128, 128], F32)
    hexp = sb("hexp_s", [16, 1024], BF16)
    maskU = sb("maskU_s", [128, 128], BF16)
    ones_t = sb("ones_t", [128, 512], F32)
    xn_sc = sb("xn_sc", [128, 8192], BF16)
    cfm_t = sb("cfm_t", [128, 2048], BF16)
    sfm_t = sb("sfm_t", [128, 2048], BF16)
    ctm_t = sb("ctm_t", [128, 2048], BF16)
    stm_t = sb("stm_t", [128, 2048], BF16)
    e_bf = sb("e_bf", [128, 4096], BF16)
    csrq = sb("csrq", [128, 4096], F32)
    carry = [sb("carryA", [128, 8], F32), sb("carryB", [128, 8], F32)]
    rsum_scr = sb("rsum_scr", [128, 1], F32)
    qt_bf = sb("qt_bf", [128, 4096], BF16)
    e_scr = sb("e_scr", [128, 512], F32)
    ln_scr = sb("ln_scr", [128, 512], F32)
    t_scr = sb("t_scr", [128, 512], F32)
    u_scr = sb("u_scr", [128, 512], F32)
    v_tm = sb("v_tm", [128, 4096], BF16)
    etm_scr = sb("etm_scr", [128, 1024], BF16)
    kctm = sb("kctm", [128, 1024], BF16)
    kstm = sb("kstm", [128, 1024], BF16)
    qc_t = sb("qc_t", [128, 1024], BF16)
    qs_t = sb("qs_t", [128, 1024], BF16)
    kc_t = sb("kc_t", [128, 1024], BF16)
    ks_t = sb("ks_t", [128, 1024], BF16)
    atm = sb("atm", [128, 256], BF16)
    o_sb = sb("o_sb", [128, 4096], F32)
    Sc = [sb("Sc0", [128, 512], F32), sb("Sc1", [128, 512], F32)]
    Ss = [sb("Ss0", [128, 512], F32), sb("Ss1", [128, 512], F32)]
    Scb = sb("Scb", [128, 512], BF16)
    Ssb = sb("Ssb", [128, 512], BF16)
    g1_bf = sb("g1_bf", [64, 512], BF16)
    g2_sb = csrq
    sq_scr = sb("sq_scr", [128, 512], F32)
    rstd16 = sb("rstd16", [16, 512], F32)
    rstdb16 = sb("rstdb16", [16, 512], BF16)
    ln16 = sb("ln16", [16, 512], F32)
    nrm_bf = qt_bf
    a0_scr = sb("a0_scr", [128, 1024], F32)
    PS, psT, psTf = _mk_psum(nc, ctx)
    cpool, cvals = ctx["cpool"] if ctx else setup_const_pool(nc, [-0.6931472, 1e-6])

    sched = None if ctx else Sched(nc, ["dq", "dg", "pe", "v", "s"])

    def body(sd):
        op, at = sd.op, sd.at
        cp = emit_const_pool(sd, cpool, cvals)
        op("gpsimd", lambda e: e.memset(ones_t[:], 1.0), inc="dg")
        for t in (Sc[0], Sc[1], Ss[0], Ss[1], Scb, Ssb):
            op("gpsimd", lambda e, t=t: e.memset(t[:], 0.0), inc="dg")
        init_done = at("dg")
        for src, dst in ((wq_d, wq), (wk_d, wk), (wv_d, wv), (wo_d, wo)):
            for k in range(8):
                op("gpsimd", lambda e, s=src, d=dst, k=k: e.dma_start(
                    out=d[:, k * 1024:(k + 1) * 1024],
                    in_=s[k * 128:(k + 1) * 128, :]), inc="dg", inc_n=16)
        for k in range(8):
            op("gpsimd", lambda e, k=k: e.dma_start(
                out=wg1[:, k * 64:(k + 1) * 64],
                in_=wg1_d[k * 128:(k + 1) * 128, :]), inc="dg", inc_n=16)
        op("gpsimd", lambda e: e.dma_start(out=wg2[:], in_=wg2_d), inc="dg", inc_n=16)
        op("gpsimd", lambda e: e.dma_start(out=hexp[:], in_=hexp_d), inc="dg", inc_n=16)
        op("sync", lambda e: e.dma_start(out=nw[:], in_=nw_d), inc="dq", inc_n=16)
        for k in range(8):
            op("sync", lambda e, k=k: e.dma_start(
                out=blk16[:, k * 16:(k + 1) * 16],
                in_=blk16_d[k * 128:(k + 1) * 128, :]), inc="dq", inc_n=16)
        op("sync", lambda e: e.dma_start(out=maskU[:], in_=maskU_d), inc="dq", inc_n=16)
        w_done_g = at("dg")
        w_done_q = at("dq")

        NSC = S // 512
        dg_gate = at("dg")
        for b in range(B):
            if b > 0:
                st_reset_wait = [("pe", at("pe")), ("s", at("s"))]
                for t in (Sc[0], Sc[1], Ss[0], Ss[1], Scb, Ssb):
                    op("gpsimd", lambda e, t=t: e.memset(t[:], 0.0),
                       waits=st_reset_wait, inc="dg")
                    st_reset_wait = ()
                dg_gate = at("dg")
            op("vector", lambda e, b=b: e.memset(carry[0][:], 1.0),
               waits=[("dg", at("dg"))], inc="v")
            op("vector", lambda e, b=b: e.memset(carry[1][:], 1.0), inc="v")
            for sc in range(NSC):
                t0 = sc * 512
                xnb = (sc + b * NSC) % 2 * 4096
                if sc == 0 and b == 0:
                    for c in range(8):
                        op("sync", lambda e, b=b, t0=t0, c=c, xnb=xnb:
                           e.dma_start(
                               out=xn_sc[:, xnb + c * 512:xnb + (c + 1) * 512],
                               in_=xn[b, c * 128:(c + 1) * 128, t0:t0 + 512]),
                           inc="dq", inc_n=16)
                    xn_ld = at("dq")
                else:
                    xn_ld = xn_ld_next
                for m in range(8):
                    bank = PS[m % 2]
                    for k in range(8):
                        op("tensor", lambda e, m=m, k=k, bank=bank, xnb=xnb: e.matmul(
                            bank[:],
                            wk[:, k * 1024 + m * 128:k * 1024 + (m + 1) * 128],
                            xn_sc[:, xnb + k * 512:xnb + (k + 1) * 512],
                            start=(k == 0), stop=(k == 7)),
                           waits=([("dq", xn_ld), ("dg", max(w_done_g, dg_gate,
                                                              at("dg") if dbg else 0)),
                                   ("s", at("s")), ("v", at("v"))]
                                  if k == 0 else ()),
                           inc=("pe" if k == 7 else None))
                    op("scalar", lambda e, m=m, bank=bank: e.activation(
                        out=e_bf[:, m * 512:(m + 1) * 512], in_=bank[:],
                        func=AF.Exp), waits=[("pe", at("pe"))], inc="s")
                    op("scalar", lambda e, m=m, bank=bank: e.activation(
                        out=e_scr[:], in_=bank[:], func=AF.Exp,
                        accum_out=rsum_scr[:]), inc="s")
                    op("vector", lambda e, m=m, sc=sc: e.tensor_tensor_scan(
                        csrq[:, m * 512:(m + 1) * 512], ones_t[:], e_scr[:],
                        carry[sc % 2][:, m:m + 1], ALU.mult, ALU.add),
                       waits=[("s", at("s")), ("dg", init_done)], inc="v")
                    op("vector", lambda e, m=m, sc=sc: e.tensor_add(
                        carry[1 - sc % 2][:, m:m + 1], carry[sc % 2][:, m:m + 1],
                        rsum_scr[:]), waits=[("s", at("s"))], inc="v")
                    if dbg:
                        op("gpsimd", lambda e, b=b, sc=sc, m=m: e.dma_start(
                            out=dbg_cs[b, sc, :, m * 512:(m + 1) * 512],
                            in_=csrq[:, m * 512:(m + 1) * 512]),
                           waits=[("v", at("v"))], inc="dg", inc_n=16)
                    op("scalar", lambda e, m=m: e.activation(
                        out=ln_scr[:], in_=csrq[:, m * 512:(m + 1) * 512],
                        func=AF.Ln),
                       waits=[("v", at("v"))] + ([("dg", at("dg"))] if dbg else []),
                       inc="s")
                    op("scalar", lambda e, m=m: e.activation(
                        out=csrq[:, m * 512:(m + 1) * 512], in_=ln_scr[:],
                        func=AF.Exp, scale=-1.0, bias=-0.6931472),
                       waits=[("dg", cp)], inc="s")
                for m in range(8):
                    bank = PS[m % 2]
                    for k in range(8):
                        op("tensor", lambda e, m=m, k=k, bank=bank, xnb=xnb: e.matmul(
                            bank[:],
                            wq[:, k * 1024 + m * 128:k * 1024 + (m + 1) * 128],
                            xn_sc[:, xnb + k * 512:xnb + (k + 1) * 512],
                            start=(k == 0), stop=(k == 7)),
                           waits=([("s", at("s")), ("v", at("v"))]
                                  if k == 0 else ()),
                           inc=("pe" if k == 7 else None))
                    op("scalar", lambda e, bank=bank: e.activation(
                        out=t_scr[:], in_=bank[:], func=AF.Tanh, scale=0.5),
                       waits=[("pe", at("pe"))], inc="s")
                    op("vector", lambda e, bank=bank: e.scalar_tensor_tensor(
                        u_scr[:], t_scr[:], 1.0, bank[:], ALU.add, ALU.mult),
                       waits=[("s", at("s"))], inc="v")
                    op("vector", lambda e, m=m: e.tensor_mul(
                        qt_bf[:, m * 512:(m + 1) * 512], u_scr[:],
                        csrq[:, m * 512:(m + 1) * 512]), inc="v")
                for tc in range(4):
                    for half in range(2):
                        bank = PS[(tc * 2 + half) % 2]
                        for k in range(8):
                            op("tensor", lambda e, tc=tc, h=half, k=k, t0=t0,
                               bank=bank, xnb=xnb: e.matmul(
                                   bank[:],
                                   xn_sc[:, xnb + k * 512 + tc * 128:xnb + k * 512 + (tc + 1) * 128],
                                   wv[:, k * 1024 + h * 512:k * 1024 + h * 512 + 512],
                                   start=(k == 0), stop=(k == 7)),
                               waits=([("v", at("v")), ("s", at("s"))]
                                      if k == 0 else ()),
                               inc=("pe" if k == 7 else None))
                        op("scalar", lambda e, tc=tc, h=half, bank=bank:
                           e.activation(
                               out=v_tm[:, tc * 1024 + h * 512:tc * 1024 + h * 512 + 512],
                               in_=bank[:], func=AF.Identity),
                           waits=[("pe", at("pe"))], inc="s")
                if dbg:
                    op("gpsimd", lambda e, b=b, sc=sc: e.dma_start(
                        out=dbg_carry[b, sc], in_=carry[1 - sc % 2][:]),
                       waits=[("v", at("v"))], inc="dg", inc_n=16)
                    dw = [("v", at("v")), ("s", at("s"))]
                    for t_, d_ in ((qt_bf, dbg_qt), (e_bf, dbg_e), (csrq, dbg_rq)):
                        op("gpsimd", lambda e, t_=t_, d_=d_, b=b, sc=sc:
                           e.dma_start(out=d_[b, sc], in_=t_[:]),
                           waits=dw, inc="dg", inc_n=16)
                        dw = ()
                def trig_load(tau2):
                    tg2 = t0 + tau2 * 128
                    db2 = tau2 % 2
                    ld_w = [("v", at("v"))]
                    for s_, d_ in ((cfm_d, cfm_t), (sfm_d, sfm_t)):
                        for c in range(8):
                            op("sync", lambda e, s=s_, d=d_, db=db2, c=c,
                               tg=tg2: e.dma_start(
                                   out=d[:, db * 1024 + c * 128:db * 1024 + (c + 1) * 128],
                                   in_=s[c * 128:(c + 1) * 128, tg:tg + 128]),
                               waits=ld_w, inc="dq", inc_n=16)
                            ld_w = ()
                    for s_, d_ in ((ctm_d, ctm_t), (stm_d, stm_t)):
                        op("sync", lambda e, s=s_, d=d_, db=db2, tg=tg2:
                           e.dma_start(out=d[:, db * 1024:(db + 1) * 1024],
                                       in_=s[tg:tg + 128, :]),
                           inc="dq", inc_n=16)
                    return at("dq")

                trig_at = {0: trig_load(0)}
                nsc_all = b * NSC + sc + 1
                if nsc_all < B * NSC:
                    b2, sc2 = divmod(nsc_all, NSC)
                    t02 = sc2 * 512
                    xnb2 = nsc_all % 2 * 4096
                    pw = [("pe", at("pe"))]
                    for c in range(8):
                        op("sync", lambda e, b2=b2, t02=t02, c=c, xnb2=xnb2:
                           e.dma_start(
                               out=xn_sc[:, xnb2 + c * 512:xnb2 + (c + 1) * 512],
                               in_=xn[b2, c * 128:(c + 1) * 128, t02:t02 + 512]),
                           waits=pw, inc="dq", inc_n=16)
                        pw = ()
                    xn_ld_next = at("dq")
                for tau in range(4):
                    tg = t0 + tau * 128
                    db = tau % 2
                    trig_ld = trig_at[tau]
                    rw = [("dq", trig_ld)]
                    for dst, a_, b_ in ((qc_t, qt_bf, cfm_t), (qs_t, qt_bf, sfm_t),
                                        (kc_t, e_bf, cfm_t), (ks_t, e_bf, sfm_t)):
                        op("vector", lambda e, dst=dst, a_=a_, b_=b_,
                           db=db, tau=tau, xnb=xnb: e.tensor_mul(
                               dst[:].rearrange("p (c n) -> p c n", n=128),
                               a_[:].rearrange(
                                   "p (c n) -> p c n",
                                   n=512)[:, :, tau * 128:(tau + 1) * 128],
                               b_[:, db * 1024:(db + 1) * 1024].rearrange(
                                   "p (c n) -> p c n", n=128)),
                           waits=rw, inc="v")
                        rw = ()
                    rot_done = at("v")
                    if tau < 3:
                        trig_at[tau + 1] = trig_load(tau + 1)
                    for half in range(2):
                        bank = PS[half % 2]
                        for k in range(8):
                            op("tensor", lambda e, h=half, k=k, tau=tau, bank=bank,
                               xnb=xnb:
                               e.matmul(
                                   bank[:],
                                   xn_sc[:, xnb + k * 512 + tau * 128:xnb + k * 512 + (tau + 1) * 128],
                                   wk[:, k * 1024 + h * 512:k * 1024 + h * 512 + 512],
                                   start=(k == 0), stop=(k == 7)),
                               waits=([("s", at("s"))]
                                      if k == 0 else ()),
                               inc=("pe" if k == 7 else None))
                        op("scalar", lambda e, h=half, bank=bank: e.activation(
                            out=etm_scr[:, h * 512:h * 512 + 512], in_=bank[:],
                            func=AF.Exp), waits=[("pe", at("pe"))], inc="s")
                    etm_done = at("s")
                    for dst, trg in ((kctm, ctm_t), (kstm, stm_t)):
                        op("vector", lambda e, dst=dst, trg=trg, db=db:
                           e.tensor_mul(dst[:], etm_scr[:],
                                        trg[:, db * 1024:(db + 1) * 1024]),
                           waits=[("s", etm_done)], inc="v")
                    ktm_done = at("v")
                    mask_at, oevac_at, stadd_at = {}, {}, {}

                    def head_front(h):
                        par = (h % 2) * 64
                        hc_ = h // 2
                        bAT = PS[2 + h % 2]
                        hsl = slice(par, par + 64)
                        csl = slice(hc_ * 128, hc_ * 128 + 128)
                        at_w = [("v", mask_at[h - 2] if h >= 2 else rot_done)]
                        op("tensor", lambda e, h=h, bAT=bAT, hsl=hsl, csl=csl:
                           e.matmul(bAT[:, 0:128], kc_t[hsl, csl], qc_t[hsl, csl],
                                    start=True, stop=False),
                           waits=at_w, inc=None)
                        op("tensor", lambda e, h=h, bAT=bAT, hsl=hsl, csl=csl:
                           e.matmul(bAT[:, 0:128], ks_t[hsl, csl], qs_t[hsl, csl],
                                    start=False, stop=True), inc="pe")
                        op("vector", lambda e, h=h, bAT=bAT: e.tensor_mul(
                            atm[:, (h % 2) * 128:(h % 2) * 128 + 128],
                            bAT[:, 0:128], maskU[:]),
                           waits=[("pe", at("pe"))], inc="v")
                        mask_at[h] = at("v")

                    def head_tail(h):
                        par = (h % 2) * 64
                        hc_ = h // 2
                        bO = PS[4 + h % 2]
                        bKV = PS[0 + h % 2]
                        hsl = slice(par, par + 64)
                        csl = slice(hc_ * 128, hc_ * 128 + 128)
                        o_w = [("v", mask_at[h])] + (
                            [("s", oevac_at[h - 2])] if h >= 2 else [])
                        op("tensor", lambda e, h=h, bO=bO, tau=tau, par=par:
                           e.matmul(bO[par:par + 64, 0:128],
                                    v_tm[:, tau * 1024 + h * 64:tau * 1024 + (h + 1) * 64],
                                    atm[:, (h % 2) * 128:(h % 2) * 128 + 128],
                                    start=True, stop=False),
                           waits=o_w, inc=None)
                        op("tensor", lambda e, h=h, bO=bO, hsl=hsl, csl=csl,
                           par=par, hc_=hc_: e.matmul(
                               bO[par:par + 64, 0:128],
                               Scb[hsl, hc_ * 64:hc_ * 64 + 64], qc_t[hsl, csl],
                               start=False, stop=False), inc=None)
                        op("tensor", lambda e, h=h, bO=bO, hsl=hsl, csl=csl,
                           par=par, hc_=hc_: e.matmul(
                               bO[par:par + 64, 0:128],
                               Ssb[hsl, hc_ * 64:hc_ * 64 + 64], qs_t[hsl, csl],
                               start=False, stop=True), inc="pe")
                        op("scalar", lambda e, h=h, bO=bO, tau=tau, par=par:
                           e.activation(
                               out=o_sb[par:par + 64,
                                        (h // 2) * 512 + tau * 128:
                                        (h // 2) * 512 + (tau + 1) * 128],
                               in_=bO[par:par + 64, 0:128], func=AF.Identity),
                           waits=[("pe", at("pe"))], inc="s")
                        oevac_at[h] = at("s")
                        kv_w = [("v", stadd_at[h - 2] if h >= 2 else ktm_done)]
                        op("tensor", lambda e, h=h, bKV=bKV, tau=tau, par=par:
                           e.matmul(bKV[par:par + 64, 0:64],
                                    kctm[:, h * 64:(h + 1) * 64],
                                    v_tm[:, tau * 1024 + h * 64:tau * 1024 + (h + 1) * 64],
                                    start=True, stop=True),
                           waits=kv_w, inc=None)
                        op("tensor", lambda e, h=h, bKV=bKV, tau=tau, par=par:
                           e.matmul(bKV[par:par + 64, 64:128],
                                    kstm[:, h * 64:(h + 1) * 64],
                                    v_tm[:, tau * 1024 + h * 64:tau * 1024 + (h + 1) * 64],
                                    start=True, stop=True), inc="pe")
                        cur = (tau + sc * 4) % 2
                        nxt = 1 - cur
                        op("vector", lambda e, h=h, bKV=bKV, par=par, hc_=hc_,
                           cur=cur, nxt=nxt: e.tensor_add(
                               Sc[nxt][par:par + 64, hc_ * 64:hc_ * 64 + 64],
                               Sc[cur][par:par + 64, hc_ * 64:hc_ * 64 + 64],
                               bKV[par:par + 64, 0:64]),
                           waits=[("pe", at("pe"))], inc="v")
                        op("vector", lambda e, h=h, bKV=bKV, par=par, hc_=hc_,
                           cur=cur, nxt=nxt: e.tensor_add(
                               Ss[nxt][par:par + 64, hc_ * 64:hc_ * 64 + 64],
                               Ss[cur][par:par + 64, hc_ * 64:hc_ * 64 + 64],
                               bKV[par:par + 64, 64:128]), inc="v")
                        stadd_at[h] = at("v")
                        op("scalar", lambda e, h=h, par=par, hc_=hc_, nxt=nxt:
                           e.activation(
                               out=Scb[par:par + 64, hc_ * 64:hc_ * 64 + 64],
                               in_=Sc[nxt][par:par + 64, hc_ * 64:hc_ * 64 + 64],
                               func=AF.Identity),
                           waits=[("v", at("v"))], inc="s")
                        op("scalar", lambda e, h=h, par=par, hc_=hc_, nxt=nxt:
                           e.activation(
                               out=Ssb[par:par + 64, hc_ * 64:hc_ * 64 + 64],
                               in_=Ss[nxt][par:par + 64, hc_ * 64:hc_ * 64 + 64],
                               func=AF.Identity), inc="s")

                    for h in range(16):
                        head_front(h)
                        if h >= 1:
                            head_tail(h - 1)
                    head_tail(15)
                if dbg:
                    op("gpsimd", lambda e, b=b, sc=sc: e.dma_start(
                        out=dbg_o[b, sc], in_=o_sb[:]),
                       waits=[("s", at("s"))], inc="dg", inc_n=16)
                    op("gpsimd", lambda e, b=b, sc=sc: e.dma_start(
                        out=dbg_sc[b, sc], in_=Sc[(sc * 4 + 4) % 2][:]),
                       waits=[("v", at("v"))], inc="dg", inc_n=16)
                bank = PS[0]
                for k in range(8):
                    op("tensor", lambda e, k=k, bank=bank, xnb=xnb: e.matmul(
                        bank[0:64, :], wg1[:, k * 64:(k + 1) * 64],
                        xn_sc[:, xnb + k * 512:xnb + (k + 1) * 512],
                        start=(k == 0), stop=(k == 7)),
                       waits=([("v", at("v")), ("s", at("s"))] if k == 0 else ()),
                       inc=("pe" if k == 7 else None))
                op("scalar", lambda e, bank=bank: e.activation(
                    out=g1_bf[:], in_=bank[0:64, :], func=AF.Identity),
                   waits=[("pe", at("pe"))], inc="s")
                g1_done = at("s")
                for m in range(8):
                    bank = PS[1 + m % 2]
                    op("tensor", lambda e, m=m, bank=bank: e.matmul(
                        bank[:], wg2[:, m * 128:(m + 1) * 128], g1_bf[:],
                        start=True, stop=True),
                       waits=([("s", g1_done)] if m == 0 else
                              [("s", at("s"))]), inc="pe")
                    op("scalar", lambda e, m=m, bank=bank: e.activation(
                        out=t_scr[:], in_=bank[:], func=AF.Tanh, scale=0.5),
                       waits=[("pe", at("pe"))], inc="s")
                    op("vector", lambda e, m=m: e.scalar_tensor_tensor(
                        g2_sb[:, m * 512:(m + 1) * 512], t_scr[:], 1.0,
                        o_sb[:, m * 512:(m + 1) * 512], ALU.add, ALU.mult),
                       waits=[("s", at("s"))], inc="v")
                    op("scalar", lambda e, m=m: e.activation(
                        out=sq_scr[:], in_=g2_sb[:, m * 512:(m + 1) * 512],
                        func=AF.Square), waits=[("v", at("v"))], inc="s")
                    op("tensor", lambda e, m=m: e.matmul(
                        PS[3][0:16, :], blk16[:, m * 16:(m + 1) * 16], sq_scr[:],
                        start=(m == 0), stop=(m == 7)),
                       waits=[("s", at("s"))],
                       inc=("pe" if m == 7 else None))
                op("scalar", lambda e: e.activation(
                    out=ln16[:], in_=PS[3][0:16, :], func=AF.Ln,
                    scale=1.0 / 256.0, bias=1e-6),
                   waits=[("pe", at("pe")), ("dg", cp)], inc="s")
                op("scalar", lambda e: e.activation(
                    out=rstd16[:], in_=ln16[:], func=AF.Exp, scale=-0.5,
                    bias=-0.6931472), inc="s")
                op("scalar", lambda e: e.activation(
                    out=rstdb16[:], in_=rstd16[:], func=AF.Identity), inc="s")
                rstd_done = at("s")
                for c in range(8):
                    bank = PS[4 + c % 2]
                    op("tensor", lambda e, c=c, bank=bank: e.matmul(
                        bank[:], hexp[:, c * 128:(c + 1) * 128], rstdb16[:],
                        start=True, stop=True),
                       waits=([("s", rstd_done)] if c == 0 else
                              [("v", at("v"))]), inc="pe")
                    op("vector", lambda e, c=c, bank=bank: e.tensor_mul(
                        u_scr[:], g2_sb[:, c * 512:(c + 1) * 512], bank[:]),
                       waits=[("pe", at("pe"))], inc="v")
                    op("vector", lambda e, c=c: e.tensor_scalar(
                        nrm_bf[:, c * 512:(c + 1) * 512], u_scr[:],
                        nw[:, c:c + 1], None, ALU.mult), inc="v")
                nrm_done = at("v")
                a0dma_at = {}
                for m in range(8):
                    bank = PS[0 + m % 2]
                    for k in range(8):
                        op("tensor", lambda e, m=m, k=k, bank=bank: e.matmul(
                            bank[:],
                            wo[:, k * 1024 + m * 128:k * 1024 + (m + 1) * 128],
                            nrm_bf[:, k * 512:(k + 1) * 512],
                            start=(k == 0), stop=(k == 7)),
                           waits=([("v", nrm_done)] if (m == 0 and k == 0) else
                                  ([("v", at("v"))] if k == 0 else ())),
                           inc=("pe" if k == 7 else None))
                    op("vector", lambda e, m=m, bank=bank: e.tensor_copy(
                        a0_scr[:, (m % 2) * 512:(m % 2 + 1) * 512], bank[:]),
                       waits=[("pe", at("pe"))] + (
                           [("dq", a0dma_at[m - 2])] if m >= 2 else []),
                       inc="v")
                    a0dma_at[m] = op("sync", lambda e, b=b, m=m, t0=t0: e.dma_start(
                        out=a0_d[b, m * 128:(m + 1) * 128, t0:t0 + 512],
                        in_=a0_scr[:, (m % 2) * 512:(m % 2 + 1) * 512]),
                       waits=[("v", at("v"))], inc="dq", inc_n=16)
        sd.wait("gpsimd", [("dq", at("dq"))])

    if ctx is not None:
        return body
    run_block(nc, sched, body)
    sched.free()
    return nc


def build_mid(ctx=None):
    nc = ctx["nc"] if ctx else bass.Bass("TRN2", target_bir_lowering=False)
    dram = _mk_dram(nc, ctx)
    x = dram("x", [B, D, S], F32, "ExternalInput")
    a0 = dram("a0", [B, D, S], F32, "ExternalInput")
    xn0 = dram("xn0", [B, D, S], BF16, "ExternalInput")
    M_d = dram("M", [D, D], BF16, "ExternalInput")
    bias_d = dram("bias_pc", [128, 8], F32, "ExternalInput")
    idf = dram("ident_f32", [128, 128], F32, "ExternalInput")
    sel8 = dram("sel8", [8, 1024], F32, "ExternalInput")
    inv1024 = dram("inv1024", [128, 1], F32, "ExternalInput")
    x1_out = dram("x1", [B, D, S], F32, "ExternalOutput")
    xn1_out = dram("xn1", [B, D, S], BF16, "ExternalOutput")

    sb = ctx["arena"].t if ctx else nc.alloc_sbuf_tensor
    M_sb = sb("M_sb", [128, 8192], BF16)
    bias = sb("bias_sb", [128, 8], F32)
    idf_sb = sb("idf_sb", [128, 128], F32)
    sel8_sb = sb("sel8_sb", [8, 1024], F32)
    i1024_sb = sb("i1024_sb", [128, 1], F32)
    x_sb = sb("x_sb", [128, 8192], F32)
    a0_sb = sb("a0_sb", [128, 8192], F32)
    xn0_sb = sb("xn0_sb", [128, 8192], BF16)
    x1_sb = sb("x1_sb", [128, 8192], F32)
    sq_sb = sb("sq_sb", [128, 4096], F32)
    u_sb = sb("u_sb", [128, 4096], F32)
    xn_sb = sb("xn_sb", [128, 4096], BF16)
    stats_sb = sb("stats_sb", [128, 8], F32)
    var_sb = sb("var_sb", [128, 4], F32)
    msq2_sb = sb("msq2_sb", [128, 4], F32)
    ln_sb = sb("ln_sb", [128, 4], F32)
    fct_sb = sb("fct_sb", [128, 8], F32)
    rowT_sb = sb("rowT_sb", [8, 128], F32)
    PS, psT, psTf = _mk_psum(nc, ctx)
    psA, psB = PS[0], PS[1]
    psStat = PS[2]
    psBC0, psBC1 = PS[3], PS[4]
    cpool, cvals = ctx["cpool"] if ctx else setup_const_pool(nc, [1e-5])

    sched = None if ctx else Sched(nc, ["dq", "dg", "pe", "v", "s"])
    tiles = [(b, t0) for b in range(B) for t0 in range(0, S, 512)]

    def body(sd):
        op, at = sd.op, sd.at
        cp = emit_const_pool(sd, cpool, cvals)
        for k in range(8):
            op("gpsimd", lambda e, k=k: e.dma_start(
                out=M_sb[:, k * 1024:(k + 1) * 1024],
                in_=M_d[k * 128:(k + 1) * 128, :]), inc="dg", inc_n=16)
        op("sync", lambda e: e.dma_start(out=bias[:], in_=bias_d), inc="dq", inc_n=16)
        op("sync", lambda e: e.dma_start(out=idf_sb[:], in_=idf), inc="dq", inc_n=16)
        op("sync", lambda e: e.dma_start(out=sel8_sb[:], in_=sel8), inc="dq", inc_n=16)
        op("sync", lambda e: e.dma_start(out=i1024_sb[:], in_=inv1024), inc="dq", inc_n=16)
        w_g, w_q = at("dg"), at("dq")
        ld_at, x1_at = {}, {}

        def load(i):
            b, t0 = tiles[i]
            o = i % 2 * 4096
            lw = [("v", at("v")), ("s", at("s"))]
            for src_, dst in ((x, x_sb), (a0, a0_sb), (xn0, xn0_sb)):
                for c in range(8):
                    op("sync", lambda e, s=src_, d=dst, b=b, t0=t0, c=c, o=o:
                       e.dma_start(out=d[:, o + c * 512:o + (c + 1) * 512],
                                   in_=s[b, c * 128:(c + 1) * 128, t0:t0 + 512]),
                       waits=lw, inc="dq", inc_n=16)
                    lw.clear()
            ld_at[i] = at("dq")

        def front(i):
            b, t0 = tiles[i]
            o = i % 2 * 4096
            ld = ld_at[i]
            for m in range(8):
                bank = psA if m % 2 == 0 else psB
                for k in range(8):
                    op("tensor", lambda e, m=m, k=k, bank=bank, o=o: e.matmul(
                        bank[:],
                        M_sb[:, k * 1024 + m * 128:k * 1024 + (m + 1) * 128],
                        xn0_sb[:, o + k * 512:o + (k + 1) * 512],
                        start=(k == 0), stop=(k == 7)),
                       waits=([("dq", ld), ("dg", w_g), ("v", at("v"))]
                              if k == 0 else ()),
                       inc=("pe" if k == 7 else None))
                op("vector", lambda e, m=m, bank=bank, o=o: e.scalar_tensor_tensor(
                    u_sb[:, m * 512:(m + 1) * 512], bank[:], bias[:, m:m + 1],
                    a0_sb[:, o + m * 512:o + (m + 1) * 512], ALU.add, ALU.add),
                   waits=[("pe", at("pe"))], inc="v")
                op("vector", lambda e, m=m, o=o: e.tensor_add(
                    x1_sb[:, o + m * 512:o + (m + 1) * 512],
                    u_sb[:, m * 512:(m + 1) * 512],
                    x_sb[:, o + m * 512:o + (m + 1) * 512]), inc="v")
            x1_done = at("v")
            for c in range(8):
                op("sync", lambda e, b=b, t0=t0, c=c, o=o: e.dma_start(
                    out=x1_out[b, c * 128:(c + 1) * 128, t0:t0 + 512],
                    in_=x1_sb[:, o + c * 512:o + (c + 1) * 512]),
                   waits=([("v", x1_done)] if c == 0 else ()),
                   inc="dq", inc_n=16)
            x1_at[i] = x1_done

        def tail(i):
            b, t0 = tiles[i]
            o = i % 2 * 4096
            op("scalar", lambda e, o=o: e.activation(
                out=sq_sb[:], in_=x1_sb[:, o:o + 4096], func=AF.Square),
               waits=[("v", at("v"))], inc="s")
            sq_ready = at("s")
            for jj in range(4):
                for src_, col, off in ((x1_sb, 0, o), (sq_sb, 1, 0)):
                    for cch in range(8):
                        last = cch == 7
                        op("tensor", lambda e, src=src_, col=2 * jj + col,
                           jj=jj, c=cch, off=off, f=(cch == 0), l=last: e.matmul(
                               psStat[:, col:col + 1],
                               src[:, off + c * 512 + jj * 128:
                                   off + c * 512 + jj * 128 + 128],
                               i1024_sb[:], start=f, stop=l),
                           waits=([("s", sq_ready), ("v", at("v"))]
                                  if (col == 2 * jj and cch == 0) else ()),
                           inc=("pe" if last else None))
            op("vector", lambda e: e.tensor_copy(stats_sb[:], psStat[:, 0:8]),
               waits=[("pe", at("pe"))], inc="v")
            op("scalar", lambda e: e.activation(
                out=msq2_sb[:], in_=stats_sb[:, 0:8:2], func=AF.Square),
               waits=[("v", at("v"))], inc="s")
            op("vector", lambda e: e.tensor_sub(
                var_sb[:], stats_sb[:, 1:8:2], msq2_sb[:]),
               waits=[("s", at("s"))], inc="v")
            op("scalar", lambda e: e.activation(
                out=ln_sb[:], in_=var_sb[:], func=AF.Ln, bias=1e-5),
               waits=[("v", at("v")), ("dg", cp)], inc="s")
            op("scalar", lambda e: e.activation(
                out=fct_sb[:, 4:8], in_=ln_sb[:], func=AF.Exp,
                scale=-0.5), inc="s")
            op("vector", lambda e: e.tensor_copy(
                fct_sb[:, 0:4], stats_sb[:, 0:8:2]), inc="v")
            op("tensor", lambda e: e.transpose(
                psTf[0:8, :], fct_sb[:, 0:8], idf_sb[:]),
               waits=[("s", at("s")), ("v", at("v"))], inc="pe")
            op("vector", lambda e: e.tensor_copy(rowT_sb[:], psTf[0:8, :]),
               waits=[("pe", at("pe"))], inc="v")
            rowv = at("v")
            for jj in range(4):
                for w in range(2):
                    op("tensor", lambda e, jj=jj, w=w: e.matmul(
                        (psBC0 if w == 0 else psBC1)[:, jj * 128:(jj + 1) * 128],
                        sel8_sb[:, (jj + 4 * w) * 128:(jj + 4 * w + 1) * 128],
                        rowT_sb[:], start=True, stop=True),
                       waits=([("v", rowv)] if (jj == 0 and w == 0) else ()),
                       inc="pe")
            bc = at("pe")
            for cch in range(8):
                op("vector", lambda e, c=cch, o=o: e.scalar_tensor_tensor(
                    sq_sb[:, c * 512:(c + 1) * 512], psBC0[:], -1.0,
                    x1_sb[:, o + c * 512:o + (c + 1) * 512], ALU.mult, ALU.add),
                   waits=([("pe", bc)] if cch == 0 else ()), inc="v")
                op("vector", lambda e, c=cch: e.tensor_mul(
                    xn_sb[:, c * 512:(c + 1) * 512],
                    sq_sb[:, c * 512:(c + 1) * 512],
                    psBC1[:]), inc="v")
            xn_ready = at("v")
            for cch in range(8):
                op("sync", lambda e, b=b, t0=t0, c=cch: e.dma_start(
                    out=xn1_out[b, c * 128:(c + 1) * 128, t0:t0 + 512],
                    in_=xn_sb[:, c * 512:(c + 1) * 512]),
                   waits=([("v", xn_ready)] if cch == 0 else ()),
                   inc="dq", inc_n=16)

        load(0)
        for i in range(len(tiles)):
            if i + 1 < len(tiles):
                load(i + 1)
            front(i)
            if i >= 1:
                tail(i - 1)
        tail(len(tiles) - 1)
        sd.wait("gpsimd", [("dq", at("dq"))])

    if ctx is not None:
        return body
    run_block(nc, sched, body)
    sched.free()
    return nc

def build_attn1(phases=('t1','t2','t3','t4','t5'), dbg=False, ctx=None):
    nc = ctx["nc"] if ctx else bass.Bass("TRN2", target_bir_lowering=False)
    dram = _mk_dram(nc, ctx)
    xn = dram("xn1", [B, D, S], BF16, "ExternalInput")
    x1_d = dram("x1", [B, D, S], F32, "ExternalInput")
    wq_d = dram("wq", [D, D], F32, "ExternalInput")
    wk_d = dram("wk", [D, D], F32, "ExternalInput")
    wv_d = dram("wv", [D, D], F32, "ExternalInput")
    wgq_d = dram("wgq", [D, 512], F32, "ExternalInput")
    wgk_d = dram("wgk", [D, 512], F32, "ExternalInput")
    wgv_d = dram("wgv", [D, 512], F32, "ExternalInput")
    rout_d = dram("r_out_w", [512, D], F32, "ExternalInput")
    wout_d = dram("w_out", [D, D], F32, "ExternalInput")
    glgr_d = dram("glgr", [128, 32], F32, "ExternalInput")
    maskU_d = dram("maskU", [128, 128], BF16, "ExternalInput")
    maskLs_d = dram("maskLs", [128, 128], BF16, "ExternalInput")
    lmask_d = dram("lmask", [64, S], BF16, "ExternalInput")
    hexp_d = dram("hexp", [16, 8 * 128], BF16, "ExternalInput")
    lexp_d = dram("lexp", [8, 4 * 128], BF16, "ExternalInput")
    idf_d = dram("ident_f32", [128, 128], F32, "ExternalInput")
    onesb_d = dram("ones_bf", [128, 1], BF16, "ExternalInput")
    a1_d = dram("a1", [B, D, S], F32, "ExternalOutput")
    if dbg:
        dbg_fct = dram("dbg_fct", [2, 128, 56], F32, "ExternalOutput")
        dbg_fs = dram("dbg_fs", [2, 128, 56], F32, "ExternalOutput")
        dbg_loc = dram("dbg_loc", [2, 128, 1024], F32, "ExternalOutput")
        dbg_rl = dram("dbg_rl", [2, 128, 512], BF16, "ExternalOutput")
        dbg_comb = dram("dbg_comb", [2, 128, 1024], BF16, "ExternalOutput")
        dbg_pt = dram("dbg_pt", [2, 128, 768], BF16, "ExternalOutput")
        dbg_k = dram("dbg_k", [2, 128, 4096], BF16, "ExternalOutput")
        dbg_v = dram("dbg_v", [2, 128, 4096], BF16, "ExternalOutput")

    sb = ctx["arena"].t if ctx else nc.alloc_sbuf_tensor
    wq = sb("wq_s", [128, 8192], BF16)
    wk = sb("wk_s", [128, 8192], BF16)
    wv = sb("wv_s", [128, 8192], BF16)
    wout = sb("wout_s", [128, 8192], BF16)
    wgq = sb("wgq_s", [128, 4096], BF16)
    wgk = sb("wgk_s", [128, 4096], BF16)
    wgv = sb("wgv_s", [128, 4096], BF16)
    rout = sb("rout_s", [128, 4096], BF16)
    glgr = sb("glgr_s", [128, 32], BF16)
    maskU = sb("maskU_s", [128, 128], BF16)
    maskLs = sb("maskLs_s", [128, 128], BF16)
    lmask = sb("lmask_s", [64, 2048], BF16)
    hexp = sb("hexp_s", [16, 1024], BF16)
    lexp = sb("lexp_s", [8, 512], BF16)
    idf_sb = sb("idf_s", [128, 128], F32)
    onesb = sb("onesb_s", [128, 1], BF16)
    xn_st = sb("xn_st", [128, 8192], BF16)
    x1_st = sb("x1_st", [128, 4096], F32)
    Q_bf = sb("Q_bf", [128, 4096], BF16)
    K_bf = sb("K_bf", [128, 4096], BF16)
    v_tm = sb("v_tm1", [128, 4096], BF16)
    K_tail = sb("K_tail", [128, 2048], BF16)
    V_tail = sb("V_tail", [128, 2048], BF16)
    GQ_bf = sb("GQ_bf", [128, 2048], BF16)
    xpool = sb("xpool", [128, 512], F32)
    xpool_bf = sb("xpool_bf", [128, 512], BF16)
    GK_sb = sb("GK_sb", [128, 256], BF16)
    GV_sb = sb("GV_sb", [64, 512], BF16)
    P_T = sb("P_T", [128, 768], BF16)
    eL_T = sb("eL_T", [64, 256], BF16)
    local_sb = sb("local_sb", [128, 1024], F32)
    remlat_sb = sb("remlat_sb", [128, 512], BF16)
    remn_sb = sb("remn_sb", [128, 512], BF16)
    rem_sb = sb("rem_sb", [128, 256], F32)
    fct = sb("fct", [128, 56], F32)
    f_scr = sb("f_scr", [128, 56], F32)
    fT = sb("fT", [16, 256], BF16)
    rlT = sb("rlT", [8, 128], BF16)
    comb_bf = sb("comb_bf", [128, 1024], BF16)
    c1_sb = sb("c1_sb", [128, 128], F32)
    rem2_sb = sb("rem2_sb", [128, 128], F32)
    tg_sb = sb("tg_sb", [128, 32], F32)
    a1_scr = sb("a1_scr", [128, 2048], F32)
    PS, psT, psTf = _mk_psum(nc, ctx)

    sched = None if ctx else Sched(nc, ["dq", "dg", "pe", "v", "s"])

    def body(sd):
        op, at = sd.op, sd.at
        for src, dst, n in ((wq_d, wq, 8), (wk_d, wk, 8), (wv_d, wv, 8),
                            (wout_d, wout, 8), (wgq_d, wgq, 8), (wgk_d, wgk, 8),
                            (wgv_d, wgv, 8), (rout_d, rout, 4)):
            w = 1024 if n == 8 and dst in (wq, wk, wv, wout) else 512
            w = dst.shape[1] // n
            for k in range(n):
                op("gpsimd", lambda e, s=src, d=dst, k=k, w=w: e.dma_start(
                    out=d[:, k * w:(k + 1) * w],
                    in_=s[k * 128:(k + 1) * 128, :]), inc="dg", inc_n=16)
        op("gpsimd", lambda e: e.dma_start(out=glgr[:], in_=glgr_d),
           inc="dg", inc_n=16)
        for src, dst in ((maskU_d, maskU), (maskLs_d, maskLs), (lmask_d, lmask),
                         (hexp_d, hexp), (lexp_d, lexp), (idf_d, idf_sb),
                         (onesb_d, onesb)):
            op("sync", lambda e, s=src, d=dst: e.dma_start(out=d[:], in_=s),
               inc="dq", inc_n=16)
        w_g, w_q = at("dg"), at("dq")

        NST = S // 512
        fT_ev_prev = 0
        a1dma_at = {}
        for b in range(B):
            op("gpsimd", lambda e: e.memset(xpool[:], 0.0),
               waits=[("pe", at("pe")), ("s", at("s")), ("v", at("v"))],
               inc="dg")
            xp_init = at("dg")
            for st in range(NST):
                t0 = st * 512
                if st > 0:
                    tw = [("pe", at("pe"))]
                    for c in range(8):
                        op("gpsimd", lambda e, c=c: e.tensor_copy(
                            K_tail[:, c * 256:(c + 1) * 256],
                            K_bf[:, c * 512 + 256:(c + 1) * 512]),
                           waits=tw, inc="dg")
                        tw = ()
                    op("gpsimd", lambda e: e.tensor_copy(
                        V_tail[:], v_tm[:, 2048:4096]), inc="dg")
                tails = at("dg")
                xsb = (b * NST + st) % 2 * 4096
                if b == 0 and st == 0:
                    for c in range(8):
                        op("sync", lambda e, b=b, t0=t0, c=c, xsb=xsb:
                           e.dma_start(
                               out=xn_st[:, xsb + c * 512:xsb + (c + 1) * 512],
                               in_=xn[b, c * 128:(c + 1) * 128, t0:t0 + 512]),
                           inc="dq", inc_n=16)
                    ld = at("dq")
                else:
                    ld = xn_ld_next
                x1w = [("v", at("v"))]
                for c in range(8):
                    op("sync", lambda e, b=b, t0=t0, c=c: e.dma_start(
                        out=x1_st[:, c * 512:(c + 1) * 512],
                        in_=x1_d[b, c * 128:(c + 1) * 128, t0:t0 + 512]),
                       waits=x1w, inc="dq", inc_n=16)
                    x1w = ()
                x1_ld = at("dq")
                nst_all = b * NST + st + 1
                if nst_all < B * NST:
                    b2, st2 = divmod(nst_all, NST)
                    t02 = st2 * 512
                    xsb2 = nst_all % 2 * 4096
                    pw = [("pe", at("pe"))]
                    for c in range(8):
                        op("sync", lambda e, b2=b2, t02=t02, c=c, xsb2=xsb2:
                           e.dma_start(
                               out=xn_st[:, xsb2 + c * 512:xsb2 + (c + 1) * 512],
                               in_=xn[b2, c * 128:(c + 1) * 128, t02:t02 + 512]),
                           waits=pw, inc="dq", inc_n=16)
                        pw = ()
                    xn_ld_next = at("dq")
                for wsrc, dst in ((wq, Q_bf), (wk, K_bf)):
                    for m in range(8):
                        bank = PS[m % 2]
                        for k in range(8):
                            op("tensor", lambda e, wsrc=wsrc, m=m, k=k, bank=bank,
                               xsb=xsb:
                               e.matmul(
                                   bank[:],
                                   wsrc[:, k * 1024 + m * 128:k * 1024 + (m + 1) * 128],
                                   xn_st[:, xsb + k * 512:xsb + (k + 1) * 512],
                                   start=(k == 0), stop=(k == 7)),
                               waits=([("dq", ld), ("dg", max(w_g, tails)),
                                       ("s", at("s")), ("v", at("v"))]
                                      if k == 0 else ()),
                               inc=("pe" if k == 7 else None))
                        op("scalar", lambda e, dst=dst, m=m, bank=bank:
                           e.activation(out=dst[:, m * 512:(m + 1) * 512],
                                        in_=bank[:], func=AF.Identity),
                           waits=[("pe", at("pe"))], inc="s")
                for tc in range(4):
                    for half in range(2):
                        bank = PS[(tc * 2 + half) % 2]
                        for k in range(8):
                            op("tensor", lambda e, tc=tc, h=half, k=k, bank=bank,
                               xsb=xsb:
                               e.matmul(
                                   bank[:],
                                   xn_st[:, xsb + k * 512 + tc * 128:xsb + k * 512 + (tc + 1) * 128],
                                   wv[:, k * 1024 + h * 512:k * 1024 + h * 512 + 512],
                                   start=(k == 0), stop=(k == 7)),
                               waits=([("s", at("s"))] if k == 0 else ()),
                               inc=("pe" if k == 7 else None))
                        op("scalar", lambda e, tc=tc, h=half, bank=bank:
                           e.activation(
                               out=v_tm[:, tc * 1024 + h * 512:tc * 1024 + h * 512 + 512],
                               in_=bank[:], func=AF.Identity),
                           waits=[("pe", at("pe"))], inc="s")
                for gm in range(4):
                    bank = PS[gm % 2]
                    for k in range(8):
                        op("tensor", lambda e, gm=gm, k=k, bank=bank, xsb=xsb: e.matmul(
                            bank[:],
                            wgq[:, k * 512 + gm * 128:k * 512 + (gm + 1) * 128],
                            xn_st[:, xsb + k * 512:xsb + (k + 1) * 512],
                            start=(k == 0), stop=(k == 7)),
                           waits=([("s", at("s"))] if k == 0 else ()),
                           inc=("pe" if k == 7 else None))
                    op("scalar", lambda e, gm=gm, bank=bank: e.activation(
                        out=GQ_bf[:, gm * 512:(gm + 1) * 512], in_=bank[:],
                        func=AF.Identity),
                       waits=[("pe", at("pe"))], inc="s")
                for c in range(8):
                    op("vector", lambda e, c=c, st=st, xsb=xsb: e.reduce_sum(
                        xpool[:, c * 64 + st * 16:c * 64 + (st + 1) * 16],
                        xn_st[:, xsb + c * 512:xsb + (c + 1) * 512].rearrange(
                            "p (g k) -> p g k", k=32), axis=AX),
                       waits=([("dg", xp_init), ("dq", ld)] if c == 0 else ()),
                       inc="v")
                xp_done = at("v")
                op("scalar", lambda e: e.activation(
                    out=xpool_bf[:], in_=xpool[:], func=AF.Identity),
                   waits=[("v", xp_done)], inc="s")
                xpb_done = at("s")
                for gc in range(4):
                    bank = PS[2 + gc % 2]
                    for k in range(8):
                        op("tensor", lambda e, gc=gc, k=k, bank=bank: e.matmul(
                            bank[:, 0:64],
                            wgk[:, k * 512 + gc * 128:k * 512 + (gc + 1) * 128],
                            xpool_bf[:, k * 64:(k + 1) * 64],
                            start=(k == 0), stop=(k == 7)),
                           waits=([("s", at("s"))]
                                  if k == 0 else ()),
                           inc=("pe" if k == 7 else None))
                    op("scalar", lambda e, gc=gc, bank=bank: e.activation(
                        out=GK_sb[:, gc * 64:(gc + 1) * 64], in_=bank[:, 0:64],
                        func=AF.Identity, scale=1.0 / 32.0),
                       waits=[("pe", at("pe"))], inc="s")
                bank = PS[4]
                for k in range(8):
                    op("tensor", lambda e, k=k, bank=bank: e.matmul(
                        bank[0:64, :], xpool_bf[:, k * 64:(k + 1) * 64],
                        wgv[:, k * 512:(k + 1) * 512],
                        start=(k == 0), stop=(k == 7)),
                       waits=([("s", at("s"))] if k == 0 else ()),
                       inc=("pe" if k == 7 else None))
                op("scalar", lambda e, bank=bank: e.activation(
                    out=GV_sb[:], in_=bank[0:64, :], func=AF.Identity,
                    scale=1.0 / 32.0),
                   waits=[("pe", at("pe"))], inc="s")
                proj_done_s = at("s")

                for tb in range(4):
                    if not phases:
                        continue
                    loc_ev_at, lat_ev_at = {}, {}
                    tbg = st * 4 + tb
                    tkey = b * 16 + tbg
                    tg = t0 + tb * 128
                    dbf = tbg % 2
                    kbs = [kb for kb in (tbg - 2, tbg - 1, tbg) if kb >= 0]
                    exp_at, pmask_at = {}, {}
                    tile_start_s = at("s")

                    def t1_front(h):
                        par = (h % 2) * 64
                        hc_ = h // 2
                        bS = PS[h % 2]
                        sl = (h % 2) * 384
                        first_w = [("s", exp_at[h - 2] if h >= 2
                                    else tile_start_s)] + (
                            [("dg", at("dg"))] if (dbg and h == 0) else [])
                        for ci, kb in enumerate(kbs):
                            if kb >= st * 4:
                                kloc = kb - st * 4
                                kap = K_bf[par:par + 64,
                                           hc_ * 512 + kloc * 128:
                                           hc_ * 512 + (kloc + 1) * 128]
                            else:
                                kloc = kb - (st * 4 - 2)
                                kap = K_tail[par:par + 64,
                                             hc_ * 256 + kloc * 128:
                                             hc_ * 256 + (kloc + 1) * 128]
                            op("tensor", lambda e, kap=kap, h=h, ci=ci, bS=bS,
                               par=par, hc_=hc_, tb=tb: e.matmul(
                                   bS[:, ci * 128:(ci + 1) * 128], kap,
                                   Q_bf[par:par + 64,
                                        hc_ * 512 + tb * 128:hc_ * 512 + (tb + 1) * 128],
                                   start=True, stop=True),
                               waits=(first_w if ci == 0 else ()), inc=None)
                            first_w = ()
                        op("tensor", lambda e, h=h, par=par, hc_=hc_, tb=tb:
                           e.matmul(
                               PS[4][:, 24 + 2 * h:26 + 2 * h],
                               Q_bf[par:par + 64,
                                    hc_ * 512 + tb * 128:hc_ * 512 + (tb + 1) * 128],
                               glgr[par:par + 64, 2 * h:2 * h + 2],
                               start=True, stop=True), inc="pe")
                        pe_sc = at("pe")
                        op("scalar", lambda e, h=h, bS=bS, sl=sl, n=len(kbs):
                           e.activation(
                               out=P_T[:, sl:sl + n * 128],
                               in_=bS[:, 0:n * 128], func=AF.Exp, scale=0.125),
                           waits=[("pe", pe_sc)], inc="s")
                        exp_at[h] = at("s")
                        if len(kbs) == 3:
                            op("vector", lambda e, sl=sl: e.tensor_mul(
                                P_T[:, sl:sl + 128],
                                P_T[:, sl:sl + 128], maskLs[:]),
                               waits=[("s", at("s"))], inc="v")
                        ci_last = len(kbs) - 1
                        op("vector", lambda e, sl=sl, ci=ci_last: e.tensor_mul(
                            P_T[:, sl + ci * 128:sl + (ci + 1) * 128],
                            P_T[:, sl + ci * 128:sl + (ci + 1) * 128],
                            maskU[:]),
                           waits=[("s", at("s"))], inc="v")
                        pmask_at[h] = at("v")

                    def t1_tail(h):
                        par = (h % 2) * 64
                        hc_ = h // 2
                        bL = PS[2 + (h // 2) % 2]
                        sl = (h % 2) * 384
                        for ci in range(len(kbs)):
                            op("tensor", lambda e, h=h, ci=ci, sl=sl,
                               f=(ci == 0), l=(ci == len(kbs) - 1): e.matmul(
                                   PS[4][:, h:h + 1],
                                   P_T[:, sl + ci * 128:sl + (ci + 1) * 128],
                                   onesb[:], start=f, stop=l),
                               waits=([("v", pmask_at[h])] if ci == 0 else ()),
                               inc=None)
                        for ci, kb in enumerate(kbs):
                            if kb >= st * 4:
                                kloc = kb - st * 4
                                vap = v_tm[:, kloc * 1024 + h * 64:
                                           kloc * 1024 + (h + 1) * 64]
                            else:
                                kloc = kb - (st * 4 - 2)
                                vap = V_tail[:, kloc * 1024 + h * 64:
                                             kloc * 1024 + (h + 1) * 64]
                            op("tensor", lambda e, vap=vap, h=h, ci=ci, bL=bL,
                               par=par, sl=sl, f=(ci == 0),
                               l=(ci == len(kbs) - 1): e.matmul(
                                   bL[par:par + 64, 0:128], vap,
                                   P_T[:, sl + ci * 128:sl + (ci + 1) * 128],
                                   start=f, stop=l),
                               waits=([("s", loc_ev_at[h - 4])]
                                      if (ci == 0 and h >= 4) else ()),
                               inc=("pe" if ci == len(kbs) - 1 else None))
                        op("scalar", lambda e, h=h, bL=bL, par=par, hc_=hc_:
                           e.activation(
                               out=local_sb[par:par + 64, hc_ * 128:(hc_ + 1) * 128],
                               in_=bL[par:par + 64, 0:128], func=AF.Identity),
                           waits=[("pe", at("pe"))], inc="s")
                        loc_ev_at[h] = at("s")

                    if 't1' in phases:
                        for h in range(16):
                            t1_front(h)
                            if h >= 1:
                                t1_tail(h - 1)
                        t1_tail(15)
                    lexp_at, lmask_at = {}, {}

                    def t2_front(lh):
                        par = (lh % 2) * 64
                        lc = lh // 2
                        bS = PS[0 + lh % 2]
                        sl = (lh % 2) * 128
                        op("tensor", lambda e, lh=lh, par=par, lc=lc, bS=bS,
                           tb=tb: e.matmul(
                               bS[0:64, 0:128],
                               GK_sb[par:par + 64, lc * 64:(lc + 1) * 64],
                               GQ_bf[par:par + 64,
                                     lc * 512 + tb * 128:lc * 512 + (tb + 1) * 128],
                               start=True, stop=True),
                           waits=[("v", lat_ev_at[lh - 2] if lh >= 2
                                   else at("v"))],
                           inc="pe")
                        op("scalar", lambda e, lh=lh, bS=bS, sl=sl: e.activation(
                            out=eL_T[:, sl:sl + 128],
                            in_=bS[0:64, 0:128], func=AF.Exp, scale=0.125),
                           waits=[("pe", at("pe"))], inc="s")
                        lexp_at[lh] = at("s")
                        op("vector", lambda e, lh=lh, sl=sl, tg=tg: e.tensor_mul(
                            eL_T[:, sl:sl + 128],
                            eL_T[:, sl:sl + 128],
                            lmask[:, tg:tg + 128]),
                           waits=[("s", at("s"))], inc="v")
                        lmask_at[lh] = at("v")

                    def t2_tail(lh):
                        par = (lh % 2) * 64
                        lc = lh // 2
                        bR = PS[0 + lh % 2]
                        sl = (lh % 2) * 128
                        op("tensor", lambda e, lh=lh, sl=sl: e.matmul(
                            PS[4][:, 16 + lh:17 + lh],
                            eL_T[:, sl:sl + 128], onesb[0:64, :],
                            start=True, stop=True),
                           waits=[("v", lmask_at[lh])], inc=None)
                        op("tensor", lambda e, lh=lh, par=par, sl=sl, bR=bR:
                           e.matmul(
                               bR[par:par + 64, 128:256],
                               GV_sb[:, lh * 64:(lh + 1) * 64],
                               eL_T[:, sl:sl + 128],
                               start=True, stop=True), inc="pe")
                        op("vector", lambda e, lh=lh, par=par, lc=lc, bR=bR:
                           e.tensor_copy(
                               remlat_sb[par:par + 64, lc * 128:(lc + 1) * 128],
                               bR[par:par + 64, 128:256]),
                           waits=[("pe", at("pe"))], inc="v")
                        lat_ev_at[lh] = at("v")

                    if 't2' in phases:
                        for lh in range(8):
                            t2_front(lh)
                            if lh >= 1:
                                t2_tail(lh - 1)
                        t2_tail(7)
                    if 't3' not in phases:
                        continue
                    op("vector", lambda e: e.tensor_copy(fct[:], PS[4][:, 0:56]),
                       waits=[("pe", at("pe")), ("v", at("v")),
                              ("s", at("s"))], inc="v")
                    op("vector", lambda e: e.reciprocal(
                        f_scr[:, 0:24], fct[:, 0:24]), inc="v")
                    rd_done = at("v")
                    op("scalar", lambda e: e.activation(
                        out=tg_sb[:], in_=fct[:, 24:56], func=AF.Tanh,
                        scale=0.5), waits=[("v", rd_done)], inc="s")
                    op("vector", lambda e: e.scalar_tensor_tensor(
                        f_scr[:, 24:40], tg_sb[:, 0:32:2], 1.0, f_scr[:, 0:16],
                        ALU.add, ALU.mult), waits=[("s", at("s"))], inc="v")
                    op("vector", lambda e: e.tensor_scalar(
                        f_scr[:, 40:56], tg_sb[:, 1:32:2], 1.0, None, ALU.add),
                       inc="v")
                    f_done = at("v")
                    op("tensor", lambda e: e.transpose(
                        PS[5][0:16, 0:128], f_scr[:, 24:40], idf_sb[:]),
                       waits=[("v", f_done)] + (
                           [("s", fT_ev_prev)] if fT_ev_prev else []), inc=None)
                    op("tensor", lambda e: e.transpose(
                        PS[5][0:16, 128:256], f_scr[:, 40:56], idf_sb[:]),
                       inc=None)
                    op("tensor", lambda e: e.transpose(
                        PS[5][0:8, 256:384], f_scr[:, 16:24], idf_sb[:]),
                       inc="pe")
                    op("scalar", lambda e: e.activation(
                        out=fT[:], in_=PS[5][0:16, 0:256], func=AF.Identity),
                       waits=[("pe", at("pe"))], inc="s")
                    op("scalar", lambda e: e.activation(
                        out=rlT[:], in_=PS[5][0:8, 256:384], func=AF.Identity),
                       inc="s")
                    fT_ev_prev = at("s")
                    if 't4' not in phases:
                        continue
                    for c4 in range(4):
                        op("tensor", lambda e, c4=c4: e.matmul(
                            PS[2][:, 0:128], lexp[:, c4 * 128:(c4 + 1) * 128],
                            rlT[:], start=True, stop=True),
                           waits=[("s", fT_ev_prev)] if c4 == 0 else
                                 [("v", at("v"))], inc="pe")
                        op("vector", lambda e, c4=c4: e.tensor_mul(
                            remn_sb[:, c4 * 128:(c4 + 1) * 128],
                            remlat_sb[:, c4 * 128:(c4 + 1) * 128],
                            PS[2][:, 0:128]),
                           waits=[("pe", at("pe"))], inc="v")
                    remn_done = at("v")
                    for m in range(8):
                        bR = PS[m % 2]
                        for k in range(4):
                            op("tensor", lambda e, m=m, k=k, bR=bR: e.matmul(
                                bR[:, 0:128],
                                rout[:, k * 1024 + m * 128:k * 1024 + (m + 1) * 128],
                                remn_sb[:, k * 128:(k + 1) * 128],
                                start=(k == 0), stop=(k == 3)),
                               waits=([("v", remn_done if m < 1 else at("v"))]
                                      if k == 0 else ()),
                               inc=("pe" if k == 3 else None))
                        op("tensor", lambda e, m=m: e.matmul(
                            PS[3][:, 0:128], hexp[:, m * 128:(m + 1) * 128],
                            fT[:, 0:128], start=True, stop=True), inc=None)
                        op("tensor", lambda e, m=m: e.matmul(
                            PS[3][:, 128:256], hexp[:, m * 128:(m + 1) * 128],
                            fT[:, 128:256], start=True, stop=True), inc="pe")
                        pe_m = at("pe")
                        op("scalar", lambda e, m=m, bR=bR: e.activation(
                            out=rem_sb[:, (m % 2) * 128:(m % 2 + 1) * 128],
                            in_=bR[:, 0:128], func=AF.Identity),
                           waits=[("pe", pe_m)], inc="s")
                        op("vector", lambda e, m=m: e.tensor_mul(
                            c1_sb[:], local_sb[:, m * 128:(m + 1) * 128],
                            PS[3][:, 0:128]),
                           waits=[("pe", pe_m), ("s", at("s"))], inc="v")
                        op("vector", lambda e, m=m: e.tensor_mul(
                            rem2_sb[:], rem_sb[:, (m % 2) * 128:(m % 2 + 1) * 128],
                            PS[3][:, 128:256]), inc="v")
                        op("vector", lambda e, m=m: e.tensor_add(
                            comb_bf[:, m * 128:(m + 1) * 128], c1_sb[:],
                            rem2_sb[:]), inc="v")
                    comb_done = at("v")
                    if dbg and tkey < 2:
                        for t_, d_ in ((fct, dbg_fct), (f_scr, dbg_fs),
                                       (local_sb, dbg_loc), (remlat_sb, dbg_rl),
                                       (comb_bf, dbg_comb), (P_T, dbg_pt),
                                       (K_bf, dbg_k), (v_tm, dbg_v)):
                            op("gpsimd", lambda e, t_=t_, d_=d_, tk=tkey:
                               e.dma_start(out=d_[tk], in_=t_[:]),
                               waits=[("v", at("v")), ("s", at("s"))],
                               inc="dg", inc_n=16)
                    if 't5' not in phases:
                        continue
                    for m in range(8):
                        bW = PS[4 + m % 2]
                        for k in range(8):
                            op("tensor", lambda e, m=m, k=k, bW=bW: e.matmul(
                                bW[:, 0:128],
                                wout[:, k * 1024 + m * 128:k * 1024 + (m + 1) * 128],
                                comb_bf[:, k * 128:(k + 1) * 128],
                                start=(k == 0), stop=(k == 7)),
                               waits=([("v", comb_done if m < 2 else at("v"))]
                                      if k == 0 else ()),
                               inc=("pe" if k == 7 else None))
                        op("vector", lambda e, m=m, bW=bW, dbf=dbf, tb=tb:
                           e.scalar_tensor_tensor(
                            a1_scr[:, dbf * 1024 + m * 128:dbf * 1024 + (m + 1) * 128],
                            bW[:, 0:128], 0.5,
                            x1_st[:, m * 512 + tb * 128:m * 512 + (tb + 1) * 128],
                            ALU.mult, ALU.add),
                           waits=[("pe", at("pe"))] + (
                               [("dq", max(a1dma_at.get(tkey - 2, 0), x1_ld))]
                               if m == 0 else []),
                           inc="v")
                    a1_done = at("v")
                    for m in range(8):
                        a1dma_at[tkey] = op("sync", lambda e, b=b, m=m, tg=tg,
                                           dbf=dbf: e.dma_start(
                            out=a1_d[b, m * 128:(m + 1) * 128, tg:tg + 128],
                            in_=a1_scr[:, dbf * 1024 + m * 128:dbf * 1024 + (m + 1) * 128]),
                           waits=([("v", a1_done)] if m == 0 else ()),
                           inc="dq", inc_n=16)
        sd.wait("gpsimd", [("dq", at("dq"))])

    if ctx is not None:
        return body
    run_block(nc, sched, body)
    sched.free()
    return nc


def build_ffn1(ctx=None):
    nc = ctx["nc"] if ctx else bass.Bass("TRN2", target_bir_lowering=False)
    dram = _mk_dram(nc, ctx)
    xa = dram("x1", [B, D, S], F32, "ExternalInput")
    xn1 = dram("xn1", [B, D, S], BF16, "ExternalInput")
    w1_d = dram("w1c", [D, 4096], F32, "ExternalInput")
    w2_d = dram("w2c", [4096, D], F32, "ExternalInput")
    b1_d = dram("b1_pc", [128, 32], F32, "ExternalInput")
    b2_d = dram("b2_pc", [128, 8], F32, "ExternalInput")
    out_d = dram("out", [B, D, S], F32, "ExternalOutput")

    sb = ctx["arena"].t if ctx else nc.alloc_sbuf_tensor
    w1 = sb("w1_s", [128, 32768], BF16)
    w2 = sb("w2_s", [128, 32768], BF16)
    b1 = sb("b1_s", [128, 32], F32)
    b2 = sb("b2_s", [128, 8], F32)
    xn_sb = sb("xn_sb", [128, 4096], BF16)
    xa_sb = sb("xa_sb", [128, 4096], F32)
    h_bf = sb("h_bf", [128, 16384], BF16)
    o_scr = sb("o_scr", [128, 1024], F32)
    PS, psT, psTf = _mk_psum(nc, ctx)

    sched = None if ctx else Sched(nc, ["dq", "dg", "pe", "v", "s"])
    TT = 512

    def body(sd):
        op, at = sd.op, sd.at
        for k in range(8):
            op("gpsimd", lambda e, k=k: e.dma_start(
                out=w1[:, k * 4096:(k + 1) * 4096],
                in_=w1_d[k * 128:(k + 1) * 128, :]), inc="dg", inc_n=16)
        for k in range(32):
            op("gpsimd", lambda e, k=k: e.dma_start(
                out=w2[:, k * 1024:(k + 1) * 1024],
                in_=w2_d[k * 128:(k + 1) * 128, :]), inc="dg", inc_n=16)
        op("sync", lambda e: e.dma_start(out=b1[:], in_=b1_d), inc="dq", inc_n=16)
        op("sync", lambda e: e.dma_start(out=b2[:], in_=b2_d), inc="dq", inc_n=16)
        w_g, w_q = at("dg"), at("dq")
        out_dma_at = 0
        for b in range(B):
            for t0 in range(0, S, TT):
                lw = [("v", at("v")), ("s", at("s"))]
                for src_, dst in ((xn1, xn_sb), (xa, xa_sb)):
                    for c in range(8):
                        op("sync", lambda e, s=src_, d=dst, b=b, t0=t0, c=c:
                           e.dma_start(out=d[:, c * TT:(c + 1) * TT],
                                       in_=s[b, c * 128:(c + 1) * 128, t0:t0 + TT]),
                           waits=lw, inc="dq", inc_n=16)
                        lw = ()
                ld = at("dq")
                for fc in range(32):
                    bank = PS[fc % 2]
                    for k in range(8):
                        op("tensor", lambda e, fc=fc, k=k, bank=bank: e.matmul(
                            bank[:],
                            w1[:, k * 4096 + fc * 128:k * 4096 + (fc + 1) * 128],
                            xn_sb[:, k * TT:(k + 1) * TT],
                            start=(k == 0), stop=(k == 7)),
                           waits=([("dq", ld), ("dg", w_g), ("s", at("s"))]
                                  if k == 0 else ()),
                           inc=("pe" if k == 7 else None))
                    op("scalar", lambda e, fc=fc, bank=bank: e.activation(
                        out=h_bf[:, fc * TT:(fc + 1) * TT], in_=bank[:],
                        func=AF.Gelu_apprx_tanh, bias=b1[:, fc:fc + 1]),
                       waits=[("pe", at("pe")), ("dq", w_q)], inc="s")
                h_done = at("s")
                for m in range(8):
                    bank = PS[2 + m % 2]
                    for k in range(32):
                        op("tensor", lambda e, m=m, k=k, bank=bank: e.matmul(
                            bank[:],
                            w2[:, k * 1024 + m * 128:k * 1024 + (m + 1) * 128],
                            h_bf[:, k * TT:(k + 1) * TT],
                            start=(k == 0), stop=(k == 31)),
                           waits=([("s", h_done), ("v", at("v"))] if k == 0 else ()),
                           inc=("pe" if k == 31 else None))
                    op("vector", lambda e, m=m, bank=bank: e.scalar_tensor_tensor(
                        o_scr[:, (m % 2) * TT:(m % 2 + 1) * TT], bank[:],
                        b2[:, m:m + 1], xa_sb[:, m * TT:(m + 1) * TT],
                        ALU.add, ALU.add),
                       waits=[("pe", at("pe"))] + (
                           [("dq", out_dma_at)] if m < 2 else []), inc="v")
                    out_dma_at = op("sync", lambda e, b=b, t0=t0, m=m: e.dma_start(
                        out=out_d[b, m * 128:(m + 1) * 128, t0:t0 + TT],
                        in_=o_scr[:, (m % 2) * TT:(m % 2 + 1) * TT]),
                       waits=[("v", at("v"))], inc="dq", inc_n=16)
        sd.wait("gpsimd", [("dq", at("dq"))])

    if ctx is not None:
        return body
    run_block(nc, sched, body)
    sched.free()
    return nc



_NC_CACHE = {}


def _get_nc(name, builder):
    if name not in _NC_CACHE:
        _NC_CACHE[name] = builder()
    return _NC_CACHE[name]


def _fm(a):
    return np.ascontiguousarray(np.asarray(a).transpose(0, 2, 1))


def _pc(v, cols):
    return np.ascontiguousarray(np.asarray(v, np.float32).reshape(cols, 128).T)


def kernel_unfused(**inputs):
    hc = host_consts()
    inp = {k: np.asarray(v) for k, v in inputs.items()}
    x = inp["L0_attn_x"]

    glw = inp["L1_attn_gate_local_w"]
    grw = inp["L1_attn_gate_remote_w"]
    glgr = np.empty((64, 32), np.float32)
    glgr[:, 0::2] = glw.T
    glgr[:, 1::2] = grw.T
    glgr_dup = np.ascontiguousarray(np.concatenate([glgr, glgr], axis=0))

    cores = list(range(NCORES))
    x_fm = [_fm(x[2 * c:2 * c + 2]) for c in cores]

    nc = _get_nc("prep", build_prep)
    maps = [{
        "x": x_fm[c],
        "wfr": inp["L0_ffn_w_freq_real"], "wfi": inp["L0_ffn_w_freq_imag"],
        "Fr": hc["Fr"], "Fi": hc["Fi"], "FiN": hc["FiN"],
        "Gr": hc["Gr"], "G2": hc["G2"],
        "ident_bf": hc["ident_bf"], "ident_f32": hc["ident_f32"],
        "sel8": hc["sel8"], "inv1024": hc["inv1024"],
    } for c in cores]
    r1 = run_bass_kernel_spmd(nc, maps, core_ids=cores).results

    nc = _get_nc("attn0", build_attn0)
    nw_pc = _pc(inp["L0_attn_norm_w"], 8)
    maps = [{
        "xn": r1[c]["xn"],
        "wq": inp["L0_attn_wq"], "wk": inp["L0_attn_wk"],
        "wv": inp["L0_attn_wv"], "wo": inp["L0_attn_wo"],
        "wg1": inp["L0_attn_wg1"], "wg2": inp["L0_attn_wg2"],
        "norm_w_pc": nw_pc,
        "cfm": hc["cfm"], "sfm": hc["sfm"], "ctm": hc["ctm"], "stm": hc["stm"],
        "maskU": hc["maskU"], "blk16": hc["blk16"], "hexp": hc["hexp"],
    } for c in cores]
    r2 = run_bass_kernel_spmd(nc, maps, core_ids=cores).results

    nc = _get_nc("mid", build_mid)
    bias_pc = _pc(inp["L0_ffn_bias"], 8)
    maps = [{
        "x": x_fm[c], "a0": r2[c]["a0"], "xn0": r1[c]["xn"], "M": r1[c]["M"],
        "bias_pc": bias_pc,
        "ident_f32": hc["ident_f32"], "sel8": hc["sel8"],
        "inv1024": hc["inv1024"],
    } for c in cores]
    r3 = run_bass_kernel_spmd(nc, maps, core_ids=cores).results

    nc = _get_nc("attn1", build_attn1)
    maps = [{
        "xn1": r3[c]["xn1"], "x1": r3[c]["x1"],
        "wq": inp["L1_attn_wq"], "wk": inp["L1_attn_wk"],
        "wv": inp["L1_attn_wv"], "wgq": inp["L1_attn_wgq"],
        "wgk": inp["L1_attn_wgk"], "wgv": inp["L1_attn_wgv"],
        "r_out_w": inp["L1_attn_r_out_w"], "w_out": inp["L1_attn_w_out"],
        "glgr": glgr_dup,
        "maskU": hc["maskU"], "maskLs": hc["maskLs"], "lmask": hc["lmask"],
        "hexp": hc["hexp"], "lexp": hc["lexp"], "ident_f32": hc["ident_f32"],
        "ones_bf": hc["ones_bf"],
    } for c in cores]
    r4 = run_bass_kernel_spmd(nc, maps, core_ids=cores).results

    nc = _get_nc("ffn1", build_ffn1)
    maps = [{
        "x1": r4[c]["a1"], "xn1": r3[c]["xn1"],
        "w1c": inp["L1_ffn_w1"][c], "w2c": inp["L1_ffn_w2"][c],
        "b1_pc": _pc(inp["L1_ffn_b1"][c], 32),
        "b2_pc": _pc(inp["L1_ffn_b2"][c], 8),
    } for c in cores]
    r5 = run_bass_kernel_spmd(nc, maps, core_ids=cores).results

    out = np.empty((16, 2048, 1024), np.float32)
    for c in cores:
        out[2 * c:2 * c + 2] = r5[c]["out"].transpose(0, 2, 1)
    return out


def build_fused(nphases=5):
    nc = bass.Bass("TRN2", target_bir_lowering=False)

    def ein(name, shape, dt):
        return nc.dram_tensor(name, shape, dt, kind="ExternalInput").ap()

    def internal(name, shape, dt):
        return nc.dram_tensor(name, shape, dt, kind="Internal").ap()

    E = {}
    E["x"] = ein("x", [B, D, S], F32)
    for n in ("wfr", "wfi"):
        E[n] = ein(n, [D, D], F32)
    for n in ("Fr", "Fi", "FiN", "Gr", "G2"):
        E[n] = ein(n, [D, D], BF16)
    E["ident_bf"] = ein("ident_bf", [128, 128], BF16)
    E["ident_f32"] = ein("ident_f32", [128, 128], F32)
    E["sel8"] = ein("sel8", [8, 1024], F32)
    E["inv1024"] = ein("inv1024", [128, 1], F32)
    for n in ("l0_wq", "l0_wk", "l0_wv", "l0_wo"):
        E[n] = ein(n, [D, D], F32)
    E["wg1"] = ein("wg1", [D, 64], F32)
    E["wg2"] = ein("wg2", [64, D], F32)
    E["norm_w_pc"] = ein("norm_w_pc", [128, 8], F32)
    E["cfm"] = ein("cfm", [D, S], BF16)
    E["sfm"] = ein("sfm", [D, S], BF16)
    E["ctm"] = ein("ctm", [S, D], BF16)
    E["stm"] = ein("stm", [S, D], BF16)
    E["maskU"] = ein("maskU", [128, 128], BF16)
    E["maskLs"] = ein("maskLs", [128, 128], BF16)
    E["blk16"] = ein("blk16", [8 * 128, 16], F32)
    E["hexp"] = ein("hexp", [16, 8 * 128], BF16)
    E["lexp"] = ein("lexp", [8, 4 * 128], BF16)
    E["bias_pc"] = ein("bias_pc", [128, 8], F32)
    for n in ("l1_wq", "l1_wk", "l1_wv", "w_out"):
        E[n] = ein(n, [D, D], F32)
    for n in ("wgq", "wgk", "wgv"):
        E[n] = ein(n, [D, 512], F32)
    E["r_out_w"] = ein("r_out_w", [512, D], F32)
    E["glgr"] = ein("glgr", [128, 32], F32)
    E["lmask"] = ein("lmask", [64, S], BF16)
    E["ones_bf"] = ein("ones_bf", [128, 1], BF16)
    E["w1c"] = ein("w1c", [D, 4096], F32)
    E["w2c"] = ein("w2c", [4096, D], F32)
    E["b1_pc"] = ein("b1_pc", [128, 32], F32)
    E["b2_pc"] = ein("b2_pc", [128, 8], F32)

    xn0_i = internal("i_xn0", [B, D, S], BF16)
    M_i = internal("i_M", [D, D], BF16)
    a0_i = internal("i_a0", [B, D, S], F32)
    x1_i = internal("i_x1", [B, D, S], F32)
    xn1_i = internal("i_xn1", [B, D, S], BF16)
    a1_i = internal("i_a1", [B, D, S], F32)
    out_e = nc.dram_tensor("out", [B, D, S], F32, kind="ExternalOutput").ap()

    PS, psT, psTf = _mk_psum(nc, None)
    cpool = setup_const_pool(nc, [1e-5, -0.6931472, 1e-6])
    ar = Arena(nc, nc.sbuf_base, nc.sbuf_top)
    ctx = {"nc": nc, "PS": PS, "psT": psT, "psTf": psTf, "cpool": cpool,
           "arena": ar}

    emits = []
    ar.reset("p1_")
    ctx["D"] = {"x": E["x"], "wfr": E["wfr"], "wfi": E["wfi"],
                "Fr": E["Fr"], "Fi": E["Fi"], "FiN": E["FiN"],
                "Gr": E["Gr"], "G2": E["G2"], "ident_bf": E["ident_bf"],
                "ident_f32": E["ident_f32"], "sel8": E["sel8"],
                "inv1024": E["inv1024"], "M": M_i, "xn": xn0_i}
    emits.append(build_prep(ctx=ctx))
    ar.reset("p2_")
    ctx["D"] = {"xn": xn0_i, "wq": E["l0_wq"], "wk": E["l0_wk"],
                "wv": E["l0_wv"], "wo": E["l0_wo"], "wg1": E["wg1"],
                "wg2": E["wg2"], "norm_w_pc": E["norm_w_pc"],
                "cfm": E["cfm"], "sfm": E["sfm"], "ctm": E["ctm"],
                "stm": E["stm"], "maskU": E["maskU"], "blk16": E["blk16"],
                "hexp": E["hexp"], "a0": a0_i}
    emits.append(build_attn0(ctx=ctx))
    ar.reset("p3_")
    ctx["D"] = {"x": E["x"], "a0": a0_i, "xn0": xn0_i, "M": M_i,
                "bias_pc": E["bias_pc"], "ident_f32": E["ident_f32"],
                "sel8": E["sel8"], "inv1024": E["inv1024"],
                "x1": x1_i, "xn1": xn1_i}
    emits.append(build_mid(ctx=ctx))
    ar.reset("p4_")
    ctx["D"] = {"xn1": xn1_i, "x1": x1_i, "wq": E["l1_wq"], "wk": E["l1_wk"],
                "wv": E["l1_wv"], "wgq": E["wgq"], "wgk": E["wgk"],
                "wgv": E["wgv"], "r_out_w": E["r_out_w"],
                "w_out": E["w_out"], "glgr": E["glgr"], "maskU": E["maskU"],
                "maskLs": E["maskLs"], "lmask": E["lmask"], "hexp": E["hexp"],
                "lexp": E["lexp"], "ident_f32": E["ident_f32"],
                "ones_bf": E["ones_bf"], "a1": a1_i}
    emits.append(build_attn1(ctx=ctx))
    ar.reset("p5_")
    ctx["D"] = {"x1": a1_i, "xn1": xn1_i, "w1c": E["w1c"],
                "w2c": E["w2c"], "b1_pc": E["b1_pc"], "b2_pc": E["b2_pc"],
                "out": out_e}
    emits.append(build_ffn1(ctx=ctx))

    emits = emits[:nphases]

    def mega(sd):
        for em in emits:
            em(sd)
            barrier(sd)

    sched = Sched(nc, ["dq", "dg", "pe", "v", "s"])
    run_block(nc, sched, mega)
    sched.free()
    return nc


def kernel(**inputs):
    hc = host_consts()
    inp = {k: np.asarray(v) for k, v in inputs.items()}
    x = inp["L0_attn_x"]

    glw = inp["L1_attn_gate_local_w"]
    grw = inp["L1_attn_gate_remote_w"]
    glgr = np.empty((64, 32), np.float32)
    glgr[:, 0::2] = glw.T
    glgr[:, 1::2] = grw.T
    glgr_dup = np.ascontiguousarray(np.concatenate([glgr, glgr], axis=0))

    cores = list(range(NCORES))
    nc = _get_nc("fused", build_fused)
    shared = {
        "wfr": inp["L0_ffn_w_freq_real"], "wfi": inp["L0_ffn_w_freq_imag"],
        "Fr": hc["Fr"], "Fi": hc["Fi"], "FiN": hc["FiN"],
        "Gr": hc["Gr"], "G2": hc["G2"],
        "ident_bf": hc["ident_bf"], "ident_f32": hc["ident_f32"],
        "sel8": hc["sel8"], "inv1024": hc["inv1024"],
        "l0_wq": inp["L0_attn_wq"], "l0_wk": inp["L0_attn_wk"],
        "l0_wv": inp["L0_attn_wv"], "l0_wo": inp["L0_attn_wo"],
        "wg1": inp["L0_attn_wg1"], "wg2": inp["L0_attn_wg2"],
        "norm_w_pc": _pc(inp["L0_attn_norm_w"], 8),
        "cfm": hc["cfm"], "sfm": hc["sfm"], "ctm": hc["ctm"], "stm": hc["stm"],
        "maskU": hc["maskU"], "maskLs": hc["maskLs"], "blk16": hc["blk16"],
        "hexp": hc["hexp"], "lexp": hc["lexp"],
        "bias_pc": _pc(inp["L0_ffn_bias"], 8),
        "l1_wq": inp["L1_attn_wq"], "l1_wk": inp["L1_attn_wk"],
        "l1_wv": inp["L1_attn_wv"], "w_out": inp["L1_attn_w_out"],
        "wgq": inp["L1_attn_wgq"], "wgk": inp["L1_attn_wgk"],
        "wgv": inp["L1_attn_wgv"], "r_out_w": inp["L1_attn_r_out_w"],
        "glgr": glgr_dup, "lmask": hc["lmask"], "ones_bf": hc["ones_bf"],
    }
    maps = []
    for c in cores:
        m = dict(shared)
        m["x"] = _fm(x[2 * c:2 * c + 2])
        m["w1c"] = inp["L1_ffn_w1"][c]
        m["w2c"] = inp["L1_ffn_w2"][c]
        m["b1_pc"] = _pc(inp["L1_ffn_b1"][c], 32)
        m["b2_pc"] = _pc(inp["L1_ffn_b2"][c], 8)
        maps.append(m)
    res = run_bass_kernel_spmd(nc, maps, core_ids=cores).results
    out = np.empty((16, 2048, 1024), np.float32)
    for c in cores:
        out[2 * c:2 * c + 2] = res[c]["out"].transpose(0, 2, 1)
    return out


# version 10
# speedup vs baseline: 1.4185x; 1.0339x over previous
import numpy as np
import ml_dtypes
import concourse.bass as bass
import concourse.mybir as mybir
from concourse.bass_utils import run_bass_kernel_spmd

F32 = mybir.dt.float32
BF16 = mybir.dt.bfloat16
AF = mybir.ActivationFunctionType
ALU = mybir.AluOpType
AX = mybir.AxisListType.X
BFNP = ml_dtypes.bfloat16

NCORES = 8
B = 2
S = 2048
D = 1024
H, HD = 16, 64
NT = S // 128


_CONST_CACHE = {}


def host_consts():
    if _CONST_CACHE:
        return _CONST_CACHE
    c = {}
    j = np.arange(D)
    Fc = np.exp(-2j * np.pi * np.outer(j, j) / D)
    c["Fr"] = Fc.real.astype(np.float32).astype(BFNP)
    c["Fi"] = Fc.imag.astype(np.float32).astype(BFNP)
    c["FiN"] = (-Fc.imag).astype(np.float32).astype(BFNP)
    c["Gr"] = (Fc.real / D).astype(np.float32).astype(BFNP)
    c["G2"] = (Fc.imag / D).astype(np.float32).astype(BFNP)
    c["ident_bf"] = np.eye(128, dtype=np.float32).astype(BFNP)
    c["ident_f32"] = np.eye(128, dtype=np.float32)
    theta = 10000.0 ** (-2.0 / D * np.arange(D, dtype=np.float64))
    ang = np.arange(S, dtype=np.float64)[:, None] * theta[None]
    c["cfm"] = np.ascontiguousarray(np.cos(ang).T).astype(np.float32).astype(BFNP)
    c["sfm"] = np.ascontiguousarray(np.sin(ang).T).astype(np.float32).astype(BFNP)
    c["ctm"] = np.cos(ang).astype(np.float32).astype(BFNP)
    c["stm"] = np.sin(ang).astype(np.float32).astype(BFNP)
    i_ = np.arange(128)
    c["maskU"] = (i_[:, None] <= i_[None, :]).astype(np.float32).astype(BFNP)
    c["maskLs"] = (i_[:, None] > i_[None, :]).astype(np.float32).astype(BFNP)
    c["lmask"] = (np.arange(64)[:, None] <= (np.arange(S) // 32)[None, :]
                  ).astype(np.float32).astype(BFNP)
    sel8 = np.zeros((8, 1024), np.float32)
    for jj in range(8):
        sel8[jj, jj * 128:(jj + 1) * 128] = 1.0
    c["sel8"] = sel8
    blk = np.zeros((8, 128, 16), np.float32)
    for cc in range(8):
        blk[cc, 0:64, 2 * cc] = 1.0
        blk[cc, 64:128, 2 * cc + 1] = 1.0
    c["blk16"] = blk.reshape(8 * 128, 16)
    hx = np.zeros((8, 16, 128), np.float32)
    for cc in range(8):
        hx[cc, 2 * cc, 0:64] = 1.0
        hx[cc, 2 * cc + 1, 64:128] = 1.0
    c["hexp"] = np.ascontiguousarray(hx.transpose(1, 0, 2)).reshape(16, 8 * 128
                                                                    ).astype(BFNP)
    lx = np.zeros((4, 8, 128), np.float32)
    for cc in range(4):
        lx[cc, 2 * cc, 0:64] = 1.0
        lx[cc, 2 * cc + 1, 64:128] = 1.0
    c["lexp"] = np.ascontiguousarray(lx.transpose(1, 0, 2)).reshape(8, 4 * 128
                                                                    ).astype(BFNP)
    c["inv1024"] = np.full((128, 1), 1.0 / 1024.0, np.float32)
    c["ones_bf"] = np.ones((128, 1), np.float32).astype(BFNP)
    _CONST_CACHE.update(c)
    return c


class Sched:
    def __init__(self, nc, sem_names):
        self.nc = nc
        self.sems = {n: nc.alloc_semaphore(n) for n in sem_names}
        self.cnt = None
        self.E = None
        self.eng = None

    def free(self):
        for h in self.sems.values():
            self.nc.release_semaphore(h)

    def start(self, engine_name, eng):
        self.E = engine_name
        self.eng = eng
        self.cnt = {n: 0 for n in self.sems}

    def op(self, e, fn, waits=(), inc=None, inc_n=1):
        if e == self.E:
            for s, v in waits:
                if v > 0:
                    self.eng.wait_ge(self.sems[s], v)
            instr = fn(self.eng)
            if inc is not None and instr is not None:
                instr.then_inc(self.sems[inc], inc_n)
        if inc is not None:
            self.cnt[inc] += inc_n
        return self.cnt[inc] if inc is not None else None

    def wait(self, e, waits):
        if e == self.E:
            for s, v in waits:
                if v > 0:
                    self.eng.wait_ge(self.sems[s], v)

    def at(self, sem):
        return self.cnt[sem]


def setup_const_pool(nc, values):
    vals = sorted(set(float(v) for v in values))
    pool = nc.alloc_sbuf_tensor("constpool", [128, max(1, len(vals))], F32)
    for i, v in enumerate(vals):
        nc.const_aps.register(F32, v, pool[:, i:i + 1])
    return pool, vals


def emit_const_pool(sd, pool, vals, inc="dg"):
    for i, v in enumerate(vals):
        sd.op("gpsimd", lambda e, i=i, v=v: e.memset(pool[:, i:i + 1], v),
              inc=inc)
    return sd.at(inc)


def run_block(nc, sched, body):
    with nc.Block() as block:
        @block.sync
        def _(eng):
            sched.start("sync", eng)
            body(sched)

        @block.tensor
        def _(eng):
            sched.start("tensor", eng)
            body(sched)

        @block.vector
        def _(eng):
            sched.start("vector", eng)
            body(sched)

        @block.scalar
        def _(eng):
            sched.start("scalar", eng)
            body(sched)

        @block.gpsimd
        def _(eng):
            sched.start("gpsimd", eng)
            body(sched)



class Arena:

    def __init__(self, nc, base, limit):
        base = (base + 31) // 32 * 32
        self.nc, self.base, self.limit = nc, base, limit
        self.off = base
        self.prefix = ""
        self.peak = base

    def reset(self, prefix):
        self.off = self.base
        self.prefix = prefix

    def t(self, name, shape, dtype):
        esz = {F32: 4, BF16: 2}[dtype]
        fbytes = int(np.prod(shape[1:])) * esz
        fbytes = (fbytes + 31) // 32 * 32
        t = self.nc.alloc_sbuf_tensor_at(self.prefix + name, list(shape), dtype,
                                         offset=self.off)
        self.off += fbytes
        self.peak = max(self.peak, self.off)
        assert self.off <= self.limit, (self.prefix, name, self.off, self.limit)
        return t


def _mk_dram(nc, ctx):
    def dram(name, shape, dtype, kind):
        if ctx is not None:
            return ctx["D"][name]
        return nc.dram_tensor(name, shape, dtype, kind=kind).ap()
    return dram


def _mk_psum(nc, ctx):
    if ctx is not None:
        return ctx["PS"], ctx["psT"], ctx["psTf"]
    PS = [nc.alloc_psum_tensor(f"bank{i}", [128, 512], F32) for i in range(6)]
    psT = nc.alloc_psum_tensor("psT", [128, 128], BF16)
    psTf = nc.alloc_psum_tensor("psTf", [128, 128], F32)
    return PS, psT, psTf


def barrier(sd):
    vals = [(s, sd.at(s)) for s in sd.sems]
    for e in ("sync", "tensor", "vector", "scalar", "gpsimd"):
        sd.wait(e, vals)


def build_prep(ctx=None):
    nc = ctx["nc"] if ctx else bass.Bass("TRN2", target_bir_lowering=False)
    dram = _mk_dram(nc, ctx)
    x = dram("x", [B, D, S], F32, "ExternalInput")
    wfr = dram("wfr", [D, D], F32, "ExternalInput")
    wfi = dram("wfi", [D, D], F32, "ExternalInput")
    cF = {}
    for n in ("Fr", "Fi", "FiN", "Gr", "G2"):
        cF[n] = dram(n, [D, D], BF16, "ExternalInput")
    idb = dram("ident_bf", [128, 128], BF16, "ExternalInput")
    idf = dram("ident_f32", [128, 128], F32, "ExternalInput")
    sel8 = dram("sel8", [8, 1024], F32, "ExternalInput")
    inv1024 = dram("inv1024", [128, 1], F32, "ExternalInput")
    M_out = dram("M", [D, D], BF16, "ExternalOutput")
    xn_out = dram("xn", [B, D, S], BF16, "ExternalOutput")

    sb = ctx["arena"].t if ctx else nc.alloc_sbuf_tensor
    cFr = sb("cFr", [128, 8192], BF16)
    cFi = sb("cFi", [128, 8192], BF16)
    cFiN = sb("cFiN", [128, 8192], BF16)
    cGr = sb("cGr", [128, 8192], BF16)
    cG2 = sb("cG2", [128, 8192], BF16)
    Wr = sb("Wr", [128, 8192], BF16)
    Wi = sb("Wi", [128, 8192], BF16)
    Qr = sb("Qr", [128, 8192], BF16)
    Qi = sb("Qi", [128, 8192], BF16)
    idb_sb = sb("idb_sb", [128, 128], BF16)
    idf_sb = sb("idf_sb", [128, 128], F32)
    sel8_sb = sb("sel8_sb", [8, 1024], F32)
    i1024_sb = sb("i1024_sb", [128, 1], F32)
    x_sb = sb("x_sb", [128, 4096], F32)
    sq_sb = sb("sq_sb", [128, 4096], F32)
    xn_sb = sb("xn_sb", [128, 4096], BF16)
    stats_sb = sb("stats_sb", [128, 8], F32)
    var_sb = sb("var_sb", [128, 4], F32)
    msq2_sb = sb("msq2_sb", [128, 4], F32)
    ln_sb = sb("ln_sb", [128, 4], F32)
    fct_sb = sb("fct_sb", [128, 8], F32)
    rowT_sb = sb("rowT_sb", [8, 128], F32)
    PS, psT, psTf = _mk_psum(nc, ctx)
    psA, psB = PS[0], PS[1]
    psStat = PS[2]
    psBC0, psBC1 = PS[3], PS[4]
    cpool, cvals = ctx["cpool"] if ctx else setup_const_pool(nc, [1e-5])
    if True:
        sched = None if ctx else Sched(nc, ["dq", "dg", "pe", "v", "s"])

        def body(sd):
            op, at = sd.op, sd.at
            cpool_done = emit_const_pool(sd, cpool, cvals)
            for name, t in (("Fr", cFr), ("Fi", cFi), ("FiN", cFiN),
                            ("Gr", cGr), ("G2", cG2)):
                for k in range(8):
                    op("sync", lambda e, n=name, t=t, k=k: e.dma_start(
                        out=t[:, k * 1024:(k + 1) * 1024],
                        in_=cF[n][k * 128:(k + 1) * 128, :]), inc="dq", inc_n=16)
            op("sync", lambda e: e.dma_start(out=idb_sb[:], in_=idb), inc="dq", inc_n=16)
            op("sync", lambda e: e.dma_start(out=idf_sb[:], in_=idf), inc="dq", inc_n=16)
            op("sync", lambda e: e.dma_start(out=sel8_sb[:], in_=sel8), inc="dq", inc_n=16)
            op("sync", lambda e: e.dma_start(out=i1024_sb[:], in_=inv1024), inc="dq", inc_n=16)
            for src, t in ((wfr, Wr), (wfi, Wi)):
                for k in range(8):
                    op("gpsimd", lambda e, s=src, t=t, k=k: e.dma_start(
                        out=t[:, k * 1024:(k + 1) * 1024],
                        in_=s[k * 128:(k + 1) * 128, :]), inc="dg", inc_n=16)
            consts_done = at("dq")
            w_done = at("dg")

            ln_state = {}

            def ln_tile(b, t0):
                for cch in range(8):
                    op("sync", lambda e, b=b, t0=t0, c=cch: e.dma_start(
                        out=x_sb[:, c * 512:(c + 1) * 512],
                        in_=x[b, c * 128:(c + 1) * 128, t0:t0 + 512]),
                       waits=([("v", at("v"))] if cch == 0 else ()),
                       inc="dq", inc_n=16)
                ld = at("dq")
                op("scalar", lambda e: e.activation(
                    out=sq_sb[:], in_=x_sb[:], func=AF.Square),
                   waits=[("dq", ld), ("v", at("v"))], inc="s")
                sq_ready = at("s")
                for jj in range(4):
                    for src, col in ((x_sb, 2 * jj), (sq_sb, 2 * jj + 1)):
                        for cch in range(8):
                            last = cch == 7
                            op("tensor", lambda e, src=src, col=col, jj=jj,
                               c=cch, f=(cch == 0), l=last: e.matmul(
                                   psStat[:, col:col + 1],
                                   src[:, c * 512 + jj * 128:c * 512 + jj * 128 + 128],
                                   i1024_sb[:], start=f, stop=l),
                               waits=([("s", sq_ready), ("dq", ld),
                                       ("v", at("v"))]
                                      if (col == 0 and cch == 0) else ()),
                               inc=("pe" if last else None))
                op("vector", lambda e: e.tensor_copy(stats_sb[:], psStat[:, 0:8]),
                   waits=[("pe", at("pe"))], inc="v")
                op("scalar", lambda e: e.activation(
                    out=msq2_sb[:], in_=stats_sb[:, 0:8:2], func=AF.Square),
                   waits=[("v", at("v"))], inc="s")
                op("vector", lambda e: e.tensor_sub(
                    var_sb[:], stats_sb[:, 1:8:2], msq2_sb[:]),
                   waits=[("s", at("s"))], inc="v")
                op("scalar", lambda e: e.activation(
                    out=ln_sb[:], in_=var_sb[:], func=AF.Ln, bias=1e-5),
                   waits=[("v", at("v")), ("dg", cpool_done)], inc="s")
                op("scalar", lambda e: e.activation(
                    out=fct_sb[:, 4:8], in_=ln_sb[:], func=AF.Exp,
                    scale=-0.5), inc="s")
                op("vector", lambda e: e.tensor_copy(
                    fct_sb[:, 0:4], stats_sb[:, 0:8:2]), inc="v")
                op("tensor", lambda e: e.transpose(
                    psTf[0:8, :], fct_sb[:, 0:8], idf_sb[:]),
                   waits=[("s", at("s")), ("v", at("v"))], inc="pe")
                op("vector", lambda e: e.tensor_copy(rowT_sb[:], psTf[0:8, :]),
                   waits=[("pe", at("pe"))], inc="v")
                rowv = at("v")
                for jj in range(4):
                    for w in range(2):
                        op("tensor", lambda e, jj=jj, w=w: e.matmul(
                            (psBC0 if w == 0 else psBC1)[:, jj * 128:(jj + 1) * 128],
                            sel8_sb[:, (jj + 4 * w) * 128:(jj + 4 * w + 1) * 128],
                            rowT_sb[:], start=True, stop=True),
                           waits=([("v", rowv)] if (jj == 0 and w == 0) else ()),
                           inc="pe")
                bc = at("pe")
                for cch in range(8):
                    op("vector", lambda e, c=cch: e.scalar_tensor_tensor(
                        sq_sb[:, c * 512:(c + 1) * 512], psBC0[:], -1.0,
                        x_sb[:, c * 512:(c + 1) * 512], ALU.mult, ALU.add),
                       waits=([("pe", bc)] if cch == 0 else ()), inc="v")
                    op("vector", lambda e, c=cch: e.tensor_mul(
                        xn_sb[:, c * 512:(c + 1) * 512],
                        sq_sb[:, c * 512:(c + 1) * 512],
                        psBC1[:]), inc="v")
                xn_ready = at("v")
                for cch in range(8):
                    op("sync", lambda e, b=b, t0=t0, c=cch: e.dma_start(
                        out=xn_out[b, c * 128:(c + 1) * 128, t0:t0 + 512],
                        in_=xn_sb[:, c * 512:(c + 1) * 512]),
                       waits=([("v", xn_ready)] if cch == 0 else ()),
                       inc="dq", inc_n=16)

            ln_iter = iter([(b, t0) for b in range(B)
                            for t0 in range(0, S, 512)])

            def maybe_ln():
                try:
                    b, t0 = next(ln_iter)
                except StopIteration:
                    return
                ln_tile(b, t0)

            evac_at = {0: 0, 1: 0}
            for dst, terms in ((Qr, ((cFr, Wr), (cFiN, Wi))),
                               (Qi, ((cFr, Wi), (cFi, Wr)))):
                for m in range(8):
                    for half in range(2):
                        bk = (m + half) % 2
                        ps = psA if bk == 0 else psB
                        nmm = 0
                        for k in range(8):
                            for A, Bm in terms:
                                first = nmm == 0
                                last = nmm == 15
                                op("tensor", lambda e, ps=ps, A=A, Bm=Bm, k=k, m=m,
                                   h=half, f=first, l=last: e.matmul(
                                       ps[:],
                                       A[:, k * 1024 + m * 128:k * 1024 + (m + 1) * 128],
                                       Bm[:, k * 1024 + h * 512:k * 1024 + h * 512 + 512],
                                       start=f, stop=l),
                                   waits=([("dq", consts_done), ("dg", w_done),
                                           ("s", evac_at[bk])]
                                          if nmm == 0 else ()),
                                   inc=("pe" if last else None))
                                nmm += 1
                        op("scalar", lambda e, ps=ps, dst=dst, m=m, h=half: e.copy(
                            out=dst[:, m * 1024 + h * 512:m * 1024 + h * 512 + 512],
                            in_=ps[:]),
                           waits=[("pe", at("pe"))], inc="s")
                        evac_at[bk] = at("s")
                        if half == 1:
                            maybe_ln()
            t_ev = 0
            for src, dst in ((Qr, Wr), (Qi, Wi)):
                for r in range(8):
                    for cc in range(8):
                        op("tensor", lambda e, src=src, r=r, cc=cc: e.transpose(
                            psT[:], src[:, cc * 1024 + r * 128:cc * 1024 + (r + 1) * 128],
                            idb_sb[:]),
                           waits=[("s", t_ev)], inc="pe")
                        op("scalar", lambda e, dst=dst, r=r, cc=cc: e.copy(
                            out=dst[:, r * 1024 + cc * 128:r * 1024 + (cc + 1) * 128],
                            in_=psT[:]),
                           waits=[("pe", at("pe"))], inc="s")
                        t_ev = at("s")
            for m in range(8):
                for half in range(2):
                    bk = (m + half) % 2
                    ps = psA if bk == 0 else psB
                    nmm = 0
                    for k in range(8):
                        for A, Bm in ((cGr, Wr), (cG2, Wi)):
                            first = nmm == 0
                            last = nmm == 15
                            op("tensor", lambda e, ps=ps, A=A, Bm=Bm, k=k, m=m,
                               h=half, f=first, l=last: e.matmul(
                                   ps[:],
                                   A[:, k * 1024 + m * 128:k * 1024 + (m + 1) * 128],
                                   Bm[:, k * 1024 + h * 512:k * 1024 + h * 512 + 512],
                                   start=f, stop=l),
                               waits=([("s", max(evac_at[bk], t_ev))]
                                      if nmm == 0 else ()),
                               inc=("pe" if last else None))
                            nmm += 1
                    op("scalar", lambda e, ps=ps, m=m, h=half: e.copy(
                        out=Qr[:, m * 1024 + h * 512:m * 1024 + h * 512 + 512],
                        in_=ps[:]),
                       waits=[("pe", at("pe"))], inc="s")
                    evac_at[bk] = at("s")
                    if half == 1:
                        maybe_ln()
            for r in range(8):
                for cc in range(8):
                    op("tensor", lambda e, r=r, cc=cc: e.transpose(
                        psT[:], Qr[:, cc * 1024 + r * 128:cc * 1024 + (r + 1) * 128],
                        idb_sb[:]),
                       waits=[("s", t_ev)], inc="pe")
                    op("scalar", lambda e, r=r, cc=cc: e.copy(
                        out=Qi[:, r * 1024 + cc * 128:r * 1024 + (cc + 1) * 128],
                        in_=psT[:]),
                       waits=[("pe", at("pe"))], inc="s")
                    t_ev = at("s")
            for r in range(8):
                op("sync", lambda e, r=r: e.dma_start(
                    out=M_out[r * 128:(r + 1) * 128, :],
                    in_=Qi[:, r * 1024:(r + 1) * 1024]),
                   waits=[("s", at("s"))], inc="dq", inc_n=16)
            while True:
                try:
                    b, t0 = next(ln_iter)
                except StopIteration:
                    break
                ln_tile(b, t0)
            sd.wait("gpsimd", [("dq", at("dq"))])

        if ctx is not None:
            return body
        run_block(nc, sched, body)
        sched.free()
    return nc


def build_attn0(dbg=False, ctx=None):
    nc = ctx["nc"] if ctx else bass.Bass("TRN2", target_bir_lowering=False)
    dram = _mk_dram(nc, ctx)
    xn = dram("xn", [B, D, S], BF16, "ExternalInput")
    wq_d = dram("wq", [D, D], F32, "ExternalInput")
    wk_d = dram("wk", [D, D], F32, "ExternalInput")
    wv_d = dram("wv", [D, D], F32, "ExternalInput")
    wo_d = dram("wo", [D, D], F32, "ExternalInput")
    wg1_d = dram("wg1", [D, 64], F32, "ExternalInput")
    wg2_d = dram("wg2", [64, D], F32, "ExternalInput")
    nw_d = dram("norm_w_pc", [128, 8], F32, "ExternalInput")
    cfm_d = dram("cfm", [D, S], BF16, "ExternalInput")
    sfm_d = dram("sfm", [D, S], BF16, "ExternalInput")
    ctm_d = dram("ctm", [S, D], BF16, "ExternalInput")
    stm_d = dram("stm", [S, D], BF16, "ExternalInput")
    maskU_d = dram("maskU", [128, 128], BF16, "ExternalInput")
    blk16_d = dram("blk16", [8 * 128, 16], F32, "ExternalInput")
    hexp_d = dram("hexp", [16, 8 * 128], BF16, "ExternalInput")
    a0_d = dram("a0", [B, D, S], F32, "ExternalOutput")
    if dbg:
        dbg_qt = dram("dbg_qt", [B, 4, 128, 4096], BF16, "ExternalOutput")
        dbg_e = dram("dbg_e", [B, 4, 128, 4096], BF16, "ExternalOutput")
        dbg_o = dram("dbg_o", [B, 4, 128, 4096], F32, "ExternalOutput")
        dbg_rq = dram("dbg_rq", [B, 4, 128, 4096], F32, "ExternalOutput")
        dbg_sc = dram("dbg_sc", [B, 4, 128, 512], F32, "ExternalOutput")
        dbg_carry = dram("dbg_carry", [B, 4, 128, 8], F32, "ExternalOutput")
        dbg_cs = dram("dbg_cs", [B, 4, 128, 4096], F32, "ExternalOutput")

    sb = ctx["arena"].t if ctx else nc.alloc_sbuf_tensor
    wq = sb("wq_s", [128, 8192], BF16)
    wk = sb("wk_s", [128, 8192], BF16)
    wv = sb("wv_s", [128, 8192], BF16)
    wo = sb("wo_s", [128, 8192], BF16)
    wg1 = sb("wg1_s", [128, 512], BF16)
    wg2 = sb("wg2_s", [64, 1024], BF16)
    nw = sb("nw_s", [128, 8], F32)
    blk16 = sb("blk16_s", [128, 128], F32)
    hexp = sb("hexp_s", [16, 1024], BF16)
    maskU = sb("maskU_s", [128, 128], BF16)
    ones_t = sb("ones_t", [128, 512], F32)
    xn_sc = sb("xn_sc", [128, 8192], BF16)
    cfm_t = sb("cfm_t", [128, 2048], BF16)
    sfm_t = sb("sfm_t", [128, 2048], BF16)
    ctm_t = sb("ctm_t", [128, 2048], BF16)
    stm_t = sb("stm_t", [128, 2048], BF16)
    e_bf = sb("e_bf", [128, 4096], BF16)
    csrq = sb("csrq", [128, 4096], F32)
    carry = [sb("carryA", [128, 8], F32), sb("carryB", [128, 8], F32)]
    rsum_scr = sb("rsum_scr", [128, 1], F32)
    qt_bf = sb("qt_bf", [128, 4096], BF16)
    e_scr = sb("e_scr", [128, 512], F32)
    ln_scr = sb("ln_scr", [128, 512], F32)
    t_scr = sb("t_scr", [128, 512], F32)
    u_scr = sb("u_scr", [128, 512], F32)
    v_tm = sb("v_tm", [128, 4096], BF16)
    etm_scr = sb("etm_scr", [128, 1024], BF16)
    kctm = sb("kctm", [128, 1024], BF16)
    kstm = sb("kstm", [128, 1024], BF16)
    qc_t = sb("qc_t", [128, 1024], BF16)
    qs_t = sb("qs_t", [128, 1024], BF16)
    kc_t = sb("kc_t", [128, 1024], BF16)
    ks_t = sb("ks_t", [128, 1024], BF16)
    atm = sb("atm", [128, 512], BF16)
    o_sb = sb("o_sb", [128, 4096], F32)
    Sc = [sb("Sc0", [128, 512], F32), sb("Sc1", [128, 512], F32)]
    Ss = [sb("Ss0", [128, 512], F32), sb("Ss1", [128, 512], F32)]
    Scb = sb("Scb", [128, 512], BF16)
    Ssb = sb("Ssb", [128, 512], BF16)
    g1_bf = sb("g1_bf", [64, 512], BF16)
    g2_sb = csrq
    sq_scr = sb("sq_scr", [128, 512], F32)
    rstd16 = sb("rstd16", [16, 512], F32)
    rstdb16 = sb("rstdb16", [16, 512], BF16)
    ln16 = sb("ln16", [16, 512], F32)
    nrm_bf = qt_bf
    a0_scr = sb("a0_scr", [128, 1024], F32)
    PS, psT, psTf = _mk_psum(nc, ctx)
    cpool, cvals = ctx["cpool"] if ctx else setup_const_pool(nc, [-0.6931472, 1e-6])

    sched = None if ctx else Sched(nc, ["dq", "dg", "pe", "v", "s"])

    def body(sd):
        op, at = sd.op, sd.at
        cp = emit_const_pool(sd, cpool, cvals)
        op("gpsimd", lambda e: e.memset(ones_t[:], 1.0), inc="dg")
        for t in (Sc[0], Sc[1], Ss[0], Ss[1], Scb, Ssb):
            op("gpsimd", lambda e, t=t: e.memset(t[:], 0.0), inc="dg")
        init_done = at("dg")
        for src, dst in ((wq_d, wq), (wk_d, wk), (wv_d, wv), (wo_d, wo)):
            for k in range(8):
                op("gpsimd", lambda e, s=src, d=dst, k=k: e.dma_start(
                    out=d[:, k * 1024:(k + 1) * 1024],
                    in_=s[k * 128:(k + 1) * 128, :]), inc="dg", inc_n=16)
        for k in range(8):
            op("gpsimd", lambda e, k=k: e.dma_start(
                out=wg1[:, k * 64:(k + 1) * 64],
                in_=wg1_d[k * 128:(k + 1) * 128, :]), inc="dg", inc_n=16)
        op("gpsimd", lambda e: e.dma_start(out=wg2[:], in_=wg2_d), inc="dg", inc_n=16)
        op("gpsimd", lambda e: e.dma_start(out=hexp[:], in_=hexp_d), inc="dg", inc_n=16)
        op("sync", lambda e: e.dma_start(out=nw[:], in_=nw_d), inc="dq", inc_n=16)
        for k in range(8):
            op("sync", lambda e, k=k: e.dma_start(
                out=blk16[:, k * 16:(k + 1) * 16],
                in_=blk16_d[k * 128:(k + 1) * 128, :]), inc="dq", inc_n=16)
        op("sync", lambda e: e.dma_start(out=maskU[:], in_=maskU_d), inc="dq", inc_n=16)
        w_done_g = at("dg")
        w_done_q = at("dq")

        NSC = S // 512
        dg_gate = at("dg")
        for b in range(B):
            if b > 0:
                st_reset_wait = [("pe", at("pe")), ("s", at("s"))]
                for t in (Sc[0], Sc[1], Ss[0], Ss[1], Scb, Ssb):
                    op("gpsimd", lambda e, t=t: e.memset(t[:], 0.0),
                       waits=st_reset_wait, inc="dg")
                    st_reset_wait = ()
                dg_gate = at("dg")
            op("vector", lambda e, b=b: e.memset(carry[0][:], 1.0),
               waits=[("dg", at("dg"))], inc="v")
            op("vector", lambda e, b=b: e.memset(carry[1][:], 1.0), inc="v")
            for sc in range(NSC):
                t0 = sc * 512
                xnb = (sc + b * NSC) % 2 * 4096
                if sc == 0 and b == 0:
                    for c in range(8):
                        op("sync", lambda e, b=b, t0=t0, c=c, xnb=xnb:
                           e.dma_start(
                               out=xn_sc[:, xnb + c * 512:xnb + (c + 1) * 512],
                               in_=xn[b, c * 128:(c + 1) * 128, t0:t0 + 512]),
                           inc="dq", inc_n=16)
                    xn_ld = at("dq")
                else:
                    xn_ld = xn_ld_next
                for m in range(8):
                    bank = PS[m % 2]
                    for k in range(8):
                        op("tensor", lambda e, m=m, k=k, bank=bank, xnb=xnb: e.matmul(
                            bank[:],
                            wk[:, k * 1024 + m * 128:k * 1024 + (m + 1) * 128],
                            xn_sc[:, xnb + k * 512:xnb + (k + 1) * 512],
                            start=(k == 0), stop=(k == 7)),
                           waits=([("dq", xn_ld), ("dg", max(w_done_g, dg_gate,
                                                              at("dg") if dbg else 0)),
                                   ("s", at("s")), ("v", at("v"))]
                                  if k == 0 else ()),
                           inc=("pe" if k == 7 else None))
                    op("scalar", lambda e, m=m, bank=bank: e.activation(
                        out=e_bf[:, m * 512:(m + 1) * 512], in_=bank[:],
                        func=AF.Exp), waits=[("pe", at("pe"))], inc="s")
                    op("scalar", lambda e, m=m, bank=bank: e.activation(
                        out=e_scr[:], in_=bank[:], func=AF.Exp,
                        accum_out=rsum_scr[:]), inc="s")
                    op("vector", lambda e, m=m, sc=sc: e.tensor_tensor_scan(
                        csrq[:, m * 512:(m + 1) * 512], ones_t[:], e_scr[:],
                        carry[sc % 2][:, m:m + 1], ALU.mult, ALU.add),
                       waits=[("s", at("s")), ("dg", init_done)], inc="v")
                    op("vector", lambda e, m=m, sc=sc: e.tensor_add(
                        carry[1 - sc % 2][:, m:m + 1], carry[sc % 2][:, m:m + 1],
                        rsum_scr[:]), waits=[("s", at("s"))], inc="v")
                    if dbg:
                        op("gpsimd", lambda e, b=b, sc=sc, m=m: e.dma_start(
                            out=dbg_cs[b, sc, :, m * 512:(m + 1) * 512],
                            in_=csrq[:, m * 512:(m + 1) * 512]),
                           waits=[("v", at("v"))], inc="dg", inc_n=16)
                    op("scalar", lambda e, m=m: e.activation(
                        out=ln_scr[:], in_=csrq[:, m * 512:(m + 1) * 512],
                        func=AF.Ln),
                       waits=[("v", at("v"))] + ([("dg", at("dg"))] if dbg else []),
                       inc="s")
                    op("scalar", lambda e, m=m: e.activation(
                        out=csrq[:, m * 512:(m + 1) * 512], in_=ln_scr[:],
                        func=AF.Exp, scale=-1.0, bias=-0.6931472),
                       waits=[("dg", cp)], inc="s")
                for m in range(8):
                    bank = PS[m % 2]
                    for k in range(8):
                        op("tensor", lambda e, m=m, k=k, bank=bank, xnb=xnb: e.matmul(
                            bank[:],
                            wq[:, k * 1024 + m * 128:k * 1024 + (m + 1) * 128],
                            xn_sc[:, xnb + k * 512:xnb + (k + 1) * 512],
                            start=(k == 0), stop=(k == 7)),
                           waits=([("s", at("s")), ("v", at("v"))]
                                  if k == 0 else ()),
                           inc=("pe" if k == 7 else None))
                    op("scalar", lambda e, bank=bank: e.activation(
                        out=t_scr[:], in_=bank[:], func=AF.Tanh, scale=0.5),
                       waits=[("pe", at("pe"))], inc="s")
                    op("vector", lambda e, bank=bank: e.scalar_tensor_tensor(
                        u_scr[:], t_scr[:], 1.0, bank[:], ALU.add, ALU.mult),
                       waits=[("s", at("s"))], inc="v")
                    op("vector", lambda e, m=m: e.tensor_mul(
                        qt_bf[:, m * 512:(m + 1) * 512], u_scr[:],
                        csrq[:, m * 512:(m + 1) * 512]), inc="v")
                for tc in range(4):
                    for half in range(2):
                        bank = PS[(tc * 2 + half) % 2]
                        for k in range(8):
                            op("tensor", lambda e, tc=tc, h=half, k=k, t0=t0,
                               bank=bank, xnb=xnb: e.matmul(
                                   bank[:],
                                   xn_sc[:, xnb + k * 512 + tc * 128:xnb + k * 512 + (tc + 1) * 128],
                                   wv[:, k * 1024 + h * 512:k * 1024 + h * 512 + 512],
                                   start=(k == 0), stop=(k == 7)),
                               waits=([("v", at("v")), ("s", at("s"))]
                                      if k == 0 else ()),
                               inc=("pe" if k == 7 else None))
                        op("scalar", lambda e, tc=tc, h=half, bank=bank:
                           e.activation(
                               out=v_tm[:, tc * 1024 + h * 512:tc * 1024 + h * 512 + 512],
                               in_=bank[:], func=AF.Identity),
                           waits=[("pe", at("pe"))], inc="s")
                if dbg:
                    op("gpsimd", lambda e, b=b, sc=sc: e.dma_start(
                        out=dbg_carry[b, sc], in_=carry[1 - sc % 2][:]),
                       waits=[("v", at("v"))], inc="dg", inc_n=16)
                    dw = [("v", at("v")), ("s", at("s"))]
                    for t_, d_ in ((qt_bf, dbg_qt), (e_bf, dbg_e), (csrq, dbg_rq)):
                        op("gpsimd", lambda e, t_=t_, d_=d_, b=b, sc=sc:
                           e.dma_start(out=d_[b, sc], in_=t_[:]),
                           waits=dw, inc="dg", inc_n=16)
                        dw = ()
                def trig_load(tau2):
                    tg2 = t0 + tau2 * 128
                    db2 = tau2 % 2
                    ld_w = [("v", at("v"))]
                    for s_, d_ in ((cfm_d, cfm_t), (sfm_d, sfm_t)):
                        for c in range(8):
                            op("sync", lambda e, s=s_, d=d_, db=db2, c=c,
                               tg=tg2: e.dma_start(
                                   out=d[:, db * 1024 + c * 128:db * 1024 + (c + 1) * 128],
                                   in_=s[c * 128:(c + 1) * 128, tg:tg + 128]),
                               waits=ld_w, inc="dq", inc_n=16)
                            ld_w = ()
                    for s_, d_ in ((ctm_d, ctm_t), (stm_d, stm_t)):
                        op("sync", lambda e, s=s_, d=d_, db=db2, tg=tg2:
                           e.dma_start(out=d[:, db * 1024:(db + 1) * 1024],
                                       in_=s[tg:tg + 128, :]),
                           inc="dq", inc_n=16)
                    return at("dq")

                trig_at = {0: trig_load(0)}
                nsc_all = b * NSC + sc + 1
                if nsc_all < B * NSC:
                    b2, sc2 = divmod(nsc_all, NSC)
                    t02 = sc2 * 512
                    xnb2 = nsc_all % 2 * 4096
                    pw = [("pe", at("pe"))]
                    for c in range(8):
                        op("sync", lambda e, b2=b2, t02=t02, c=c, xnb2=xnb2:
                           e.dma_start(
                               out=xn_sc[:, xnb2 + c * 512:xnb2 + (c + 1) * 512],
                               in_=xn[b2, c * 128:(c + 1) * 128, t02:t02 + 512]),
                           waits=pw, inc="dq", inc_n=16)
                        pw = ()
                    xn_ld_next = at("dq")
                for tau in range(4):
                    tg = t0 + tau * 128
                    db = tau % 2
                    trig_ld = trig_at[tau]
                    rw = [("dq", trig_ld)]
                    for dst, a_, b_ in ((qc_t, qt_bf, cfm_t), (qs_t, qt_bf, sfm_t),
                                        (kc_t, e_bf, cfm_t), (ks_t, e_bf, sfm_t)):
                        op("vector", lambda e, dst=dst, a_=a_, b_=b_,
                           db=db, tau=tau, xnb=xnb: e.tensor_mul(
                               dst[:].rearrange("p (c n) -> p c n", n=128),
                               a_[:].rearrange(
                                   "p (c n) -> p c n",
                                   n=512)[:, :, tau * 128:(tau + 1) * 128],
                               b_[:, db * 1024:(db + 1) * 1024].rearrange(
                                   "p (c n) -> p c n", n=128)),
                           waits=rw, inc="v")
                        rw = ()
                    rot_done = at("v")
                    if tau < 3:
                        trig_at[tau + 1] = trig_load(tau + 1)
                    for half in range(2):
                        bank = PS[half % 2]
                        for k in range(8):
                            op("tensor", lambda e, h=half, k=k, tau=tau, bank=bank,
                               xnb=xnb:
                               e.matmul(
                                   bank[:],
                                   xn_sc[:, xnb + k * 512 + tau * 128:xnb + k * 512 + (tau + 1) * 128],
                                   wk[:, k * 1024 + h * 512:k * 1024 + h * 512 + 512],
                                   start=(k == 0), stop=(k == 7)),
                               waits=([("s", at("s"))]
                                      if k == 0 else ()),
                               inc=("pe" if k == 7 else None))
                        op("scalar", lambda e, h=half, bank=bank: e.activation(
                            out=etm_scr[:, h * 512:h * 512 + 512], in_=bank[:],
                            func=AF.Exp), waits=[("pe", at("pe"))], inc="s")
                    etm_done = at("s")
                    for dst, trg in ((kctm, ctm_t), (kstm, stm_t)):
                        op("vector", lambda e, dst=dst, trg=trg, db=db:
                           e.tensor_mul(dst[:], etm_scr[:],
                                        trg[:, db * 1024:(db + 1) * 1024]),
                           waits=[("s", etm_done)], inc="v")
                    ktm_done = at("v")
                    mask_at, oevac_at, stadd_at = {}, {}, {}

                    def head_front(h):
                        par = (h % 2) * 64
                        hc_ = h // 2
                        bAT = PS[2 + h % 2]
                        hsl = slice(par, par + 64)
                        csl = slice(hc_ * 128, hc_ * 128 + 128)
                        at_w = [("v", mask_at[h - 2] if h >= 2 else rot_done)]
                        op("tensor", lambda e, h=h, bAT=bAT, hsl=hsl, csl=csl:
                           e.matmul(bAT[:, 0:128], kc_t[hsl, csl], qc_t[hsl, csl],
                                    start=True, stop=False),
                           waits=at_w, inc=None)
                        op("tensor", lambda e, h=h, bAT=bAT, hsl=hsl, csl=csl:
                           e.matmul(bAT[:, 0:128], ks_t[hsl, csl], qs_t[hsl, csl],
                                    start=False, stop=True), inc="pe")
                        op("vector", lambda e, h=h, bAT=bAT: e.tensor_mul(
                            atm[:, (h % 4) * 128:(h % 4) * 128 + 128],
                            bAT[:, 0:128], maskU[:]),
                           waits=[("pe", at("pe"))], inc="v")
                        mask_at[h] = at("v")

                    def head_tail(h):
                        par = (h % 2) * 64
                        hc_ = h // 2
                        bO = PS[4 + h % 2]
                        bKV = PS[0 + h % 2]
                        hsl = slice(par, par + 64)
                        csl = slice(hc_ * 128, hc_ * 128 + 128)
                        o_w = [("v", mask_at[h])] + (
                            [("s", oevac_at[h - 2])] if h >= 2 else [])
                        op("tensor", lambda e, h=h, bO=bO, tau=tau, par=par:
                           e.matmul(bO[par:par + 64, 0:128],
                                    v_tm[:, tau * 1024 + h * 64:tau * 1024 + (h + 1) * 64],
                                    atm[:, (h % 4) * 128:(h % 4) * 128 + 128],
                                    start=True, stop=False),
                           waits=o_w, inc=None)
                        op("tensor", lambda e, h=h, bO=bO, hsl=hsl, csl=csl,
                           par=par, hc_=hc_: e.matmul(
                               bO[par:par + 64, 0:128],
                               Scb[hsl, hc_ * 64:hc_ * 64 + 64], qc_t[hsl, csl],
                               start=False, stop=False), inc=None)
                        op("tensor", lambda e, h=h, bO=bO, hsl=hsl, csl=csl,
                           par=par, hc_=hc_: e.matmul(
                               bO[par:par + 64, 0:128],
                               Ssb[hsl, hc_ * 64:hc_ * 64 + 64], qs_t[hsl, csl],
                               start=False, stop=True), inc="pe")
                        op("scalar", lambda e, h=h, bO=bO, tau=tau, par=par:
                           e.activation(
                               out=o_sb[par:par + 64,
                                        (h // 2) * 512 + tau * 128:
                                        (h // 2) * 512 + (tau + 1) * 128],
                               in_=bO[par:par + 64, 0:128], func=AF.Identity),
                           waits=[("pe", at("pe"))], inc="s")
                        oevac_at[h] = at("s")
                        kv_w = [("v", stadd_at[h - 2] if h >= 2 else ktm_done)]
                        op("tensor", lambda e, h=h, bKV=bKV, tau=tau, par=par:
                           e.matmul(bKV[par:par + 64, 0:64],
                                    kctm[:, h * 64:(h + 1) * 64],
                                    v_tm[:, tau * 1024 + h * 64:tau * 1024 + (h + 1) * 64],
                                    start=True, stop=True),
                           waits=kv_w, inc=None)
                        op("tensor", lambda e, h=h, bKV=bKV, tau=tau, par=par:
                           e.matmul(bKV[par:par + 64, 64:128],
                                    kstm[:, h * 64:(h + 1) * 64],
                                    v_tm[:, tau * 1024 + h * 64:tau * 1024 + (h + 1) * 64],
                                    start=True, stop=True), inc="pe")
                        cur = (tau + sc * 4) % 2
                        nxt = 1 - cur
                        op("vector", lambda e, h=h, bKV=bKV, par=par, hc_=hc_,
                           cur=cur, nxt=nxt: e.tensor_add(
                               Sc[nxt][par:par + 64, hc_ * 64:hc_ * 64 + 64],
                               Sc[cur][par:par + 64, hc_ * 64:hc_ * 64 + 64],
                               bKV[par:par + 64, 0:64]),
                           waits=[("pe", at("pe"))], inc="v")
                        op("vector", lambda e, h=h, bKV=bKV, par=par, hc_=hc_,
                           cur=cur, nxt=nxt: e.tensor_add(
                               Ss[nxt][par:par + 64, hc_ * 64:hc_ * 64 + 64],
                               Ss[cur][par:par + 64, hc_ * 64:hc_ * 64 + 64],
                               bKV[par:par + 64, 64:128]), inc="v")
                        stadd_at[h] = at("v")
                        op("scalar", lambda e, h=h, par=par, hc_=hc_, nxt=nxt:
                           e.activation(
                               out=Scb[par:par + 64, hc_ * 64:hc_ * 64 + 64],
                               in_=Sc[nxt][par:par + 64, hc_ * 64:hc_ * 64 + 64],
                               func=AF.Identity),
                           waits=[("v", at("v"))], inc="s")
                        op("scalar", lambda e, h=h, par=par, hc_=hc_, nxt=nxt:
                           e.activation(
                               out=Ssb[par:par + 64, hc_ * 64:hc_ * 64 + 64],
                               in_=Ss[nxt][par:par + 64, hc_ * 64:hc_ * 64 + 64],
                               func=AF.Identity), inc="s")

                    for c2 in range(8):
                        head_front(2 * c2)
                        head_front(2 * c2 + 1)
                        if c2 >= 1:
                            head_tail(2 * c2 - 2)
                            head_tail(2 * c2 - 1)
                    head_tail(14)
                    head_tail(15)
                if dbg:
                    op("gpsimd", lambda e, b=b, sc=sc: e.dma_start(
                        out=dbg_o[b, sc], in_=o_sb[:]),
                       waits=[("s", at("s"))], inc="dg", inc_n=16)
                    op("gpsimd", lambda e, b=b, sc=sc: e.dma_start(
                        out=dbg_sc[b, sc], in_=Sc[(sc * 4 + 4) % 2][:]),
                       waits=[("v", at("v"))], inc="dg", inc_n=16)
                bank = PS[0]
                for k in range(8):
                    op("tensor", lambda e, k=k, bank=bank, xnb=xnb: e.matmul(
                        bank[0:64, :], wg1[:, k * 64:(k + 1) * 64],
                        xn_sc[:, xnb + k * 512:xnb + (k + 1) * 512],
                        start=(k == 0), stop=(k == 7)),
                       waits=([("v", at("v")), ("s", at("s"))] if k == 0 else ()),
                       inc=("pe" if k == 7 else None))
                op("scalar", lambda e, bank=bank: e.activation(
                    out=g1_bf[:], in_=bank[0:64, :], func=AF.Identity),
                   waits=[("pe", at("pe"))], inc="s")
                g1_done = at("s")
                for m in range(8):
                    bank = PS[1 + m % 2]
                    op("tensor", lambda e, m=m, bank=bank: e.matmul(
                        bank[:], wg2[:, m * 128:(m + 1) * 128], g1_bf[:],
                        start=True, stop=True),
                       waits=([("s", g1_done)] if m == 0 else
                              [("s", at("s"))]), inc="pe")
                    op("scalar", lambda e, m=m, bank=bank: e.activation(
                        out=t_scr[:], in_=bank[:], func=AF.Tanh, scale=0.5),
                       waits=[("pe", at("pe"))], inc="s")
                    op("vector", lambda e, m=m: e.scalar_tensor_tensor(
                        g2_sb[:, m * 512:(m + 1) * 512], t_scr[:], 1.0,
                        o_sb[:, m * 512:(m + 1) * 512], ALU.add, ALU.mult),
                       waits=[("s", at("s"))], inc="v")
                    op("scalar", lambda e, m=m: e.activation(
                        out=sq_scr[:], in_=g2_sb[:, m * 512:(m + 1) * 512],
                        func=AF.Square), waits=[("v", at("v"))], inc="s")
                    op("tensor", lambda e, m=m: e.matmul(
                        PS[3][0:16, :], blk16[:, m * 16:(m + 1) * 16], sq_scr[:],
                        start=(m == 0), stop=(m == 7)),
                       waits=[("s", at("s"))],
                       inc=("pe" if m == 7 else None))
                op("scalar", lambda e: e.activation(
                    out=ln16[:], in_=PS[3][0:16, :], func=AF.Ln,
                    scale=1.0 / 256.0, bias=1e-6),
                   waits=[("pe", at("pe")), ("dg", cp)], inc="s")
                op("scalar", lambda e: e.activation(
                    out=rstd16[:], in_=ln16[:], func=AF.Exp, scale=-0.5,
                    bias=-0.6931472), inc="s")
                op("scalar", lambda e: e.activation(
                    out=rstdb16[:], in_=rstd16[:], func=AF.Identity), inc="s")
                rstd_done = at("s")
                for c in range(8):
                    bank = PS[4 + c % 2]
                    op("tensor", lambda e, c=c, bank=bank: e.matmul(
                        bank[:], hexp[:, c * 128:(c + 1) * 128], rstdb16[:],
                        start=True, stop=True),
                       waits=([("s", rstd_done)] if c == 0 else
                              [("v", at("v"))]), inc="pe")
                    op("vector", lambda e, c=c, bank=bank: e.tensor_mul(
                        u_scr[:], g2_sb[:, c * 512:(c + 1) * 512], bank[:]),
                       waits=[("pe", at("pe"))], inc="v")
                    op("vector", lambda e, c=c: e.tensor_scalar(
                        nrm_bf[:, c * 512:(c + 1) * 512], u_scr[:],
                        nw[:, c:c + 1], None, ALU.mult), inc="v")
                nrm_done = at("v")
                a0dma_at = {}
                for m in range(8):
                    bank = PS[0 + m % 2]
                    for k in range(8):
                        op("tensor", lambda e, m=m, k=k, bank=bank: e.matmul(
                            bank[:],
                            wo[:, k * 1024 + m * 128:k * 1024 + (m + 1) * 128],
                            nrm_bf[:, k * 512:(k + 1) * 512],
                            start=(k == 0), stop=(k == 7)),
                           waits=([("v", nrm_done)] if (m == 0 and k == 0) else
                                  ([("v", at("v"))] if k == 0 else ())),
                           inc=("pe" if k == 7 else None))
                    op("vector", lambda e, m=m, bank=bank: e.tensor_copy(
                        a0_scr[:, (m % 2) * 512:(m % 2 + 1) * 512], bank[:]),
                       waits=[("pe", at("pe"))] + (
                           [("dq", a0dma_at[m - 2])] if m >= 2 else []),
                       inc="v")
                    a0dma_at[m] = op("sync", lambda e, b=b, m=m, t0=t0: e.dma_start(
                        out=a0_d[b, m * 128:(m + 1) * 128, t0:t0 + 512],
                        in_=a0_scr[:, (m % 2) * 512:(m % 2 + 1) * 512]),
                       waits=[("v", at("v"))], inc="dq", inc_n=16)
        sd.wait("gpsimd", [("dq", at("dq"))])

    if ctx is not None:
        return body
    run_block(nc, sched, body)
    sched.free()
    return nc


def build_mid(ctx=None):
    nc = ctx["nc"] if ctx else bass.Bass("TRN2", target_bir_lowering=False)
    dram = _mk_dram(nc, ctx)
    x = dram("x", [B, D, S], F32, "ExternalInput")
    a0 = dram("a0", [B, D, S], F32, "ExternalInput")
    xn0 = dram("xn0", [B, D, S], BF16, "ExternalInput")
    M_d = dram("M", [D, D], BF16, "ExternalInput")
    bias_d = dram("bias_pc", [128, 8], F32, "ExternalInput")
    idf = dram("ident_f32", [128, 128], F32, "ExternalInput")
    sel8 = dram("sel8", [8, 1024], F32, "ExternalInput")
    inv1024 = dram("inv1024", [128, 1], F32, "ExternalInput")
    x1_out = dram("x1", [B, D, S], F32, "ExternalOutput")
    xn1_out = dram("xn1", [B, D, S], BF16, "ExternalOutput")

    sb = ctx["arena"].t if ctx else nc.alloc_sbuf_tensor
    M_sb = sb("M_sb", [128, 8192], BF16)
    bias = sb("bias_sb", [128, 8], F32)
    idf_sb = sb("idf_sb", [128, 128], F32)
    sel8_sb = sb("sel8_sb", [8, 1024], F32)
    i1024_sb = sb("i1024_sb", [128, 1], F32)
    x_sb = sb("x_sb", [128, 8192], F32)
    a0_sb = sb("a0_sb", [128, 8192], F32)
    xn0_sb = sb("xn0_sb", [128, 8192], BF16)
    x1_sb = sb("x1_sb", [128, 8192], F32)
    sq_sb = sb("sq_sb", [128, 4096], F32)
    u_sb = sb("u_sb", [128, 4096], F32)
    xn_sb = sb("xn_sb", [128, 4096], BF16)
    stats_sb = sb("stats_sb", [128, 8], F32)
    var_sb = sb("var_sb", [128, 4], F32)
    msq2_sb = sb("msq2_sb", [128, 4], F32)
    ln_sb = sb("ln_sb", [128, 4], F32)
    fct_sb = sb("fct_sb", [128, 8], F32)
    rowT_sb = sb("rowT_sb", [8, 128], F32)
    PS, psT, psTf = _mk_psum(nc, ctx)
    psA, psB = PS[0], PS[1]
    psStat = PS[2]
    psBC0, psBC1 = PS[3], PS[4]
    cpool, cvals = ctx["cpool"] if ctx else setup_const_pool(nc, [1e-5])

    sched = None if ctx else Sched(nc, ["dq", "dg", "pe", "v", "s"])
    tiles = [(b, t0) for b in range(B) for t0 in range(0, S, 512)]

    def body(sd):
        op, at = sd.op, sd.at
        cp = emit_const_pool(sd, cpool, cvals)
        for k in range(8):
            op("gpsimd", lambda e, k=k: e.dma_start(
                out=M_sb[:, k * 1024:(k + 1) * 1024],
                in_=M_d[k * 128:(k + 1) * 128, :]), inc="dg", inc_n=16)
        op("sync", lambda e: e.dma_start(out=bias[:], in_=bias_d), inc="dq", inc_n=16)
        op("sync", lambda e: e.dma_start(out=idf_sb[:], in_=idf), inc="dq", inc_n=16)
        op("sync", lambda e: e.dma_start(out=sel8_sb[:], in_=sel8), inc="dq", inc_n=16)
        op("sync", lambda e: e.dma_start(out=i1024_sb[:], in_=inv1024), inc="dq", inc_n=16)
        w_g, w_q = at("dg"), at("dq")
        ld_at, x1_at = {}, {}

        def load(i):
            b, t0 = tiles[i]
            o = i % 2 * 4096
            lw = [("v", at("v")), ("s", at("s"))]
            for src_, dst in ((x, x_sb), (a0, a0_sb), (xn0, xn0_sb)):
                for c in range(8):
                    op("sync", lambda e, s=src_, d=dst, b=b, t0=t0, c=c, o=o:
                       e.dma_start(out=d[:, o + c * 512:o + (c + 1) * 512],
                                   in_=s[b, c * 128:(c + 1) * 128, t0:t0 + 512]),
                       waits=lw, inc="dq", inc_n=16)
                    lw.clear()
            ld_at[i] = at("dq")

        def front(i):
            b, t0 = tiles[i]
            o = i % 2 * 4096
            ld = ld_at[i]
            for m in range(8):
                bank = psA if m % 2 == 0 else psB
                for k in range(8):
                    op("tensor", lambda e, m=m, k=k, bank=bank, o=o: e.matmul(
                        bank[:],
                        M_sb[:, k * 1024 + m * 128:k * 1024 + (m + 1) * 128],
                        xn0_sb[:, o + k * 512:o + (k + 1) * 512],
                        start=(k == 0), stop=(k == 7)),
                       waits=([("dq", ld), ("dg", w_g), ("v", at("v"))]
                              if k == 0 else ()),
                       inc=("pe" if k == 7 else None))
                op("vector", lambda e, m=m, bank=bank, o=o: e.scalar_tensor_tensor(
                    u_sb[:, m * 512:(m + 1) * 512], bank[:], bias[:, m:m + 1],
                    a0_sb[:, o + m * 512:o + (m + 1) * 512], ALU.add, ALU.add),
                   waits=[("pe", at("pe"))], inc="v")
                op("vector", lambda e, m=m, o=o: e.tensor_add(
                    x1_sb[:, o + m * 512:o + (m + 1) * 512],
                    u_sb[:, m * 512:(m + 1) * 512],
                    x_sb[:, o + m * 512:o + (m + 1) * 512]), inc="v")
            x1_done = at("v")
            for c in range(8):
                op("sync", lambda e, b=b, t0=t0, c=c, o=o: e.dma_start(
                    out=x1_out[b, c * 128:(c + 1) * 128, t0:t0 + 512],
                    in_=x1_sb[:, o + c * 512:o + (c + 1) * 512]),
                   waits=([("v", x1_done)] if c == 0 else ()),
                   inc="dq", inc_n=16)
            x1_at[i] = x1_done

        def tail(i):
            b, t0 = tiles[i]
            o = i % 2 * 4096
            op("scalar", lambda e, o=o: e.activation(
                out=sq_sb[:], in_=x1_sb[:, o:o + 4096], func=AF.Square),
               waits=[("v", at("v"))], inc="s")
            sq_ready = at("s")
            for jj in range(4):
                for src_, col, off in ((x1_sb, 0, o), (sq_sb, 1, 0)):
                    for cch in range(8):
                        last = cch == 7
                        op("tensor", lambda e, src=src_, col=2 * jj + col,
                           jj=jj, c=cch, off=off, f=(cch == 0), l=last: e.matmul(
                               psStat[:, col:col + 1],
                               src[:, off + c * 512 + jj * 128:
                                   off + c * 512 + jj * 128 + 128],
                               i1024_sb[:], start=f, stop=l),
                           waits=([("s", sq_ready), ("v", at("v"))]
                                  if (col == 2 * jj and cch == 0) else ()),
                           inc=("pe" if last else None))
            op("vector", lambda e: e.tensor_copy(stats_sb[:], psStat[:, 0:8]),
               waits=[("pe", at("pe"))], inc="v")
            op("scalar", lambda e: e.activation(
                out=msq2_sb[:], in_=stats_sb[:, 0:8:2], func=AF.Square),
               waits=[("v", at("v"))], inc="s")
            op("vector", lambda e: e.tensor_sub(
                var_sb[:], stats_sb[:, 1:8:2], msq2_sb[:]),
               waits=[("s", at("s"))], inc="v")
            op("scalar", lambda e: e.activation(
                out=ln_sb[:], in_=var_sb[:], func=AF.Ln, bias=1e-5),
               waits=[("v", at("v")), ("dg", cp)], inc="s")
            op("scalar", lambda e: e.activation(
                out=fct_sb[:, 4:8], in_=ln_sb[:], func=AF.Exp,
                scale=-0.5), inc="s")
            op("vector", lambda e: e.tensor_copy(
                fct_sb[:, 0:4], stats_sb[:, 0:8:2]), inc="v")
            op("tensor", lambda e: e.transpose(
                psTf[0:8, :], fct_sb[:, 0:8], idf_sb[:]),
               waits=[("s", at("s")), ("v", at("v"))], inc="pe")
            op("vector", lambda e: e.tensor_copy(rowT_sb[:], psTf[0:8, :]),
               waits=[("pe", at("pe"))], inc="v")
            rowv = at("v")
            for jj in range(4):
                for w in range(2):
                    op("tensor", lambda e, jj=jj, w=w: e.matmul(
                        (psBC0 if w == 0 else psBC1)[:, jj * 128:(jj + 1) * 128],
                        sel8_sb[:, (jj + 4 * w) * 128:(jj + 4 * w + 1) * 128],
                        rowT_sb[:], start=True, stop=True),
                       waits=([("v", rowv)] if (jj == 0 and w == 0) else ()),
                       inc="pe")
            bc = at("pe")
            for cch in range(8):
                op("vector", lambda e, c=cch, o=o: e.scalar_tensor_tensor(
                    sq_sb[:, c * 512:(c + 1) * 512], psBC0[:], -1.0,
                    x1_sb[:, o + c * 512:o + (c + 1) * 512], ALU.mult, ALU.add),
                   waits=([("pe", bc)] if cch == 0 else ()), inc="v")
                op("vector", lambda e, c=cch: e.tensor_mul(
                    xn_sb[:, c * 512:(c + 1) * 512],
                    sq_sb[:, c * 512:(c + 1) * 512],
                    psBC1[:]), inc="v")
            xn_ready = at("v")
            for cch in range(8):
                op("sync", lambda e, b=b, t0=t0, c=cch: e.dma_start(
                    out=xn1_out[b, c * 128:(c + 1) * 128, t0:t0 + 512],
                    in_=xn_sb[:, c * 512:(c + 1) * 512]),
                   waits=([("v", xn_ready)] if cch == 0 else ()),
                   inc="dq", inc_n=16)

        load(0)
        for i in range(len(tiles)):
            if i + 1 < len(tiles):
                load(i + 1)
            front(i)
            if i >= 1:
                tail(i - 1)
        tail(len(tiles) - 1)
        sd.wait("gpsimd", [("dq", at("dq"))])

    if ctx is not None:
        return body
    run_block(nc, sched, body)
    sched.free()
    return nc

def build_attn1(phases=('t1','t2','t3','t4','t5'), dbg=False, ctx=None):
    nc = ctx["nc"] if ctx else bass.Bass("TRN2", target_bir_lowering=False)
    dram = _mk_dram(nc, ctx)
    xn = dram("xn1", [B, D, S], BF16, "ExternalInput")
    x1_d = dram("x1", [B, D, S], F32, "ExternalInput")
    wq_d = dram("wq", [D, D], F32, "ExternalInput")
    wk_d = dram("wk", [D, D], F32, "ExternalInput")
    wv_d = dram("wv", [D, D], F32, "ExternalInput")
    wgq_d = dram("wgq", [D, 512], F32, "ExternalInput")
    wgk_d = dram("wgk", [D, 512], F32, "ExternalInput")
    wgv_d = dram("wgv", [D, 512], F32, "ExternalInput")
    rout_d = dram("r_out_w", [512, D], F32, "ExternalInput")
    wout_d = dram("w_out", [D, D], F32, "ExternalInput")
    glgr_d = dram("glgr", [128, 32], F32, "ExternalInput")
    maskU_d = dram("maskU", [128, 128], BF16, "ExternalInput")
    maskLs_d = dram("maskLs", [128, 128], BF16, "ExternalInput")
    lmask_d = dram("lmask", [64, S], BF16, "ExternalInput")
    hexp_d = dram("hexp", [16, 8 * 128], BF16, "ExternalInput")
    lexp_d = dram("lexp", [8, 4 * 128], BF16, "ExternalInput")
    idf_d = dram("ident_f32", [128, 128], F32, "ExternalInput")
    onesb_d = dram("ones_bf", [128, 1], BF16, "ExternalInput")
    a1_d = dram("a1", [B, D, S], F32, "ExternalOutput")
    if dbg:
        dbg_fct = dram("dbg_fct", [2, 128, 56], F32, "ExternalOutput")
        dbg_fs = dram("dbg_fs", [2, 128, 56], F32, "ExternalOutput")
        dbg_loc = dram("dbg_loc", [2, 128, 1024], F32, "ExternalOutput")
        dbg_rl = dram("dbg_rl", [2, 128, 512], BF16, "ExternalOutput")
        dbg_comb = dram("dbg_comb", [2, 128, 1024], BF16, "ExternalOutput")
        dbg_pt = dram("dbg_pt", [2, 128, 768], BF16, "ExternalOutput")
        dbg_k = dram("dbg_k", [2, 128, 4096], BF16, "ExternalOutput")
        dbg_v = dram("dbg_v", [2, 128, 4096], BF16, "ExternalOutput")

    sb = ctx["arena"].t if ctx else nc.alloc_sbuf_tensor
    wq = sb("wq_s", [128, 8192], BF16)
    wk = sb("wk_s", [128, 8192], BF16)
    wv = sb("wv_s", [128, 8192], BF16)
    wout = sb("wout_s", [128, 8192], BF16)
    wgq = sb("wgq_s", [128, 4096], BF16)
    wgk = sb("wgk_s", [128, 4096], BF16)
    wgv = sb("wgv_s", [128, 4096], BF16)
    rout = sb("rout_s", [128, 4096], BF16)
    glgr = sb("glgr_s", [128, 32], BF16)
    maskU = sb("maskU_s", [128, 128], BF16)
    maskLs = sb("maskLs_s", [128, 128], BF16)
    lmask = sb("lmask_s", [64, 2048], BF16)
    hexp = sb("hexp_s", [16, 1024], BF16)
    lexp = sb("lexp_s", [8, 512], BF16)
    idf_sb = sb("idf_s", [128, 128], F32)
    onesb = sb("onesb_s", [128, 1], BF16)
    xn_st = sb("xn_st", [128, 8192], BF16)
    x1_st = sb("x1_st", [128, 4096], F32)
    Q_bf = sb("Q_bf", [128, 4096], BF16)
    K_bf = sb("K_bf", [128, 4096], BF16)
    v_tm = sb("v_tm1", [128, 4096], BF16)
    K_tail = sb("K_tail", [128, 2048], BF16)
    V_tail = sb("V_tail", [128, 2048], BF16)
    GQ_bf = sb("GQ_bf", [128, 2048], BF16)
    xpool = sb("xpool", [128, 512], F32)
    xpool_bf = sb("xpool_bf", [128, 512], BF16)
    GK_sb = sb("GK_sb", [128, 256], BF16)
    GV_sb = sb("GV_sb", [64, 512], BF16)
    P_T = sb("P_T", [128, 1536], BF16)
    eL_T = sb("eL_T", [64, 512], BF16)
    local_sb = sb("local_sb", [128, 1024], F32)
    remlat_sb = sb("remlat_sb", [128, 512], BF16)
    remn_sb = sb("remn_sb", [128, 512], BF16)
    rem_sb = sb("rem_sb", [128, 256], F32)
    fct = sb("fct", [128, 56], F32)
    f_scr = sb("f_scr", [128, 56], F32)
    fT = sb("fT", [16, 256], BF16)
    rlT = sb("rlT", [8, 128], BF16)
    comb_bf = sb("comb_bf", [128, 1024], BF16)
    c1_sb = sb("c1_sb", [128, 128], F32)
    rem2_sb = sb("rem2_sb", [128, 128], F32)
    tg_sb = sb("tg_sb", [128, 32], F32)
    a1_scr = sb("a1_scr", [128, 2048], F32)
    PS, psT, psTf = _mk_psum(nc, ctx)

    sched = None if ctx else Sched(nc, ["dq", "dg", "pe", "v", "s"])

    def body(sd):
        op, at = sd.op, sd.at
        for src, dst, n in ((wq_d, wq, 8), (wk_d, wk, 8), (wv_d, wv, 8),
                            (wout_d, wout, 8), (wgq_d, wgq, 8), (wgk_d, wgk, 8),
                            (wgv_d, wgv, 8), (rout_d, rout, 4)):
            w = 1024 if n == 8 and dst in (wq, wk, wv, wout) else 512
            w = dst.shape[1] // n
            for k in range(n):
                op("gpsimd", lambda e, s=src, d=dst, k=k, w=w: e.dma_start(
                    out=d[:, k * w:(k + 1) * w],
                    in_=s[k * 128:(k + 1) * 128, :]), inc="dg", inc_n=16)
        op("gpsimd", lambda e: e.dma_start(out=glgr[:], in_=glgr_d),
           inc="dg", inc_n=16)
        for src, dst in ((maskU_d, maskU), (maskLs_d, maskLs), (lmask_d, lmask),
                         (hexp_d, hexp), (lexp_d, lexp), (idf_d, idf_sb),
                         (onesb_d, onesb)):
            op("sync", lambda e, s=src, d=dst: e.dma_start(out=d[:], in_=s),
               inc="dq", inc_n=16)
        w_g, w_q = at("dg"), at("dq")

        NST = S // 512
        fT_ev_prev = 0
        a1dma_at = {}
        for b in range(B):
            op("gpsimd", lambda e: e.memset(xpool[:], 0.0),
               waits=[("pe", at("pe")), ("s", at("s")), ("v", at("v"))],
               inc="dg")
            xp_init = at("dg")
            for st in range(NST):
                t0 = st * 512
                if st > 0:
                    tw = [("pe", at("pe"))]
                    for c in range(8):
                        op("gpsimd", lambda e, c=c: e.tensor_copy(
                            K_tail[:, c * 256:(c + 1) * 256],
                            K_bf[:, c * 512 + 256:(c + 1) * 512]),
                           waits=tw, inc="dg")
                        tw = ()
                    op("gpsimd", lambda e: e.tensor_copy(
                        V_tail[:], v_tm[:, 2048:4096]), inc="dg")
                tails = at("dg")
                xsb = (b * NST + st) % 2 * 4096
                if b == 0 and st == 0:
                    for c in range(8):
                        op("sync", lambda e, b=b, t0=t0, c=c, xsb=xsb:
                           e.dma_start(
                               out=xn_st[:, xsb + c * 512:xsb + (c + 1) * 512],
                               in_=xn[b, c * 128:(c + 1) * 128, t0:t0 + 512]),
                           inc="dq", inc_n=16)
                    ld = at("dq")
                else:
                    ld = xn_ld_next
                x1w = [("v", at("v"))]
                for c in range(8):
                    op("sync", lambda e, b=b, t0=t0, c=c: e.dma_start(
                        out=x1_st[:, c * 512:(c + 1) * 512],
                        in_=x1_d[b, c * 128:(c + 1) * 128, t0:t0 + 512]),
                       waits=x1w, inc="dq", inc_n=16)
                    x1w = ()
                x1_ld = at("dq")
                nst_all = b * NST + st + 1
                if nst_all < B * NST:
                    b2, st2 = divmod(nst_all, NST)
                    t02 = st2 * 512
                    xsb2 = nst_all % 2 * 4096
                    pw = [("pe", at("pe"))]
                    for c in range(8):
                        op("sync", lambda e, b2=b2, t02=t02, c=c, xsb2=xsb2:
                           e.dma_start(
                               out=xn_st[:, xsb2 + c * 512:xsb2 + (c + 1) * 512],
                               in_=xn[b2, c * 128:(c + 1) * 128, t02:t02 + 512]),
                           waits=pw, inc="dq", inc_n=16)
                        pw = ()
                    xn_ld_next = at("dq")
                for wsrc, dst in ((wq, Q_bf), (wk, K_bf)):
                    for m in range(8):
                        bank = PS[m % 2]
                        for k in range(8):
                            op("tensor", lambda e, wsrc=wsrc, m=m, k=k, bank=bank,
                               xsb=xsb:
                               e.matmul(
                                   bank[:],
                                   wsrc[:, k * 1024 + m * 128:k * 1024 + (m + 1) * 128],
                                   xn_st[:, xsb + k * 512:xsb + (k + 1) * 512],
                                   start=(k == 0), stop=(k == 7)),
                               waits=([("dq", ld), ("dg", max(w_g, tails)),
                                       ("s", at("s")), ("v", at("v"))]
                                      if k == 0 else ()),
                               inc=("pe" if k == 7 else None))
                        op("scalar", lambda e, dst=dst, m=m, bank=bank:
                           e.activation(out=dst[:, m * 512:(m + 1) * 512],
                                        in_=bank[:], func=AF.Identity),
                           waits=[("pe", at("pe"))], inc="s")
                for tc in range(4):
                    for half in range(2):
                        bank = PS[(tc * 2 + half) % 2]
                        for k in range(8):
                            op("tensor", lambda e, tc=tc, h=half, k=k, bank=bank,
                               xsb=xsb:
                               e.matmul(
                                   bank[:],
                                   xn_st[:, xsb + k * 512 + tc * 128:xsb + k * 512 + (tc + 1) * 128],
                                   wv[:, k * 1024 + h * 512:k * 1024 + h * 512 + 512],
                                   start=(k == 0), stop=(k == 7)),
                               waits=([("s", at("s"))] if k == 0 else ()),
                               inc=("pe" if k == 7 else None))
                        op("scalar", lambda e, tc=tc, h=half, bank=bank:
                           e.activation(
                               out=v_tm[:, tc * 1024 + h * 512:tc * 1024 + h * 512 + 512],
                               in_=bank[:], func=AF.Identity),
                           waits=[("pe", at("pe"))], inc="s")
                for gm in range(4):
                    bank = PS[gm % 2]
                    for k in range(8):
                        op("tensor", lambda e, gm=gm, k=k, bank=bank, xsb=xsb: e.matmul(
                            bank[:],
                            wgq[:, k * 512 + gm * 128:k * 512 + (gm + 1) * 128],
                            xn_st[:, xsb + k * 512:xsb + (k + 1) * 512],
                            start=(k == 0), stop=(k == 7)),
                           waits=([("s", at("s"))] if k == 0 else ()),
                           inc=("pe" if k == 7 else None))
                    op("scalar", lambda e, gm=gm, bank=bank: e.activation(
                        out=GQ_bf[:, gm * 512:(gm + 1) * 512], in_=bank[:],
                        func=AF.Identity),
                       waits=[("pe", at("pe"))], inc="s")
                for c in range(8):
                    op("vector", lambda e, c=c, st=st, xsb=xsb: e.reduce_sum(
                        xpool[:, c * 64 + st * 16:c * 64 + (st + 1) * 16],
                        xn_st[:, xsb + c * 512:xsb + (c + 1) * 512].rearrange(
                            "p (g k) -> p g k", k=32), axis=AX),
                       waits=([("dg", xp_init), ("dq", ld)] if c == 0 else ()),
                       inc="v")
                xp_done = at("v")
                op("scalar", lambda e: e.activation(
                    out=xpool_bf[:], in_=xpool[:], func=AF.Identity),
                   waits=[("v", xp_done)], inc="s")
                xpb_done = at("s")
                for gc in range(4):
                    bank = PS[2 + gc % 2]
                    for k in range(8):
                        op("tensor", lambda e, gc=gc, k=k, bank=bank: e.matmul(
                            bank[:, 0:64],
                            wgk[:, k * 512 + gc * 128:k * 512 + (gc + 1) * 128],
                            xpool_bf[:, k * 64:(k + 1) * 64],
                            start=(k == 0), stop=(k == 7)),
                           waits=([("s", at("s"))]
                                  if k == 0 else ()),
                           inc=("pe" if k == 7 else None))
                    op("scalar", lambda e, gc=gc, bank=bank: e.activation(
                        out=GK_sb[:, gc * 64:(gc + 1) * 64], in_=bank[:, 0:64],
                        func=AF.Identity, scale=1.0 / 32.0),
                       waits=[("pe", at("pe"))], inc="s")
                bank = PS[4]
                for k in range(8):
                    op("tensor", lambda e, k=k, bank=bank: e.matmul(
                        bank[0:64, :], xpool_bf[:, k * 64:(k + 1) * 64],
                        wgv[:, k * 512:(k + 1) * 512],
                        start=(k == 0), stop=(k == 7)),
                       waits=([("s", at("s"))] if k == 0 else ()),
                       inc=("pe" if k == 7 else None))
                op("scalar", lambda e, bank=bank: e.activation(
                    out=GV_sb[:], in_=bank[0:64, :], func=AF.Identity,
                    scale=1.0 / 32.0),
                   waits=[("pe", at("pe"))], inc="s")
                proj_done_s = at("s")

                for tb in range(4):
                    if not phases:
                        continue
                    loc_ev_at, lat_ev_at = {}, {}
                    tbg = st * 4 + tb
                    tkey = b * 16 + tbg
                    tg = t0 + tb * 128
                    dbf = tbg % 2
                    kbs = [kb for kb in (tbg - 2, tbg - 1, tbg) if kb >= 0]
                    exp_at, pmask_at = {}, {}
                    tile_start_s = at("s")

                    def t1_front(h):
                        par = (h % 2) * 64
                        hc_ = h // 2
                        bS = PS[h % 2]
                        sl = (h % 4) * 384
                        first_w = [("s", exp_at[h - 2] if h >= 2
                                    else tile_start_s)] + (
                            [("dg", at("dg"))] if (dbg and h == 0) else [])
                        for ci, kb in enumerate(kbs):
                            if kb >= st * 4:
                                kloc = kb - st * 4
                                kap = K_bf[par:par + 64,
                                           hc_ * 512 + kloc * 128:
                                           hc_ * 512 + (kloc + 1) * 128]
                            else:
                                kloc = kb - (st * 4 - 2)
                                kap = K_tail[par:par + 64,
                                             hc_ * 256 + kloc * 128:
                                             hc_ * 256 + (kloc + 1) * 128]
                            op("tensor", lambda e, kap=kap, h=h, ci=ci, bS=bS,
                               par=par, hc_=hc_, tb=tb: e.matmul(
                                   bS[:, ci * 128:(ci + 1) * 128], kap,
                                   Q_bf[par:par + 64,
                                        hc_ * 512 + tb * 128:hc_ * 512 + (tb + 1) * 128],
                                   start=True, stop=True),
                               waits=(first_w if ci == 0 else ()), inc=None)
                            first_w = ()
                        op("tensor", lambda e, h=h, par=par, hc_=hc_, tb=tb:
                           e.matmul(
                               PS[4][:, 24 + 2 * h:26 + 2 * h],
                               Q_bf[par:par + 64,
                                    hc_ * 512 + tb * 128:hc_ * 512 + (tb + 1) * 128],
                               glgr[par:par + 64, 2 * h:2 * h + 2],
                               start=True, stop=True), inc="pe")
                        pe_sc = at("pe")
                        op("scalar", lambda e, h=h, bS=bS, sl=sl, n=len(kbs):
                           e.activation(
                               out=P_T[:, sl:sl + n * 128],
                               in_=bS[:, 0:n * 128], func=AF.Exp, scale=0.125),
                           waits=[("pe", pe_sc)], inc="s")
                        exp_at[h] = at("s")
                        if len(kbs) == 3:
                            op("vector", lambda e, sl=sl: e.tensor_mul(
                                P_T[:, sl:sl + 128],
                                P_T[:, sl:sl + 128], maskLs[:]),
                               waits=[("s", at("s"))], inc="v")
                        ci_last = len(kbs) - 1
                        op("vector", lambda e, sl=sl, ci=ci_last: e.tensor_mul(
                            P_T[:, sl + ci * 128:sl + (ci + 1) * 128],
                            P_T[:, sl + ci * 128:sl + (ci + 1) * 128],
                            maskU[:]),
                           waits=[("s", at("s"))], inc="v")
                        pmask_at[h] = at("v")

                    def t1_tail(h):
                        par = (h % 2) * 64
                        hc_ = h // 2
                        bL = PS[2 + (h // 2) % 2]
                        sl = (h % 4) * 384
                        for ci in range(len(kbs)):
                            op("tensor", lambda e, h=h, ci=ci, sl=sl,
                               f=(ci == 0), l=(ci == len(kbs) - 1): e.matmul(
                                   PS[4][:, h:h + 1],
                                   P_T[:, sl + ci * 128:sl + (ci + 1) * 128],
                                   onesb[:], start=f, stop=l),
                               waits=([("v", pmask_at[h])] if ci == 0 else ()),
                               inc=None)
                        for ci, kb in enumerate(kbs):
                            if kb >= st * 4:
                                kloc = kb - st * 4
                                vap = v_tm[:, kloc * 1024 + h * 64:
                                           kloc * 1024 + (h + 1) * 64]
                            else:
                                kloc = kb - (st * 4 - 2)
                                vap = V_tail[:, kloc * 1024 + h * 64:
                                             kloc * 1024 + (h + 1) * 64]
                            op("tensor", lambda e, vap=vap, h=h, ci=ci, bL=bL,
                               par=par, sl=sl, f=(ci == 0),
                               l=(ci == len(kbs) - 1): e.matmul(
                                   bL[par:par + 64, 0:128], vap,
                                   P_T[:, sl + ci * 128:sl + (ci + 1) * 128],
                                   start=f, stop=l),
                               waits=([("s", loc_ev_at[h - 4])]
                                      if (ci == 0 and h >= 4) else ()),
                               inc=("pe" if ci == len(kbs) - 1 else None))
                        op("scalar", lambda e, h=h, bL=bL, par=par, hc_=hc_:
                           e.activation(
                               out=local_sb[par:par + 64, hc_ * 128:(hc_ + 1) * 128],
                               in_=bL[par:par + 64, 0:128], func=AF.Identity),
                           waits=[("pe", at("pe"))], inc="s")
                        loc_ev_at[h] = at("s")

                    if 't1' in phases:
                        for c2 in range(8):
                            t1_front(2 * c2)
                            t1_front(2 * c2 + 1)
                            if c2 >= 1:
                                t1_tail(2 * c2 - 2)
                                t1_tail(2 * c2 - 1)
                        t1_tail(14)
                        t1_tail(15)
                    lexp_at, lmask_at = {}, {}

                    def t2_front(lh):
                        par = (lh % 2) * 64
                        lc = lh // 2
                        bS = PS[lh % 4]
                        sl = (lh % 4) * 128
                        op("tensor", lambda e, lh=lh, par=par, lc=lc, bS=bS,
                           tb=tb: e.matmul(
                               bS[0:64, 0:128],
                               GK_sb[par:par + 64, lc * 64:(lc + 1) * 64],
                               GQ_bf[par:par + 64,
                                     lc * 512 + tb * 128:lc * 512 + (tb + 1) * 128],
                               start=True, stop=True),
                           waits=([("v", lat_ev_at[lh - 4])] if lh >= 4
                                  else [("s", at("s")), ("v", at("v"))]),
                           inc="pe")
                        op("scalar", lambda e, lh=lh, bS=bS, sl=sl: e.activation(
                            out=eL_T[:, sl:sl + 128],
                            in_=bS[0:64, 0:128], func=AF.Exp, scale=0.125),
                           waits=[("pe", at("pe"))], inc="s")
                        lexp_at[lh] = at("s")
                        op("vector", lambda e, lh=lh, sl=sl, tg=tg: e.tensor_mul(
                            eL_T[:, sl:sl + 128],
                            eL_T[:, sl:sl + 128],
                            lmask[:, tg:tg + 128]),
                           waits=[("s", at("s"))], inc="v")
                        lmask_at[lh] = at("v")

                    def t2_tail(lh):
                        par = (lh % 2) * 64
                        lc = lh // 2
                        bR = PS[lh % 4]
                        sl = (lh % 4) * 128
                        op("tensor", lambda e, lh=lh, sl=sl: e.matmul(
                            PS[4][:, 16 + lh:17 + lh],
                            eL_T[:, sl:sl + 128], onesb[0:64, :],
                            start=True, stop=True),
                           waits=[("v", lmask_at[lh])], inc=None)
                        op("tensor", lambda e, lh=lh, par=par, sl=sl, bR=bR:
                           e.matmul(
                               bR[par:par + 64, 128:256],
                               GV_sb[:, lh * 64:(lh + 1) * 64],
                               eL_T[:, sl:sl + 128],
                               start=True, stop=True), inc="pe")
                        op("vector", lambda e, lh=lh, par=par, lc=lc, bR=bR:
                           e.tensor_copy(
                               remlat_sb[par:par + 64, lc * 128:(lc + 1) * 128],
                               bR[par:par + 64, 128:256]),
                           waits=[("pe", at("pe"))], inc="v")
                        lat_ev_at[lh] = at("v")

                    if 't2' in phases:
                        for c2 in range(4):
                            t2_front(2 * c2)
                            t2_front(2 * c2 + 1)
                            if c2 >= 1:
                                t2_tail(2 * c2 - 2)
                                t2_tail(2 * c2 - 1)
                        t2_tail(6)
                        t2_tail(7)
                    if 't3' not in phases:
                        continue
                    op("vector", lambda e: e.tensor_copy(fct[:], PS[4][:, 0:56]),
                       waits=[("pe", at("pe")), ("v", at("v")),
                              ("s", at("s"))], inc="v")
                    op("vector", lambda e: e.reciprocal(
                        f_scr[:, 0:24], fct[:, 0:24]), inc="v")
                    rd_done = at("v")
                    op("scalar", lambda e: e.activation(
                        out=tg_sb[:], in_=fct[:, 24:56], func=AF.Tanh,
                        scale=0.5), waits=[("v", rd_done)], inc="s")
                    op("vector", lambda e: e.scalar_tensor_tensor(
                        f_scr[:, 24:40], tg_sb[:, 0:32:2], 1.0, f_scr[:, 0:16],
                        ALU.add, ALU.mult), waits=[("s", at("s"))], inc="v")
                    op("vector", lambda e: e.tensor_scalar(
                        f_scr[:, 40:56], tg_sb[:, 1:32:2], 1.0, None, ALU.add),
                       inc="v")
                    f_done = at("v")
                    op("tensor", lambda e: e.transpose(
                        PS[5][0:16, 0:128], f_scr[:, 24:40], idf_sb[:]),
                       waits=[("v", f_done)] + (
                           [("s", fT_ev_prev)] if fT_ev_prev else []), inc=None)
                    op("tensor", lambda e: e.transpose(
                        PS[5][0:16, 128:256], f_scr[:, 40:56], idf_sb[:]),
                       inc=None)
                    op("tensor", lambda e: e.transpose(
                        PS[5][0:8, 256:384], f_scr[:, 16:24], idf_sb[:]),
                       inc="pe")
                    op("scalar", lambda e: e.activation(
                        out=fT[:], in_=PS[5][0:16, 0:256], func=AF.Identity),
                       waits=[("pe", at("pe"))], inc="s")
                    op("scalar", lambda e: e.activation(
                        out=rlT[:], in_=PS[5][0:8, 256:384], func=AF.Identity),
                       inc="s")
                    fT_ev_prev = at("s")
                    if 't4' not in phases:
                        continue
                    for c4 in range(4):
                        op("tensor", lambda e, c4=c4: e.matmul(
                            PS[2][:, 0:128], lexp[:, c4 * 128:(c4 + 1) * 128],
                            rlT[:], start=True, stop=True),
                           waits=[("s", fT_ev_prev)] if c4 == 0 else
                                 [("v", at("v"))], inc="pe")
                        op("vector", lambda e, c4=c4: e.tensor_mul(
                            remn_sb[:, c4 * 128:(c4 + 1) * 128],
                            remlat_sb[:, c4 * 128:(c4 + 1) * 128],
                            PS[2][:, 0:128]),
                           waits=[("pe", at("pe"))], inc="v")
                    remn_done = at("v")
                    for m in range(8):
                        bR = PS[m % 2]
                        for k in range(4):
                            op("tensor", lambda e, m=m, k=k, bR=bR: e.matmul(
                                bR[:, 0:128],
                                rout[:, k * 1024 + m * 128:k * 1024 + (m + 1) * 128],
                                remn_sb[:, k * 128:(k + 1) * 128],
                                start=(k == 0), stop=(k == 3)),
                               waits=([("v", remn_done if m < 1 else at("v"))]
                                      if k == 0 else ()),
                               inc=("pe" if k == 3 else None))
                        op("tensor", lambda e, m=m: e.matmul(
                            PS[3][:, 0:128], hexp[:, m * 128:(m + 1) * 128],
                            fT[:, 0:128], start=True, stop=True), inc=None)
                        op("tensor", lambda e, m=m: e.matmul(
                            PS[3][:, 128:256], hexp[:, m * 128:(m + 1) * 128],
                            fT[:, 128:256], start=True, stop=True), inc="pe")
                        pe_m = at("pe")
                        op("scalar", lambda e, m=m, bR=bR: e.activation(
                            out=rem_sb[:, (m % 2) * 128:(m % 2 + 1) * 128],
                            in_=bR[:, 0:128], func=AF.Identity),
                           waits=[("pe", pe_m)], inc="s")
                        op("vector", lambda e, m=m: e.tensor_mul(
                            c1_sb[:], local_sb[:, m * 128:(m + 1) * 128],
                            PS[3][:, 0:128]),
                           waits=[("pe", pe_m), ("s", at("s"))], inc="v")
                        op("vector", lambda e, m=m: e.tensor_mul(
                            rem2_sb[:], rem_sb[:, (m % 2) * 128:(m % 2 + 1) * 128],
                            PS[3][:, 128:256]), inc="v")
                        op("vector", lambda e, m=m: e.tensor_add(
                            comb_bf[:, m * 128:(m + 1) * 128], c1_sb[:],
                            rem2_sb[:]), inc="v")
                    comb_done = at("v")
                    if dbg and tkey < 2:
                        for t_, d_ in ((fct, dbg_fct), (f_scr, dbg_fs),
                                       (local_sb, dbg_loc), (remlat_sb, dbg_rl),
                                       (comb_bf, dbg_comb), (P_T, dbg_pt),
                                       (K_bf, dbg_k), (v_tm, dbg_v)):
                            op("gpsimd", lambda e, t_=t_, d_=d_, tk=tkey:
                               e.dma_start(out=d_[tk], in_=t_[:]),
                               waits=[("v", at("v")), ("s", at("s"))],
                               inc="dg", inc_n=16)
                    if 't5' not in phases:
                        continue
                    for m in range(8):
                        bW = PS[4 + m % 2]
                        for k in range(8):
                            op("tensor", lambda e, m=m, k=k, bW=bW: e.matmul(
                                bW[:, 0:128],
                                wout[:, k * 1024 + m * 128:k * 1024 + (m + 1) * 128],
                                comb_bf[:, k * 128:(k + 1) * 128],
                                start=(k == 0), stop=(k == 7)),
                               waits=([("v", comb_done if m < 2 else at("v"))]
                                      if k == 0 else ()),
                               inc=("pe" if k == 7 else None))
                        op("vector", lambda e, m=m, bW=bW, dbf=dbf, tb=tb:
                           e.scalar_tensor_tensor(
                            a1_scr[:, dbf * 1024 + m * 128:dbf * 1024 + (m + 1) * 128],
                            bW[:, 0:128], 0.5,
                            x1_st[:, m * 512 + tb * 128:m * 512 + (tb + 1) * 128],
                            ALU.mult, ALU.add),
                           waits=[("pe", at("pe"))] + (
                               [("dq", max(a1dma_at.get(tkey - 2, 0), x1_ld))]
                               if m == 0 else []),
                           inc="v")
                    a1_done = at("v")
                    for m in range(8):
                        a1dma_at[tkey] = op("sync", lambda e, b=b, m=m, tg=tg,
                                           dbf=dbf: e.dma_start(
                            out=a1_d[b, m * 128:(m + 1) * 128, tg:tg + 128],
                            in_=a1_scr[:, dbf * 1024 + m * 128:dbf * 1024 + (m + 1) * 128]),
                           waits=([("v", a1_done)] if m == 0 else ()),
                           inc="dq", inc_n=16)
        sd.wait("gpsimd", [("dq", at("dq"))])

    if ctx is not None:
        return body
    run_block(nc, sched, body)
    sched.free()
    return nc


def build_ffn1(ctx=None):
    nc = ctx["nc"] if ctx else bass.Bass("TRN2", target_bir_lowering=False)
    dram = _mk_dram(nc, ctx)
    xa = dram("x1", [B, D, S], F32, "ExternalInput")
    xn1 = dram("xn1", [B, D, S], BF16, "ExternalInput")
    w1_d = dram("w1c", [D, 4096], F32, "ExternalInput")
    w2_d = dram("w2c", [4096, D], F32, "ExternalInput")
    b1_d = dram("b1_pc", [128, 32], F32, "ExternalInput")
    b2_d = dram("b2_pc", [128, 8], F32, "ExternalInput")
    out_d = dram("out", [B, D, S], F32, "ExternalOutput")

    sb = ctx["arena"].t if ctx else nc.alloc_sbuf_tensor
    w1 = sb("w1_s", [128, 32768], BF16)
    w2 = sb("w2_s", [128, 32768], BF16)
    b1 = sb("b1_s", [128, 32], F32)
    b2 = sb("b2_s", [128, 8], F32)
    xn_sb = sb("xn_sb", [128, 4096], BF16)
    xa_sb = sb("xa_sb", [128, 4096], F32)
    h_bf = sb("h_bf", [128, 16384], BF16)
    o_scr = sb("o_scr", [128, 1024], F32)
    PS, psT, psTf = _mk_psum(nc, ctx)

    sched = None if ctx else Sched(nc, ["dq", "dg", "pe", "v", "s"])
    TT = 512

    def body(sd):
        op, at = sd.op, sd.at
        for k in range(8):
            op("gpsimd", lambda e, k=k: e.dma_start(
                out=w1[:, k * 4096:(k + 1) * 4096],
                in_=w1_d[k * 128:(k + 1) * 128, :]), inc="dg", inc_n=16)
        for k in range(32):
            op("gpsimd", lambda e, k=k: e.dma_start(
                out=w2[:, k * 1024:(k + 1) * 1024],
                in_=w2_d[k * 128:(k + 1) * 128, :]), inc="dg", inc_n=16)
        op("sync", lambda e: e.dma_start(out=b1[:], in_=b1_d), inc="dq", inc_n=16)
        op("sync", lambda e: e.dma_start(out=b2[:], in_=b2_d), inc="dq", inc_n=16)
        w_g, w_q = at("dg"), at("dq")
        out_dma_at = 0
        for b in range(B):
            for t0 in range(0, S, TT):
                lw = [("v", at("v")), ("s", at("s"))]
                for src_, dst in ((xn1, xn_sb), (xa, xa_sb)):
                    for c in range(8):
                        op("sync", lambda e, s=src_, d=dst, b=b, t0=t0, c=c:
                           e.dma_start(out=d[:, c * TT:(c + 1) * TT],
                                       in_=s[b, c * 128:(c + 1) * 128, t0:t0 + TT]),
                           waits=lw, inc="dq", inc_n=16)
                        lw = ()
                ld = at("dq")
                for fc in range(32):
                    bank = PS[fc % 2]
                    for k in range(8):
                        op("tensor", lambda e, fc=fc, k=k, bank=bank: e.matmul(
                            bank[:],
                            w1[:, k * 4096 + fc * 128:k * 4096 + (fc + 1) * 128],
                            xn_sb[:, k * TT:(k + 1) * TT],
                            start=(k == 0), stop=(k == 7)),
                           waits=([("dq", ld), ("dg", w_g), ("s", at("s"))]
                                  if k == 0 else ()),
                           inc=("pe" if k == 7 else None))
                    op("scalar", lambda e, fc=fc, bank=bank: e.activation(
                        out=h_bf[:, fc * TT:(fc + 1) * TT], in_=bank[:],
                        func=AF.Gelu_apprx_tanh, bias=b1[:, fc:fc + 1]),
                       waits=[("pe", at("pe")), ("dq", w_q)], inc="s")
                h_done = at("s")
                for m in range(8):
                    bank = PS[2 + m % 2]
                    for k in range(32):
                        op("tensor", lambda e, m=m, k=k, bank=bank: e.matmul(
                            bank[:],
                            w2[:, k * 1024 + m * 128:k * 1024 + (m + 1) * 128],
                            h_bf[:, k * TT:(k + 1) * TT],
                            start=(k == 0), stop=(k == 31)),
                           waits=([("s", h_done), ("v", at("v"))] if k == 0 else ()),
                           inc=("pe" if k == 31 else None))
                    op("vector", lambda e, m=m, bank=bank: e.scalar_tensor_tensor(
                        o_scr[:, (m % 2) * TT:(m % 2 + 1) * TT], bank[:],
                        b2[:, m:m + 1], xa_sb[:, m * TT:(m + 1) * TT],
                        ALU.add, ALU.add),
                       waits=[("pe", at("pe"))] + (
                           [("dq", out_dma_at)] if m < 2 else []), inc="v")
                    out_dma_at = op("sync", lambda e, b=b, t0=t0, m=m: e.dma_start(
                        out=out_d[b, m * 128:(m + 1) * 128, t0:t0 + TT],
                        in_=o_scr[:, (m % 2) * TT:(m % 2 + 1) * TT]),
                       waits=[("v", at("v"))], inc="dq", inc_n=16)
        sd.wait("gpsimd", [("dq", at("dq"))])

    if ctx is not None:
        return body
    run_block(nc, sched, body)
    sched.free()
    return nc



_NC_CACHE = {}


def _get_nc(name, builder):
    if name not in _NC_CACHE:
        _NC_CACHE[name] = builder()
    return _NC_CACHE[name]


def _fm(a):
    return np.ascontiguousarray(np.asarray(a).transpose(0, 2, 1))


def _pc(v, cols):
    return np.ascontiguousarray(np.asarray(v, np.float32).reshape(cols, 128).T)


def kernel_unfused(**inputs):
    hc = host_consts()
    inp = {k: np.asarray(v) for k, v in inputs.items()}
    x = inp["L0_attn_x"]

    glw = inp["L1_attn_gate_local_w"]
    grw = inp["L1_attn_gate_remote_w"]
    glgr = np.empty((64, 32), np.float32)
    glgr[:, 0::2] = glw.T
    glgr[:, 1::2] = grw.T
    glgr_dup = np.ascontiguousarray(np.concatenate([glgr, glgr], axis=0))

    cores = list(range(NCORES))
    x_fm = [_fm(x[2 * c:2 * c + 2]) for c in cores]

    nc = _get_nc("prep", build_prep)
    maps = [{
        "x": x_fm[c],
        "wfr": inp["L0_ffn_w_freq_real"], "wfi": inp["L0_ffn_w_freq_imag"],
        "Fr": hc["Fr"], "Fi": hc["Fi"], "FiN": hc["FiN"],
        "Gr": hc["Gr"], "G2": hc["G2"],
        "ident_bf": hc["ident_bf"], "ident_f32": hc["ident_f32"],
        "sel8": hc["sel8"], "inv1024": hc["inv1024"],
    } for c in cores]
    r1 = run_bass_kernel_spmd(nc, maps, core_ids=cores).results

    nc = _get_nc("attn0", build_attn0)
    nw_pc = _pc(inp["L0_attn_norm_w"], 8)
    maps = [{
        "xn": r1[c]["xn"],
        "wq": inp["L0_attn_wq"], "wk": inp["L0_attn_wk"],
        "wv": inp["L0_attn_wv"], "wo": inp["L0_attn_wo"],
        "wg1": inp["L0_attn_wg1"], "wg2": inp["L0_attn_wg2"],
        "norm_w_pc": nw_pc,
        "cfm": hc["cfm"], "sfm": hc["sfm"], "ctm": hc["ctm"], "stm": hc["stm"],
        "maskU": hc["maskU"], "blk16": hc["blk16"], "hexp": hc["hexp"],
    } for c in cores]
    r2 = run_bass_kernel_spmd(nc, maps, core_ids=cores).results

    nc = _get_nc("mid", build_mid)
    bias_pc = _pc(inp["L0_ffn_bias"], 8)
    maps = [{
        "x": x_fm[c], "a0": r2[c]["a0"], "xn0": r1[c]["xn"], "M": r1[c]["M"],
        "bias_pc": bias_pc,
        "ident_f32": hc["ident_f32"], "sel8": hc["sel8"],
        "inv1024": hc["inv1024"],
    } for c in cores]
    r3 = run_bass_kernel_spmd(nc, maps, core_ids=cores).results

    nc = _get_nc("attn1", build_attn1)
    maps = [{
        "xn1": r3[c]["xn1"], "x1": r3[c]["x1"],
        "wq": inp["L1_attn_wq"], "wk": inp["L1_attn_wk"],
        "wv": inp["L1_attn_wv"], "wgq": inp["L1_attn_wgq"],
        "wgk": inp["L1_attn_wgk"], "wgv": inp["L1_attn_wgv"],
        "r_out_w": inp["L1_attn_r_out_w"], "w_out": inp["L1_attn_w_out"],
        "glgr": glgr_dup,
        "maskU": hc["maskU"], "maskLs": hc["maskLs"], "lmask": hc["lmask"],
        "hexp": hc["hexp"], "lexp": hc["lexp"], "ident_f32": hc["ident_f32"],
        "ones_bf": hc["ones_bf"],
    } for c in cores]
    r4 = run_bass_kernel_spmd(nc, maps, core_ids=cores).results

    nc = _get_nc("ffn1", build_ffn1)
    maps = [{
        "x1": r4[c]["a1"], "xn1": r3[c]["xn1"],
        "w1c": inp["L1_ffn_w1"][c], "w2c": inp["L1_ffn_w2"][c],
        "b1_pc": _pc(inp["L1_ffn_b1"][c], 32),
        "b2_pc": _pc(inp["L1_ffn_b2"][c], 8),
    } for c in cores]
    r5 = run_bass_kernel_spmd(nc, maps, core_ids=cores).results

    out = np.empty((16, 2048, 1024), np.float32)
    for c in cores:
        out[2 * c:2 * c + 2] = r5[c]["out"].transpose(0, 2, 1)
    return out


def build_fused(nphases=5):
    nc = bass.Bass("TRN2", target_bir_lowering=False)

    def ein(name, shape, dt):
        return nc.dram_tensor(name, shape, dt, kind="ExternalInput").ap()

    def internal(name, shape, dt):
        return nc.dram_tensor(name, shape, dt, kind="Internal").ap()

    E = {}
    E["x"] = ein("x", [B, D, S], F32)
    for n in ("wfr", "wfi"):
        E[n] = ein(n, [D, D], F32)
    for n in ("Fr", "Fi", "FiN", "Gr", "G2"):
        E[n] = ein(n, [D, D], BF16)
    E["ident_bf"] = ein("ident_bf", [128, 128], BF16)
    E["ident_f32"] = ein("ident_f32", [128, 128], F32)
    E["sel8"] = ein("sel8", [8, 1024], F32)
    E["inv1024"] = ein("inv1024", [128, 1], F32)
    for n in ("l0_wq", "l0_wk", "l0_wv", "l0_wo"):
        E[n] = ein(n, [D, D], F32)
    E["wg1"] = ein("wg1", [D, 64], F32)
    E["wg2"] = ein("wg2", [64, D], F32)
    E["norm_w_pc"] = ein("norm_w_pc", [128, 8], F32)
    E["cfm"] = ein("cfm", [D, S], BF16)
    E["sfm"] = ein("sfm", [D, S], BF16)
    E["ctm"] = ein("ctm", [S, D], BF16)
    E["stm"] = ein("stm", [S, D], BF16)
    E["maskU"] = ein("maskU", [128, 128], BF16)
    E["maskLs"] = ein("maskLs", [128, 128], BF16)
    E["blk16"] = ein("blk16", [8 * 128, 16], F32)
    E["hexp"] = ein("hexp", [16, 8 * 128], BF16)
    E["lexp"] = ein("lexp", [8, 4 * 128], BF16)
    E["bias_pc"] = ein("bias_pc", [128, 8], F32)
    for n in ("l1_wq", "l1_wk", "l1_wv", "w_out"):
        E[n] = ein(n, [D, D], F32)
    for n in ("wgq", "wgk", "wgv"):
        E[n] = ein(n, [D, 512], F32)
    E["r_out_w"] = ein("r_out_w", [512, D], F32)
    E["glgr"] = ein("glgr", [128, 32], F32)
    E["lmask"] = ein("lmask", [64, S], BF16)
    E["ones_bf"] = ein("ones_bf", [128, 1], BF16)
    E["w1c"] = ein("w1c", [D, 4096], F32)
    E["w2c"] = ein("w2c", [4096, D], F32)
    E["b1_pc"] = ein("b1_pc", [128, 32], F32)
    E["b2_pc"] = ein("b2_pc", [128, 8], F32)

    xn0_i = internal("i_xn0", [B, D, S], BF16)
    M_i = internal("i_M", [D, D], BF16)
    a0_i = internal("i_a0", [B, D, S], F32)
    x1_i = internal("i_x1", [B, D, S], F32)
    xn1_i = internal("i_xn1", [B, D, S], BF16)
    a1_i = internal("i_a1", [B, D, S], F32)
    out_e = nc.dram_tensor("out", [B, D, S], F32, kind="ExternalOutput").ap()

    PS, psT, psTf = _mk_psum(nc, None)
    cpool = setup_const_pool(nc, [1e-5, -0.6931472, 1e-6])
    ar = Arena(nc, nc.sbuf_base, nc.sbuf_top)
    ctx = {"nc": nc, "PS": PS, "psT": psT, "psTf": psTf, "cpool": cpool,
           "arena": ar}

    emits = []
    ar.reset("p1_")
    ctx["D"] = {"x": E["x"], "wfr": E["wfr"], "wfi": E["wfi"],
                "Fr": E["Fr"], "Fi": E["Fi"], "FiN": E["FiN"],
                "Gr": E["Gr"], "G2": E["G2"], "ident_bf": E["ident_bf"],
                "ident_f32": E["ident_f32"], "sel8": E["sel8"],
                "inv1024": E["inv1024"], "M": M_i, "xn": xn0_i}
    emits.append(build_prep(ctx=ctx))
    ar.reset("p2_")
    ctx["D"] = {"xn": xn0_i, "wq": E["l0_wq"], "wk": E["l0_wk"],
                "wv": E["l0_wv"], "wo": E["l0_wo"], "wg1": E["wg1"],
                "wg2": E["wg2"], "norm_w_pc": E["norm_w_pc"],
                "cfm": E["cfm"], "sfm": E["sfm"], "ctm": E["ctm"],
                "stm": E["stm"], "maskU": E["maskU"], "blk16": E["blk16"],
                "hexp": E["hexp"], "a0": a0_i}
    emits.append(build_attn0(ctx=ctx))
    ar.reset("p3_")
    ctx["D"] = {"x": E["x"], "a0": a0_i, "xn0": xn0_i, "M": M_i,
                "bias_pc": E["bias_pc"], "ident_f32": E["ident_f32"],
                "sel8": E["sel8"], "inv1024": E["inv1024"],
                "x1": x1_i, "xn1": xn1_i}
    emits.append(build_mid(ctx=ctx))
    ar.reset("p4_")
    ctx["D"] = {"xn1": xn1_i, "x1": x1_i, "wq": E["l1_wq"], "wk": E["l1_wk"],
                "wv": E["l1_wv"], "wgq": E["wgq"], "wgk": E["wgk"],
                "wgv": E["wgv"], "r_out_w": E["r_out_w"],
                "w_out": E["w_out"], "glgr": E["glgr"], "maskU": E["maskU"],
                "maskLs": E["maskLs"], "lmask": E["lmask"], "hexp": E["hexp"],
                "lexp": E["lexp"], "ident_f32": E["ident_f32"],
                "ones_bf": E["ones_bf"], "a1": a1_i}
    emits.append(build_attn1(ctx=ctx))
    ar.reset("p5_")
    ctx["D"] = {"x1": a1_i, "xn1": xn1_i, "w1c": E["w1c"],
                "w2c": E["w2c"], "b1_pc": E["b1_pc"], "b2_pc": E["b2_pc"],
                "out": out_e}
    emits.append(build_ffn1(ctx=ctx))

    emits = emits[:nphases]

    def mega(sd):
        for em in emits:
            em(sd)
            barrier(sd)

    sched = Sched(nc, ["dq", "dg", "pe", "v", "s"])
    run_block(nc, sched, mega)
    sched.free()
    return nc


def kernel(**inputs):
    hc = host_consts()
    inp = {k: np.asarray(v) for k, v in inputs.items()}
    x = inp["L0_attn_x"]

    glw = inp["L1_attn_gate_local_w"]
    grw = inp["L1_attn_gate_remote_w"]
    glgr = np.empty((64, 32), np.float32)
    glgr[:, 0::2] = glw.T
    glgr[:, 1::2] = grw.T
    glgr_dup = np.ascontiguousarray(np.concatenate([glgr, glgr], axis=0))

    cores = list(range(NCORES))
    nc = _get_nc("fused", build_fused)
    shared = {
        "wfr": inp["L0_ffn_w_freq_real"], "wfi": inp["L0_ffn_w_freq_imag"],
        "Fr": hc["Fr"], "Fi": hc["Fi"], "FiN": hc["FiN"],
        "Gr": hc["Gr"], "G2": hc["G2"],
        "ident_bf": hc["ident_bf"], "ident_f32": hc["ident_f32"],
        "sel8": hc["sel8"], "inv1024": hc["inv1024"],
        "l0_wq": inp["L0_attn_wq"], "l0_wk": inp["L0_attn_wk"],
        "l0_wv": inp["L0_attn_wv"], "l0_wo": inp["L0_attn_wo"],
        "wg1": inp["L0_attn_wg1"], "wg2": inp["L0_attn_wg2"],
        "norm_w_pc": _pc(inp["L0_attn_norm_w"], 8),
        "cfm": hc["cfm"], "sfm": hc["sfm"], "ctm": hc["ctm"], "stm": hc["stm"],
        "maskU": hc["maskU"], "maskLs": hc["maskLs"], "blk16": hc["blk16"],
        "hexp": hc["hexp"], "lexp": hc["lexp"],
        "bias_pc": _pc(inp["L0_ffn_bias"], 8),
        "l1_wq": inp["L1_attn_wq"], "l1_wk": inp["L1_attn_wk"],
        "l1_wv": inp["L1_attn_wv"], "w_out": inp["L1_attn_w_out"],
        "wgq": inp["L1_attn_wgq"], "wgk": inp["L1_attn_wgk"],
        "wgv": inp["L1_attn_wgv"], "r_out_w": inp["L1_attn_r_out_w"],
        "glgr": glgr_dup, "lmask": hc["lmask"], "ones_bf": hc["ones_bf"],
    }
    maps = []
    for c in cores:
        m = dict(shared)
        m["x"] = _fm(x[2 * c:2 * c + 2])
        m["w1c"] = inp["L1_ffn_w1"][c]
        m["w2c"] = inp["L1_ffn_w2"][c]
        m["b1_pc"] = _pc(inp["L1_ffn_b1"][c], 32)
        m["b2_pc"] = _pc(inp["L1_ffn_b2"][c], 8)
        maps.append(m)
    res = run_bass_kernel_spmd(nc, maps, core_ids=cores).results
    out = np.empty((16, 2048, 1024), np.float32)
    for c in cores:
        out[2 * c:2 * c + 2] = res[c]["out"].transpose(0, 2, 1)
    return out
